# version 06
# baseline (speedup 1.0000x reference)
import contextlib

import numpy as np
import ml_dtypes
import concourse.bass as bass
import concourse.mybir as mybir
from concourse.bass_utils import run_bass_kernel_spmd

F32 = mybir.dt.float32
BF16 = mybir.dt.bfloat16
AF = mybir.ActivationFunctionType
ALU = mybir.AluOpType
AX = mybir.AxisListType

N_CORES = 8
D = 1024
NCH = 8
H = 16
DH = 64
E = 8
ROPE_BASE = 10000.0
HEAD_TYPES = np.arange(H) % 4
PI_BITS = ((HEAD_TYPES >> 1) & 1).astype(np.float32)
E_BITS = (HEAD_TYPES & 1).astype(np.float32)
ROPE_CHUNKS = [c for c in range(NCH) if PI_BITS[2 * c] > 0]


class EB:
    ENGINES = ("pe", "act", "dve", "pool", "sp")

    def __init__(self, nc, ctx):
        self.nc = nc
        self.ctx = ctx
        self.eng = {"pe": nc.tensor, "act": nc.scalar, "dve": nc.vector,
                    "pool": nc.gpsimd, "sp": nc.sync}
        self._sems = {}
        self._count = {}
        self._waited = {e: {} for e in self.ENGINES}
        for e in self.ENGINES:
            self._mk("ev_" + e)

    def _mk(self, name):
        h = self.ctx.enter_context(self.nc.semaphore(name))
        self._sems[name] = h
        self._count[name] = 0
        return h

    def sem(self, name):
        if name not in self._sems:
            self._mk(name)
        return self._sems[name]

    def _wait(self, engine, deps):
        for dep in deps:
            if dep is None:
                continue
            sname, val = dep
            if self._waited[engine].get(sname, 0) >= val:
                continue
            self.eng[engine].wait_ge(self._sems[sname], val)
            self._waited[engine][sname] = val

    def op(self, engine, deps, fn, ev=False):
        self._wait(engine, [d for d in deps if d is not None])
        inst = fn(self.eng[engine])
        if ev:
            sname = "ev_" + engine
            self._count[sname] += 1
            inst.then_inc(self._sems[sname], 1)
            return (sname, self._count[sname])
        return None

    def dma(self, engine, deps, out, in_, sem_name, **kw):
        self._wait(engine, [d for d in deps if d is not None])
        s = self.sem(sem_name)
        inst = self.eng[engine].dma_start(out=out, in_=in_, **kw)
        self._count[sem_name] += 16
        inst.then_inc(s, 16)
        return (sem_name, self._count[sem_name])

    def last(self, engine):
        sname = "ev_" + engine
        return (sname, self._count[sname])

    def bar(self):
        return [(s, c) for s, c in self._count.items() if c > 0]


def ts(i, n):
    return slice(i * n, (i + 1) * n)


def ln_chain(b, stats, sc, e_sum, e_sq, n):
    e_m = b.op("dve", [e_sum], lambda e: e.tensor_scalar_mul(
        stats[:, sc + 2:sc + 3], stats[:, sc:sc + 1], 1.0 / n), ev=True)
    e_x2 = b.op("dve", [e_sq], lambda e: e.tensor_scalar_mul(
        stats[:, sc + 3:sc + 4], stats[:, sc + 1:sc + 2], 1.0 / n), ev=True)
    e_m2 = b.op("dve", [e_m], lambda e: e.tensor_mul(
        stats[:, sc + 4:sc + 5], stats[:, sc + 2:sc + 3],
        stats[:, sc + 2:sc + 3]), ev=True)
    e_v = b.op("dve", [e_x2, e_m2], lambda e: e.tensor_sub(
        stats[:, sc + 5:sc + 6], stats[:, sc + 3:sc + 4],
        stats[:, sc + 4:sc + 5]), ev=True)
    e_v = b.op("dve", [e_v], lambda e: e.tensor_scalar_add(
        stats[:, sc + 5:sc + 6], stats[:, sc + 5:sc + 6], 1e-5), ev=True)
    e_l = b.op("act", [e_v], lambda e: e.activation(
        stats[:, sc + 5:sc + 6], stats[:, sc + 5:sc + 6], AF.Ln), ev=True)
    e_r = b.op("act", [e_l], lambda e: e.activation(
        stats[:, sc + 6:sc + 7], stats[:, sc + 5:sc + 6], AF.Exp,
        scale=-0.5), ev=True)
    e_nm = b.op("dve", [e_m], lambda e: e.tensor_scalar_mul(
        stats[:, sc + 7:sc + 8], stats[:, sc + 2:sc + 3], -1.0), ev=True)
    return e_r, e_nm


MARKS = []


def build_program(NB, SEQ, GB, F):
    KT = SEQ // 128
    QG = SEQ // 512
    NT = SEQ // 128
    FT = F // 128
    NGR = NB // GB
    NTG = (SEQ * GB) // 512

    nc = bass.Bass("TRN2", target_bir_lowering=False)
    ap = lambda t: t.ap()
    x_in = ap(nc.dram_tensor("x", [NB, SEQ, D], F32, kind="ExternalInput"))
    wqkv = ap(nc.dram_tensor("wqkv", [D, 3 * D], F32, kind="ExternalInput"))
    wo = ap(nc.dram_tensor("wo", [D, D], F32, kind="ExternalInput"))
    gw = ap(nc.dram_tensor("gw", [D, E], F32, kind="ExternalInput"))
    w1 = ap(nc.dram_tensor("w1", [E, D, F], F32, kind="ExternalInput"))
    w2 = ap(nc.dram_tensor("w2", [E, F, D], F32, kind="ExternalInput"))
    lxi = ap(nc.dram_tensor("lxi", [1, H], F32, kind="ExternalInput"))
    c_id = ap(nc.dram_tensor("c_id", [128, 128], F32, kind="ExternalInput"))
    c_ones = ap(nc.dram_tensor("c_ones", [1, 128], F32, kind="ExternalInput"))
    c_sel = ap(nc.dram_tensor("c_sel", [128, 256], F32, kind="ExternalInput"))
    c_cos = ap(nc.dram_tensor("c_cos", [128, SEQ], F32, kind="ExternalInput"))
    c_sin = ap(nc.dram_tensor("c_sin", [128, SEQ], F32, kind="ExternalInput"))
    c_swp = ap(nc.dram_tensor("c_swp", [128, 128], F32, kind="ExternalInput"))
    c_mask = ap(nc.dram_tensor("c_mask", [128, 4 * 512], F32, kind="ExternalInput"))
    c_iota = ap(nc.dram_tensor("c_iota", [128, KT], F32, kind="ExternalInput"))
    c_ek = ap(nc.dram_tensor("c_ek", [1, H], F32, kind="ExternalInput"))
    out = ap(nc.dram_tensor("out", [NB, SEQ, D], F32, kind="ExternalOutput"))
    spill = ap(nc.dram_tensor("spill", [GB, NCH, 128, SEQ], BF16,
                              kind="Internal"))

    ctx = contextlib.ExitStack()
    _uq = [0]

    def U(name):
        _uq[0] += 1
        return f"{name}_{_uq[0]}"
    with ctx:
        b = EB(nc, ctx)
        T = ctx.enter_context
        R = {}

        id_sb = T(nc.sbuf_tensor("id_sb", [128, 128], BF16))
        swp_sb = T(nc.sbuf_tensor("swp_sb", [128, 128], BF16))
        ones_sb = T(nc.sbuf_tensor("ones_sb", [1, 128], F32))
        sel_sb = T(nc.sbuf_tensor("sel_sb", [128, 256], BF16))
        iota_sb = T(nc.sbuf_tensor("iota_sb", [128, KT], F32))
        ek_sb = T(nc.sbuf_tensor("ek_sb", [1, H], F32))
        lxi_sb = T(nc.sbuf_tensor("lxi_sb", [1, H], F32))
        rate_sb = T(nc.sbuf_tensor("rate_sb", [1, H], F32))
        rateB = T(nc.sbuf_tensor("rateB", [128, H], F32))
        qoff = T(nc.sbuf_tensor("qoff", [128, H * QG], F32))
        biasHG = T(nc.sbuf_tensor("biasHG", [128, H * QG * KT], F32))
        gate_sb = T(nc.sbuf_tensor("gate_sb", [128, NCH * E], BF16))
        stats = T(nc.sbuf_tensor("stats", [128, 32], F32))
        gwrk = T(nc.sbuf_tensor("gwrk", [128, 32], F32))
        rcp_sb = T(nc.sbuf_tensor("rcp_sb", [1, 512], F32))
        avbf = T(nc.sbuf_tensor("avbf", [128, 512], BF16))
        onesb = T(nc.sbuf_tensor("onesb", [1, 128], BF16))
        rcpb_sb = T(nc.sbuf_tensor("rcpb_sb", [1, 512], BF16))
        den_sb = T(nc.sbuf_tensor("den_sb", [1, 512], BF16))
        rcpt_sb = T(nc.sbuf_tensor("rcpt_sb", [128, 4], F32))
        rcptb_sb = T(nc.sbuf_tensor("rcptb_sb", [128, 4], BF16))

        e_ob = b.op("dve", [], lambda e: e.memset(onesb[:, :], 1.0), ev=True)
        ld_c = None
        for dst, src in ((ones_sb, c_ones), (iota_sb, c_iota),
                         (ek_sb, c_ek), (lxi_sb, lxi)):
            ld_c = b.dma("sp", [], dst[:, :], src[:, :], "ld_const")
        ld_cc = None
        for dst, src in ((id_sb, c_id), (sel_sb, c_sel), (swp_sb, c_swp)):
            ld_cc = b.dma("pool", [], dst[:, :], src[:, :], "ld_constc")
        ld_gate = b.dma("pool", [],
                        gate_sb[:].rearrange("p (c g) -> p c g", c=NCH),
                        gw.rearrange("(c p) g -> p c g", p=128), "ld_gate")

        e0 = b.op("act", [ld_c], lambda e: e.activation(
            rate_sb[:, :], lxi_sb[:, :], AF.Exp, scale=-1.0), ev=True)
        e0 = b.op("dve", [e0, ld_c], lambda e: e.tensor_mul(
            rate_sb[:, :], rate_sb[:, :], ek_sb[:, :]), ev=True)
        with nc.psum_tensor("ps_init", [128, 512], F32) as ps_init:
            e0 = b.op("pe", [e0, ld_c], lambda e: e.matmul(
                ps_init[:, 0:H], ones_sb[0:1, :], rate_sb[0:1, :],
                start=True, stop=True), ev=True)
            e0 = b.op("dve", [e0], lambda e: e.tensor_copy(
                rateB[:, :], ps_init[:, 0:H]), ev=True)
        e_bias = e0
        for h in range(H):
            for g in range(QG):
                col = h * QG + g
                e_q = b.op("dve", [e0], lambda e, h=h, g=g, col=col:
                           e.tensor_scalar_mul(
                    qoff[:, col:col + 1], rateB[:, h:h + 1],
                    float(512 * g + 511)), ev=True)
                e_bias = b.op("dve", [e_q], lambda e, h=h, col=col:
                              e.tensor_scalar(
                    biasHG[:, ts(col, KT)], iota_sb[:, :],
                    rateB[:, h:h + 1], qoff[:, col:col + 1],
                    ALU.mult, ALU.subtract), ev=True)

        for g in range(NGR):
            gsc = contextlib.ExitStack()
            with gsc:
                Tg = gsc.enter_context
                cos_sb = Tg(nc.sbuf_tensor(U("cosb"), [128, SEQ], BF16))
                sin_sb = Tg(nc.sbuf_tensor(U("sinb"), [128, SEQ], BF16))
                mask_sb = Tg(nc.sbuf_tensor(U("maskb"), [128, 4 * 512], BF16))
                comb_sb = Tg(nc.sbuf_tensor(U("combb"), [128, GB * NT * E], F32))
                ld_cs = b.dma("pool", [], cos_sb[:, :], c_cos[:, :], "ld_cs")
                ld_cs = b.dma("pool", [], sin_sb[:, :], c_sin[:, :], "ld_cs")
                ld_cs = b.dma("pool", [], mask_sb[:, :], c_mask[:, :], "ld_cs")
                spill_evs = []
                comb_last = None
                x2w_ev = None
                x2w_ev2 = None

                for bb in range(GB):
                    bidx = g * GB + bb
                    bsc = contextlib.ExitStack()
                    with bsc:
                        Tb = bsc.enter_context
                        fmj = Tb(nc.sbuf_tensor(U("fmj"), [128, NCH * SEQ], BF16))
                        q_sb = Tb(nc.sbuf_tensor(U("q_sb"), [128, NCH * SEQ], BF16))
                        k_sb = Tb(nc.sbuf_tensor(U("k_sb"), [128, NCH * SEQ], BF16))
                        v_sb = Tb(nc.sbuf_tensor(U("v_sb"), [128, NT * H * 65], BF16))

                        MARKS.append((f"A1 g{g}b{bb}", b._count["ev_pe"]))
                        a1 = contextlib.ExitStack()
                        with a1:
                            T1 = a1.enter_context
                            xrow = [T1(nc.sbuf_tensor(U(f"xr{i}"), [128, D], F32))
                                    for i in range(4)]
                            xln = [T1(nc.sbuf_tensor(U(f"xl{i}"), [128, D], BF16))
                                   for i in range(2)]
                            sq = T1(nc.sbuf_tensor(U("sq"), [128, D], BF16))
                            wsec = T1(nc.sbuf_tensor(U("wsec"), [128, NCH * D], BF16))
                            ps_t = [T1(nc.psum_tensor(U(f"pst{i}"), [128, 512], BF16))
                                    for i in range(2)]
                            qk_ps = [T1(nc.psum_tensor(U(f"qkp{i}"), [128, 512], F32))
                                     for i in range(2)]
                            v_ps = [T1(nc.psum_tensor(U(f"vps{i}"), [128, 1024], F32))
                                    for i in range(2)]

                            a1bar = b.bar()
                            e_vone = b.op("dve", [R.get("av_done")] + a1bar,
                                          lambda e: e.memset(
                                v_sb[:].rearrange("p (t c) -> p t c", c=65)
                                [:, :, 64:65], 1.0), ev=True)

                            prev_spill = R.get(("spill_war", bb))
                            ev_sq = R.get("sq_war")
                            for tt in range(NT):
                                xs = xrow[tt % 4]
                                lx = b.dma("sp",
                                           [R.get(("xr", tt % 4))] + a1bar,
                                           xs[:, :], x_in[bidx, ts(tt, 128), :],
                                           f"ldx{tt % 4}")
                                sc = (tt % 2) * 16
                                lnwar = R.get(("ln", sc))
                                e_sum = b.op("dve", [lx, lnwar],
                                             lambda e, xs=xs, sc=sc:
                                             e.reduce_sum(stats[:, sc:sc + 1],
                                                          xs[:, :], axis=AX.X),
                                             ev=True)
                                ev_sq = b.op("act", [lx, ev_sq, lnwar],
                                             lambda e, xs=xs, sc=sc:
                                             e.activation(
                                    sq[:, :], xs[:, :], AF.Square,
                                    accum_out=stats[:, sc + 1:sc + 2]), ev=True)
                                e_r, e_nm = ln_chain(b, stats, sc, e_sum, ev_sq, D)
                                xl = xln[tt % 2]
                                e_n = b.op("dve", [e_r, e_nm, R.get(("xl", tt % 2))],
                                           lambda e, xs=xs, xl=xl, sc=sc:
                                           e.tensor_scalar(
                                    xl[:, :], xs[:, :], stats[:, sc + 7:sc + 8],
                                    stats[:, sc + 6:sc + 7], ALU.add, ALU.mult),
                                    ev=True)
                                R[("xr", tt % 4)] = e_n
                                R[("ln", sc)] = e_n
                                for c2 in range(2):
                                    dep = [e_n, R.get(("pst", c2)), ld_cc,
                                           prev_spill]
                                    e_tp = None
                                    for cc in range(4):
                                        c = c2 * 4 + cc
                                        e_tp = b.op("pe", dep if cc == 0 else [],
                                                    lambda e, xl=xl, c=c, cc=cc,
                                                    c2=c2: e.transpose(
                                            ps_t[c2][:, ts(cc, 128)],
                                            xl[:, ts(c, 128)], id_sb[:, :]),
                                            ev=(cc == 3))
                                    e_ev = b.op("dve", [e_tp],
                                                lambda e, tt=tt, c2=c2:
                                                e.tensor_copy(
                                        fmj[:].rearrange("p (c t) -> p c t",
                                                         c=NCH)
                                        [:, slice(c2 * 4, c2 * 4 + 4),
                                         ts(tt, 128)],
                                        ps_t[c2][:, :].rearrange(
                                            "p (c t) -> p c t", c=4)), ev=True)
                                    R[("pst", c2)] = e_ev
                                R[("xl", tt % 2)] = b.last("pe")
                            R["sq_war"] = ev_sq
                            a1_fmj = b.last("dve")

                            ev_wsec_user = R.get("wsec_war")
                            MARKS.append((f"QKV g{g}b{bb}", b._count["ev_pe"]))
                            for sec, colbase in (("v", 2 * D), ("q", 0),
                                                 ("k", D)):
                                lw = b.dma("pool", [ev_wsec_user] + a1bar,
                                           wsec[:].rearrange(
                                               "p (c f) -> p c f", c=NCH),
                                           wqkv[:, colbase:colbase + D]
                                           .rearrange("(c p) f -> p c f", p=128),
                                           "ld_wsec")
                                if sec == "v":
                                    for tt in range(NT):
                                        vp = v_ps[tt % 2]
                                        dep0 = [lw, a1_fmj,
                                                R.get(("vps", tt % 2))]
                                        last_mm = None
                                        for vg in range(2):
                                            for c in range(NCH):
                                                last_mm = b.op(
                                                    "pe",
                                                    dep0 if (vg == 0 and c == 0)
                                                    else [],
                                                    lambda e, vp=vp, vg=vg, c=c,
                                                    tt=tt: e.matmul(
                                                        vp[:, ts(vg, 512)],
                                                        fmj[:].rearrange(
                                                            "p (c t) -> p c t",
                                                            c=NCH)
                                                        [:, c, ts(tt, 128)],
                                                        wsec[:, c * D + vg * 512:
                                                             c * D + vg * 512
                                                             + 512],
                                                        start=(c == 0),
                                                        stop=(c == NCH - 1)),
                                                    ev=(vg == 1 and c == NCH - 1))
                                        e_ve = b.op("dve", [last_mm, e_vone],
                                                    lambda e, vp=vp, tt=tt:
                                                    e.tensor_copy(
                                            v_sb[:, tt * H * 65:
                                                 (tt + 1) * H * 65]
                                            .rearrange("p (h c) -> p h c", h=H)
                                            [:, :, 0:64],
                                            vp[:, :].rearrange(
                                                "p (h c) -> p h c", c=64)),
                                            ev=True)
                                        R[("vps", tt % 2)] = e_ve
                                    ev_wsec_user = last_mm
                                else:
                                    dst = q_sb if sec == "q" else k_sb
                                    for ft in range(NCH):
                                        for tg in range(QG):
                                            i2 = (ft * QG + tg) % 2
                                            qp = qk_ps[i2]
                                            dep0 = [lw, a1_fmj,
                                                    R.get(("qkp", i2))]
                                            last_mm = None
                                            for c in range(NCH):
                                                last_mm = b.op(
                                                    "pe", dep0 if c == 0 else [],
                                                    lambda e, qp=qp, ft=ft,
                                                    tg=tg, c=c: e.matmul(
                                                        qp[:, :],
                                                        wsec[:, c * D + ft * 128:
                                                             c * D + ft * 128
                                                             + 128],
                                                        fmj[:].rearrange(
                                                            "p (c t) -> p c t",
                                                            c=NCH)
                                                        [:, c, ts(tg, 512)],
                                                        start=(c == 0),
                                                        stop=(c == NCH - 1)),
                                                    ev=(c == NCH - 1))
                                            e_qe = b.op("dve", [last_mm],
                                                        lambda e, qp=qp,
                                                        dst=dst, ft=ft, tg=tg:
                                                        e.tensor_copy(
                                                dst[:, ft * SEQ + tg * 512:
                                                    ft * SEQ + tg * 512 + 512],
                                                qp[:, :]), ev=True)
                                            R[("qkp", i2)] = e_qe
                                    ev_wsec_user = last_mm
                            R["wsec_war"] = ev_wsec_user
                            a1_done = b.last("dve")

                        MARKS.append((f"A2 g{g}b{bb}", b._count["ev_pe"]))
                        a2 = contextlib.ExitStack()
                        with a2:
                            T2 = a2.enter_context
                            probs = T2(nc.sbuf_tensor(U("probs"), [128, 8 * 512],
                                                      BF16))
                            kmask = T2(nc.sbuf_tensor(U("kmask"), [128, 2 * SEQ],
                                                      BF16))
                            rs = T2(nc.sbuf_tensor(U("rs"), [128, 2 * SEQ], BF16))
                            s_ps = [T2(nc.psum_tensor(U(f"sps{i}"), [128, 1024],
                                                      F32)) for i in range(2)]
                            av_ps = [T2(nc.psum_tensor(U(f"avp{i}"), [128, 512],
                                                       F32)) for i in range(2)]
                            bc_ps = [T2(nc.psum_tensor(U(f"bcp{i}"), [128, 512],
                                                       F32)) for i in range(2)]
                            km_ps = [sp[:, 0:128] for sp in s_ps]

                            ev_sps = {}
                            ev_rs = None
                            NSL = SEQ // 512
                            for ch in ROPE_CHUNKS:
                                for srcb in (q_sb, k_sb):
                                    chunk = srcb[:, ts(ch, SEQ)]
                                    swp = rs[:, 0:SEQ]
                                    m1 = rs[:, SEQ:2 * SEQ]
                                    e_cps = []
                                    last_smm = None
                                    for sl in range(NSL):
                                        sp = s_ps[sl % 2]
                                        last_smm = b.op(
                                            "pe",
                                            [a1_done, ld_cc, ld_cs,
                                             ev_sps.get(sl % 2)],
                                            lambda e, sp=sp, ch=ch, sl=sl,
                                            srcb=srcb: e.matmul(
                                                sp[:, 0:512], swp_sb[:, :],
                                                srcb[:, ch * SEQ + sl * 512:
                                                     ch * SEQ + sl * 512 + 512],
                                                start=True, stop=True), ev=True)
                                        e_c = b.op("dve", [last_smm, ev_rs],
                                                   lambda e, sp=sp, sl=sl:
                                                   e.tensor_copy(
                                            rs[:, ts(sl, 512)], sp[:, 0:512]),
                                            ev=True)
                                        ev_sps[sl % 2] = e_c
                                        e_cps.append(e_c)
                                    e_m1 = b.op("dve", e_cps + [ld_cs],
                                                lambda e, chunk=chunk, m1=m1:
                                                e.tensor_mul(
                                        m1, chunk, cos_sb[:, :]), ev=True)
                                    e_sw = b.op("dve", [e_m1],
                                                lambda e, swp=swp:
                                                e.tensor_mul(
                                        swp, swp, sin_sb[:, :]), ev=True)
                                    ev_rs = b.op("dve", [e_sw, last_smm],
                                                 lambda e, chunk=chunk, m1=m1,
                                                 swp=swp: e.tensor_add(
                                        chunk, m1, swp), ev=True)
                            rope_done = b.last("dve")

                            ev_probs = {}
                            pend_bc = [None]

                            def flush_bc():
                                if pend_bc[0] is None:
                                    return
                                (e_rc0, bcp0, i20, rb0, ch0, qg0, h0) = pend_bc[0]
                                pend_bc[0] = None
                                e_dt = None
                                for j in range(4):
                                    e_dt = b.op("pe", [e_rc0, e_ob,
                                                       R.get(("bcp", i20))]
                                                if j == 0 else [],
                                                lambda e, bcp0=bcp0, j=j:
                                                e.matmul(
                                        bcp0[:, j:j + 1],
                                        den_sb[0:1, ts(j, 128)],
                                        onesb[0:1, 0:1],
                                        start=True, stop=True),
                                        ev=(j == 3))
                                R["rcp"] = e_dt
                                e_rt = b.op("dve", [e_dt, R.get("rcpt")],
                                            lambda e, bcp0=bcp0:
                                            e.reciprocal(
                                    rcpt_sb[:, :], bcp0[:, 0:4]), ev=True)
                                e_rt = b.op("dve", [e_rt],
                                            lambda e: e.tensor_copy(
                                    rcptb_sb[:, :], rcpt_sb[:, :]), ev=True)
                                e_rr = None
                                for j in range(4):
                                    e_rr = b.op("pe", [e_rt] if j == 0 else [],
                                                lambda e, bcp0=bcp0, j=j:
                                                e.matmul(
                                        bcp0[0:1, ts(j, 128)],
                                        rcptb_sb[:, j:j + 1], id_sb[:, :],
                                        start=True, stop=True),
                                        ev=(j == 3))
                                R["rcpt"] = e_rr
                                e_cb = b.op("dve", [e_rr, R.get("rcpb")],
                                            lambda e, bcp0=bcp0:
                                            e.tensor_copy(
                                    rcpb_sb[0:1, :], bcp0[0:1, :]), ev=True)
                                e_bc = b.op("pe", [e_cb],
                                            lambda e, bcp0=bcp0: e.matmul(
                                    bcp0[0:64, :], onesb[0:1, 0:64],
                                    rcpb_sb[0:1, :], start=True, stop=True),
                                    ev=True)
                                R["rcpb"] = e_bc
                                e_cpa = b.op("dve", [R.get(("avpe", i20)),
                                                     R.get("avbf")],
                                             lambda e, i20=i20:
                                             e.tensor_copy(
                                    avbf[0:64, :], av_ps[i20][0:64, :]),
                                    ev=True)
                                R[("avp", i20)] = e_cpa
                                e_fin = b.op("dve", [e_cpa, e_bc],
                                             lambda e, bcp0=bcp0, rb0=rb0,
                                             ch0=ch0, qg0=qg0: e.tensor_mul(
                                    fmj[rb0:rb0 + 64,
                                        ch0 * SEQ + qg0 * 512:
                                        ch0 * SEQ + qg0 * 512 + 512],
                                    avbf[0:64, :], bcp0[0:64, :]), ev=True)
                                R["avbf"] = e_fin
                                R[("bcp", i20)] = e_fin
                                R[("attnT", h0, qg0)] = e_fin

                            for h in range(H):
                                ch = h // 2
                                rope = PI_BITS[h] > 0
                                decay = E_BITS[h] > 0
                                rb = (h % 2) * 64
                                kb_ev = [None] * KT
                                if rope:
                                    selap = sel_sb[:, ts(h % 2, 128)]
                                    for t in range(KT):
                                        kp = km_ps[t % 2]
                                        e_mm = b.op("pe",
                                                    [rope_done, ld_cc,
                                                     ev_sps.get(t % 2),
                                                     R.get(("kmp", t % 2))],
                                                    lambda e, kp=kp,
                                                    selap=selap, ch=ch, t=t:
                                                    e.matmul(
                                            kp, selap,
                                            k_sb[:, ch * SEQ + t * 128:
                                                 ch * SEQ + t * 128 + 128],
                                            start=True, stop=True), ev=True)
                                        e_cp = b.op("dve", [e_mm],
                                                    lambda e, kp=kp, h=h, t=t:
                                                    e.tensor_copy(
                                            kmask[:, (h % 2) * SEQ + t * 128:
                                                  (h % 2) * SEQ + t * 128 + 128],
                                            kp), ev=True)
                                        R[("kmp", t % 2)] = e_cp
                                        ev_sps[t % 2] = e_cp
                                        kb_ev[t] = e_cp

                                for qg in range(QG):
                                    nkt = 4 * (qg + 1)
                                    npr = nkt // 2
                                    i2 = (h * QG + qg) % 2
                                    avp = av_ps[i2]
                                    pend_av = []

                                    def emit_av_pair(pi, avdep, avp=None,
                                                     h=None, nkt=None, i2=None):
                                        e_av = None
                                        t0 = 2 * pi
                                        for tt_ in (t0, t0 + 1):
                                            dd = [avdep] if tt_ == t0 else []
                                            if tt_ == 0:
                                                dd += [R.get(("avp", i2)),
                                                       R.get(("avpe", i2))]
                                            e_av = b.op(
                                                "pe", dd,
                                                lambda e, tt_=tt_, avp=avp,
                                                h=h, nkt=nkt: e.matmul(
                                                    avp[0:65, :],
                                                    v_sb[:, tt_ * H * 65
                                                         + h * 65:
                                                         tt_ * H * 65
                                                         + h * 65 + 65],
                                                    probs[:, ts(tt_ % 8, 512)],
                                                    start=(tt_ == 0),
                                                    stop=(tt_ == nkt - 1)),
                                                ev=True)
                                            ev_probs[tt_ % 8] = e_av
                                        return e_av

                                    for pr in range(npr):
                                        t0 = 2 * pr
                                        sp = s_ps[pr % 2]
                                        e_s = None
                                        for ih in range(2):
                                            t = t0 + ih
                                            if rope:
                                                lhs = kmask[:, (h % 2) * SEQ
                                                            + t * 128:
                                                            (h % 2) * SEQ
                                                            + t * 128 + 128]
                                                rhs = q_sb[:, ch * SEQ
                                                           + qg * 512:
                                                           ch * SEQ
                                                           + qg * 512 + 512]
                                            else:
                                                lhs = k_sb[rb:rb + 64,
                                                           ch * SEQ + t * 128:
                                                           ch * SEQ + t * 128
                                                           + 128]
                                                rhs = q_sb[rb:rb + 64,
                                                           ch * SEQ + qg * 512:
                                                           ch * SEQ + qg * 512
                                                           + 512]
                                            e_s = b.op(
                                                "pe",
                                                [kb_ev[t] if rope
                                                 else rope_done,
                                                 a1_done,
                                                 ev_sps.get(pr % 2)
                                                 if ih == 0 else None],
                                                lambda e, sp=sp, lhs=lhs,
                                                rhs=rhs, ih=ih: e.matmul(
                                                    sp[:, ts(ih, 512)],
                                                    lhs, rhs,
                                                    start=True, stop=True),
                                                ev=True)
                                            if pr == 0 and ih == 0:
                                                flush_bc()
                                        slot = t0 % 8
                                        pdep = [e_s, ev_probs.get(slot),
                                                ev_probs.get(slot + 1),
                                                e_bias, a1_done]
                                        if decay:
                                            e_x = None
                                            for ih in range(2):
                                                t = t0 + ih
                                                bcol = (h * QG + qg) * KT + t
                                                e_x = b.op(
                                                    "act", pdep,
                                                    lambda e, sp=sp, ih=ih,
                                                    slot=slot, bcol=bcol:
                                                    e.activation(
                                                        probs[:, ts(slot + ih,
                                                                    512)],
                                                        sp[:, ts(ih, 512)],
                                                        AF.Exp, scale=0.125,
                                                        bias=biasHG[:, bcol:
                                                                    bcol + 1]),
                                                    ev=True)
                                        else:
                                            e_x = b.op(
                                                "act", pdep,
                                                lambda e, sp=sp, slot=slot:
                                                e.activation(
                                                    probs[:, slot * 512:
                                                          slot * 512 + 1024],
                                                    sp[:, :], AF.Exp,
                                                    scale=0.125), ev=True)
                                        ev_sps[pr % 2] = e_x
                                        for ih in range(2):
                                            t = t0 + ih
                                            if t >= 4 * qg:
                                                m = t - 4 * qg
                                                e_x = b.op(
                                                    "dve", [e_x, ld_cs],
                                                    lambda e, slot=slot,
                                                    ih=ih, m=m: e.tensor_mul(
                                                        probs[:, ts(slot + ih,
                                                                    512)],
                                                        probs[:, ts(slot + ih,
                                                                    512)],
                                                        mask_sb[:, ts(m, 512)]),
                                                    ev=True)
                                        pend_av.append((pr, e_x))
                                        if len(pend_av) > 2:
                                            pi, dep_ = pend_av.pop(0)
                                            emit_av_pair(pi, dep_, avp=avp,
                                                         h=h, nkt=nkt, i2=i2)
                                    e_av = None
                                    for pi, dep_ in pend_av:
                                        e_av = emit_av_pair(pi, dep_, avp=avp,
                                                            h=h, nkt=nkt,
                                                            i2=i2)
                                    R[("avpe", i2)] = e_av
                                    e_rc = b.op("dve", [e_av, R.get("rcp")],
                                                lambda e, avp=avp:
                                                e.tensor_copy(
                                        den_sb[0:1, :], avp[64:65, :]),
                                        ev=True)
                                    flush_bc()
                                    pend_bc[0] = (e_rc, bc_ps[i2], i2, rb,
                                                  ch, qg, h)
                            flush_bc()
                            R["av_done"] = b.last("pe")

                        MARKS.append((f"A3 g{g}b{bb}", b._count["ev_pe"]))
                        a3 = contextlib.ExitStack()
                        with a3:
                            T3 = a3.enter_context
                            wo_sb = T3(nc.sbuf_tensor(U("wo_sb"), [128, NCH * D],
                                                      BF16))
                            xrow3 = [T3(nc.sbuf_tensor(U(f"x3r{i}"), [128, D],
                                                       F32)) for i in range(2)]
                            x2r = [T3(nc.sbuf_tensor(U(f"x2r{i}"), [128, D], F32))
                                   for i in range(2)]
                            xln3 = [T3(nc.sbuf_tensor(U(f"x3l{i}"), [128, D],
                                                      BF16)) for i in range(2)]
                            sq3 = T3(nc.sbuf_tensor(U("sq3"), [128, D], BF16))
                            ps_t3 = [T3(nc.psum_tensor(U(f"pt3{i}"), [128, 512],
                                                       BF16)) for i in range(2)]
                            wo_ps = [T3(nc.psum_tensor(U(f"wop{i}"), [128, 1024],
                                                       F32)) for i in range(2)]
                            g_ps = T3(nc.psum_tensor(U("g_ps"), [128, 512], F32))

                            a3bar = b.bar()
                            lwo = b.dma("pool", [R.get("wo_war")] + a3bar,
                                        wo_sb[:].rearrange("p (c f) -> p c f",
                                                           c=NCH),
                                        wo.rearrange("(c p) f -> p c f", p=128),
                                        "ld_wo")
                            ev_sq3 = R.get("sq_war")
                            ev_gate = []
                            lag = [None]

                            def emit_post(state):
                                ttp, xlp, e_np, last_mmp = state
                                e_tps = []
                                for c2 in range(2):
                                    dep = [e_np, R.get(("pt3", c2)),
                                           last_mmp, ld_cc]
                                    e_tp = None
                                    for cc in range(4):
                                        c = c2 * 4 + cc
                                        e_tp = b.op("pe",
                                                    dep if cc == 0 else [],
                                                    lambda e, xlp=xlp, c=c,
                                                    cc=cc, c2=c2: e.transpose(
                                            ps_t3[c2][:, ts(cc, 128)],
                                            xlp[:, ts(c, 128)], id_sb[:, :]),
                                            ev=(cc == 3))
                                    e_ev = b.op("dve", [e_tp],
                                                lambda e, ttp=ttp, c2=c2:
                                                e.tensor_copy(
                                        fmj[:].rearrange("p (c t) -> p c t",
                                                         c=NCH)
                                        [:, slice(c2 * 4, c2 * 4 + 4),
                                         ts(ttp, 128)],
                                        ps_t3[c2][:, :].rearrange(
                                            "p (c t) -> p c t", c=4)), ev=True)
                                    R[("pt3", c2)] = e_ev
                                    e_tps.append(e_ev)
                                R[("x3l", ttp % 2)] = b.last("pe")
                                last_g = None
                                for c in range(NCH):
                                    last_g = b.op(
                                        "pe",
                                        (e_tps + [ld_gate] + ev_gate[0])
                                        if c == 0 else [],
                                        lambda e, c=c, ttp=ttp: e.matmul(
                                            g_ps[:, 0:E],
                                            fmj[:].rearrange(
                                                "p (c t) -> p c t", c=NCH)
                                            [:, c, ts(ttp, 128)],
                                            gate_sb[:, ts(c, E)],
                                            start=(c == 0),
                                            stop=(c == NCH - 1)),
                                        ev=(c == NCH - 1))
                                evg, cl = emit_top2(
                                    b, g_ps, gwrk, comb_sb,
                                    (bb * NT + ttp) * E, last_g,
                                    R.get("gwrk_war"))
                                ev_gate[0] = evg
                                R["gwrk_war"] = cl
                                return cl

                            ev_gate = [[]]
                            ev_gate = ev_gate
                            ev_gate[0] = []
                            for tt in range(NT):
                                qg = min(tt // 4, QG - 1)
                                wop = wo_ps[tt % 2]
                                xs = xrow3[tt % 2]
                                lx = b.dma("sp",
                                           [R.get(("x3r", tt % 2))] + a3bar,
                                           xs[:, :], x_in[bidx, ts(tt, 128), :],
                                           f"ldx3{tt % 2}")
                                dep0 = ([lwo, R.get(("wop", tt % 2))]
                                        + [R[("attnT", h, qg)]
                                           for h in range(H)])
                                last_mm = None
                                for dg in range(2):
                                    for c in range(NCH):
                                        last_mm = b.op(
                                            "pe",
                                            dep0 if (dg == 0 and c == 0)
                                            else [],
                                            lambda e, wop=wop, dg=dg, c=c,
                                            tt=tt: e.matmul(
                                                wop[:, ts(dg, 512)],
                                                fmj[:].rearrange(
                                                    "p (c t) -> p c t", c=NCH)
                                                [:, c, ts(tt, 128)],
                                                wo_sb[:, c * D + dg * 512:
                                                      c * D + dg * 512 + 512],
                                                start=(c == 0),
                                                stop=(c == NCH - 1)),
                                            ev=(dg == 1 and c == NCH - 1))
                                x2 = x2r[tt % 2]
                                e_x2 = b.op("dve",
                                            [last_mm, lx,
                                             R.get(("x2r", tt % 2)),
                                             R.get(("x2w", tt % 2))],
                                            lambda e, x2=x2, xs=xs, wop=wop:
                                            e.tensor_add(x2[:, :], xs[:, :],
                                                         wop[:, :]), ev=True)
                                R[("x3r", tt % 2)] = e_x2
                                R[("wop", tt % 2)] = e_x2
                                x2w = b.dma("sp", [e_x2],
                                            out[bidx, ts(tt, 128), :],
                                            x2[:, :], f"st_out{tt % 2}")
                                R[("x2w", tt % 2)] = x2w
                                if tt % 2 == 0:
                                    x2w_ev = x2w
                                else:
                                    x2w_ev2 = x2w
                                sc = (tt % 2) * 16
                                lnwar = R.get(("ln", sc))
                                e_sum = b.op("dve", [e_x2, lnwar],
                                             lambda e, x2=x2, sc=sc:
                                             e.reduce_sum(
                                    stats[:, sc:sc + 1], x2[:, :], axis=AX.X),
                                    ev=True)
                                ev_sq3 = b.op("act", [e_x2, ev_sq3, lnwar],
                                              lambda e, x2=x2, sc=sc:
                                              e.activation(
                                    sq3[:, :], x2[:, :], AF.Square,
                                    accum_out=stats[:, sc + 1:sc + 2]), ev=True)
                                e_r, e_nm = ln_chain(b, stats, sc, e_sum,
                                                     ev_sq3, D)
                                xl = xln3[tt % 2]
                                e_n = b.op("dve",
                                           [e_r, e_nm, R.get(("x3l", tt % 2))],
                                           lambda e, x2=x2, xl=xl, sc=sc:
                                           e.tensor_scalar(
                                    xl[:, :], x2[:, :],
                                    stats[:, sc + 7:sc + 8],
                                    stats[:, sc + 6:sc + 7],
                                    ALU.add, ALU.mult), ev=True)
                                R[("x2r", tt % 2)] = e_n
                                R[("ln", sc)] = e_n
                                if lag[0] is not None:
                                    comb_last = emit_post(lag[0])
                                lag[0] = (tt, xl, e_n, last_mm)
                            comb_last = emit_post(lag[0])
                            lag[0] = None
                            R["sq_war"] = ev_sq3
                            R["wo_war"] = b.last("pe")
                            spl_dep = ([R[("pt3", 0)], R[("pt3", 1)],
                                        b.last("pe")]
                                       + [("ld_xs0", b._count.get("ld_xs0", 0)),
                                          ("ld_xs1", b._count.get("ld_xs1", 0))])
                            spl = b.dma("sp",
                                        [d for d in spl_dep
                                         if d[1] > 0 or d[0].startswith("ev")],
                                        spill[bb].rearrange("c p t -> p c t"),
                                        fmj[:].rearrange("p (c t) -> p c t",
                                                         c=NCH),
                                        f"spill{bb}")
                            spill_evs.append(spl)
                            R[("spill_war", bb)] = spl

                MARKS.append((f"B g{g}", b._count["ev_pe"]))
                bsc2 = contextlib.ExitStack()
                with bsc2:
                    TB = bsc2.enter_context
                    FH = F // 2
                    FTH = FT // 2
                    w1_sb = [TB(nc.sbuf_tensor(U("w1_sb"), [128, NCH * FH],
                                               BF16)) for _ in range(2)]
                    w2_sb = [TB(nc.sbuf_tensor(U("w2_sb"), [128, FTH * D],
                                               BF16)) for _ in range(2)]
                    hT = TB(nc.sbuf_tensor(U("hT"), [128, FTH * 512], BF16))
                    xs_sb = [TB(nc.sbuf_tensor(U("xsb"), [128, NCH * 512],
                                               BF16)) for i in range(2)]
                    tmp_sb = [TB(nc.sbuf_tensor(U("tmpb"), [128, D], F32))
                              for i in range(2)]
                    h_ps = [TB(nc.psum_tensor(U("hps"), [128, 512], F32))
                            for i in range(2)]
                    o_ps = [TB(nc.psum_tensor(U("ops"), [128, 512], F32))
                            for i in range(4)]

                    bbar = b.bar()

                    def load_slot(sl, deps):
                        e_i, fh = divmod(sl, 2)
                        buf = sl % 2
                        lw1 = None
                        for part in range(2):
                            lw1 = b.dma("pool", deps if part == 0 else [],
                                        w1_sb[buf][:].rearrange(
                                            "p (c f) -> p c f", c=NCH)
                                        [:, ts(part, NCH // 2), :],
                                        w1[e_i, :, ts(fh, FH)].rearrange(
                                            "(c p) f -> p c f", p=128)
                                        [:, ts(part, NCH // 2), :],
                                        f"ld_w1{buf}")
                        lw2 = None
                        for part in range(2):
                            lw2 = b.dma("pool", [],
                                        w2_sb[buf][:].rearrange(
                                            "p (c d) -> p c d", c=FTH)
                                        [:, ts(part, FTH // 2), :],
                                        w2[e_i, ts(fh, FH), :].rearrange(
                                            "(c p) d -> p c d", p=128)
                                        [:, ts(part, FTH // 2), :],
                                        f"ld_w2{buf}")
                        return [lw1, lw2]

                    ev_hT = {}
                    ev_hps = {}
                    ev_op_war = {}
                    ev_xs = {}
                    ev_tmp = {}
                    out2_last = None
                    slot_pe_end = {}
                    wload = {0: load_slot(0, bbar)}
                    for sl in range(2 * E):
                        e_i, fh = divmod(sl, 2)
                        buf = sl % 2
                        if sl + 1 < 2 * E:
                            wload[sl + 1] = load_slot(
                                sl + 1, [slot_pe_end.get(sl - 1)] + bbar)
                        lw = wload[sl]
                        for tgi in range(NTG):
                            bb2, tq = divmod(tgi, SEQ // 512)
                            xsl = xs_sb[tgi % 2]
                            lxs = b.dma("act",
                                        [ev_xs.get(tgi % 2)] + spill_evs
                                        + bbar,
                                        xsl[:].rearrange("p (c t) -> p c t",
                                                         c=NCH),
                                        spill[bb2][:, :, ts(tq, 512)]
                                        .rearrange("c p t -> p c t"),
                                        f"ld_xs{tgi % 2}")
                            last_h = None
                            for ft in range(FTH):
                                hp = h_ps[ft % 2]
                                dep0 = lw + [lxs, ev_hps.get(ft % 2),
                                             out2_last]
                                for c in range(NCH):
                                    last_h = b.op(
                                        "pe", dep0 if c == 0 else [],
                                        lambda e, hp=hp, ft=ft, c=c, xsl=xsl,
                                        buf=buf: e.matmul(
                                            hp[:, :],
                                            w1_sb[buf][:, c * FH + ft * 128:
                                                       c * FH + ft * 128
                                                       + 128],
                                            xsl[:, ts(c, 512)],
                                            start=(c == 0),
                                            stop=(c == NCH - 1)),
                                        ev=(c == NCH - 1))
                                e_g = b.op("act",
                                           [last_h, ev_hT.get(ft), out2_last],
                                           lambda e, hp=hp, ft=ft:
                                           e.activation(
                                    hT[:, ts(ft, 512)], hp[:, :],
                                    AF.Gelu_apprx_tanh), ev=True)
                                ev_hps[ft % 2] = e_g
                                ev_hT[ft] = e_g
                            ev_xs[tgi % 2] = last_h
                            for half in range(2):
                                last_o = {}
                                for ft in range(FTH):
                                    for j in range(2):
                                        for dg in range(2):
                                            pi = j * 2 + dg
                                            dd = [ev_hT[ft]]
                                            if ft == 0:
                                                dd.append(ev_op_war.get(pi))
                                            last_o[pi] = b.op(
                                                "pe", dd,
                                                lambda e, pi=pi, ft=ft, j=j,
                                                dg=dg, half=half, buf=buf:
                                                e.matmul(
                                                    o_ps[pi][:, :],
                                                    hT[:, ft * 512
                                                       + (half * 2 + j) * 128:
                                                       ft * 512
                                                       + (half * 2 + j) * 128
                                                       + 128],
                                                    w2_sb[buf][:, ft * D
                                                               + dg * 512:
                                                               ft * D
                                                               + dg * 512
                                                               + 512],
                                                    start=(ft == 0),
                                                    stop=(ft == FTH - 1)),
                                                ev=(ft == FTH - 1))
                                out2_last = last_o[3]
                                for j in range(2):
                                    gt = tgi * 4 + half * 2 + j
                                    bb3, tt3 = divmod(gt, NT)
                                    ccol = gt * E + e_i
                                    tmp = tmp_sb[j]
                                    e_t = None
                                    for dg in range(2):
                                        pi = j * 2 + dg
                                        e_t = b.op(
                                            "dve",
                                            [last_o[pi], ev_tmp.get(j),
                                             comb_last],
                                            lambda e, tmp=tmp, dg=dg, pi=pi,
                                            ccol=ccol: e.tensor_scalar_mul(
                                                tmp[:, ts(dg, 512)],
                                                o_ps[pi][:, :],
                                                comb_sb[:, ccol:ccol + 1]),
                                            ev=True)
                                        ev_op_war[pi] = e_t
                                    e_a = b.dma("pool",
                                                [e_t, x2w_ev, x2w_ev2,
                                                 R.get(("acc", gt))],
                                                out[g * GB + bb3,
                                                    ts(tt3, 128), :],
                                                tmp[:, :], f"acc{gt}",
                                                accum_op=ALU.add)
                                    R[("acc", gt)] = e_a
                                    ev_tmp[j] = e_a
                        slot_pe_end[sl] = out2_last

        final_deps = [("st_out0", b._count["st_out0"]),
                      ("st_out1", b._count.get("st_out1", 0))]
        final_deps = [d for d in final_deps if d[1] > 0]
        for gt in range(GB * (SEQ // 128)):
            sn = f"acc{gt}"
            if sn in b._count:
                final_deps.append((sn, b._count[sn]))
        b.op("sp", final_deps, lambda e: e.nop())
    return nc


def emit_top2(b, g_ps, gwrk, comb_sb, ccol, e_gmm, war):
    e1 = b.op("dve", [e_gmm, war], lambda e: e.reduce_max(
        gwrk[:, 8:9], g_ps[:, 0:E], axis=AX.X, negate=True), ev=True)
    e2 = b.op("act", [e1, war], lambda e: e.activation(
        gwrk[:, 0:E], g_ps[:, 0:E], AF.Exp, bias=gwrk[:, 8:9]), ev=True)
    e3 = b.op("dve", [e2], lambda e: e.reduce_sum(
        gwrk[:, 9:10], gwrk[:, 0:E], axis=AX.X), ev=True)
    e4 = b.op("dve", [e3], lambda e: e.reciprocal(
        gwrk[:, 10:11], gwrk[:, 9:10]), ev=True)
    e5 = b.op("dve", [e4], lambda e: e.tensor_scalar_mul(
        gwrk[:, 0:E], gwrk[:, 0:E], gwrk[:, 10:11]), ev=True)
    e6 = b.op("dve", [e5], lambda e: e.reduce_max(
        gwrk[:, 11:12], gwrk[:, 0:E], axis=AX.X), ev=True)
    e7 = b.op("dve", [e6], lambda e: e.tensor_scalar(
        gwrk[:, 16:16 + E], gwrk[:, 0:E], gwrk[:, 11:12], None, ALU.is_lt),
        ev=True)
    e8 = b.op("dve", [e7], lambda e: e.tensor_mul(
        gwrk[:, 16:16 + E], gwrk[:, 0:E], gwrk[:, 16:16 + E]), ev=True)
    e9 = b.op("dve", [e8], lambda e: e.reduce_max(
        gwrk[:, 12:13], gwrk[:, 16:16 + E], axis=AX.X), ev=True)
    e10 = b.op("dve", [e9], lambda e: e.tensor_scalar(
        gwrk[:, 16:16 + E], gwrk[:, 0:E], gwrk[:, 12:13], None, ALU.is_ge),
        ev=True)
    e11 = b.op("dve", [e10], lambda e: e.tensor_mul(
        gwrk[:, 16:16 + E], gwrk[:, 0:E], gwrk[:, 16:16 + E]), ev=True)
    e12 = b.op("dve", [e11], lambda e: e.reduce_sum(
        gwrk[:, 13:14], gwrk[:, 16:16 + E], axis=AX.X), ev=True)
    e13 = b.op("dve", [e12], lambda e: e.reciprocal(
        gwrk[:, 14:15], gwrk[:, 13:14]), ev=True)
    e14 = b.op("dve", [e13], lambda e: e.tensor_scalar_mul(
        comb_sb[:, ccol:ccol + E], gwrk[:, 16:16 + E], gwrk[:, 14:15]),
        ev=True)
    return [e1, e2], e14


def host_constants(SEQ, KT):
    ident = np.eye(128, dtype=np.float32)
    ones = np.ones((1, 128), dtype=np.float32)
    sel = np.zeros((128, 256), dtype=np.float32)
    for r in list(range(0, 32)) + list(range(64, 96)):
        sel[r, r] = 1.0
    for r in list(range(32, 64)) + list(range(96, 128)):
        sel[r, 128 + r] = 1.0
    inv_freq = ROPE_BASE ** (-(np.arange(0, DH, 2, dtype=np.float64) / DH))
    pos = np.arange(SEQ, dtype=np.float64)
    theta = inv_freq[:, None] * pos[None, :]
    c64 = np.concatenate([np.cos(theta)] * 2, 0)
    s64 = np.concatenate([np.sin(theta)] * 2, 0)
    cos = np.concatenate([c64, c64], 0).astype(np.float32)
    sin = np.concatenate([-s64, s64], 0).astype(np.float32)
    swp = np.zeros((128, 128), dtype=np.float32)
    swp[np.arange(64), 64 + np.arange(64)] = 1.0
    swp[64 + np.arange(64), np.arange(64)] = 1.0
    mask = np.zeros((128, 4 * 512), dtype=np.float32)
    p = np.arange(128)[:, None]
    q = np.arange(512)[None, :]
    for m in range(4):
        mask[:, m * 512:(m + 1) * 512] = (128 * m + p <= q).astype(np.float32)
    iota = (np.arange(128)[:, None]
            + 128 * np.arange(KT)[None, :]).astype(np.float32)
    return {"c_id": ident, "c_ones": ones, "c_sel": sel, "c_cos": cos,
            "c_sin": sin, "c_mask": mask, "c_iota": iota, "c_swp": swp,
            "c_ek": E_BITS.reshape(1, H).astype(np.float32)}


def permute_wqkv(w):
    w = np.asarray(w)
    perm = np.arange(3 * D)
    within = np.concatenate([np.arange(0, 64, 2), 64 + np.arange(0, 64, 2),
                             np.arange(1, 64, 2), 64 + np.arange(1, 64, 2)])
    for s in range(2):
        for c in ROPE_CHUNKS:
            base = s * D + c * 128
            perm[base:base + 128] = base + within
    return np.ascontiguousarray(w[:, perm])


_NC_CACHE = {}


def kernel(L0_attn_x, L0_attn_w_qkv, L0_attn_w_o, L0_attn_log_xi,
           L0_ffn_gate_w, L0_ffn_w1, L0_ffn_w2):
    NB_TOT, SEQ, _ = L0_attn_x.shape
    NB = NB_TOT // N_CORES
    GB = 2 if NB % 2 == 0 else 1
    F = L0_ffn_w1.shape[2]
    key = (NB, SEQ, GB, F)
    if key not in _NC_CACHE:
        _NC_CACHE[key] = build_program(NB, SEQ, GB, F)
    nc = _NC_CACHE[key]
    consts = host_constants(SEQ, SEQ // 128)
    shared = {
        "wqkv": permute_wqkv(L0_attn_w_qkv),
        "wo": np.ascontiguousarray(np.asarray(L0_attn_w_o)),
        "gw": np.ascontiguousarray(np.asarray(L0_ffn_gate_w)),
        "w1": np.ascontiguousarray(np.asarray(L0_ffn_w1)),
        "w2": np.ascontiguousarray(np.asarray(L0_ffn_w2)),
        "lxi": np.ascontiguousarray(
            np.asarray(L0_attn_log_xi).reshape(1, H)),
        **consts,
    }
    x = np.ascontiguousarray(np.asarray(L0_attn_x))
    in_maps = [{"x": x[i * NB:(i + 1) * NB], **shared}
               for i in range(N_CORES)]
    res = run_bass_kernel_spmd(nc, in_maps, core_ids=list(range(N_CORES)))
    return np.concatenate([res.results[i]["out"] for i in range(N_CORES)],
                          axis=0)


# version 09
# speedup vs baseline: 1.2721x; 1.2721x over previous
import contextlib

import numpy as np
import ml_dtypes
import concourse.bass as bass
import concourse.mybir as mybir
from concourse.bass_utils import run_bass_kernel_spmd
from concourse.bass_core_rust import IndirectOffsetOnAxis

F32 = mybir.dt.float32
BF16 = mybir.dt.bfloat16
I32 = mybir.dt.int32
I16 = mybir.dt.int16
AF = mybir.ActivationFunctionType
ALU = mybir.AluOpType
AX = mybir.AxisListType

N_CORES = 8
D = 1024
NCH = 8
H = 16
DH = 64
E = 8
ROPE_BASE = 10000.0
HEAD_TYPES = np.arange(H) % 4
PI_BITS = ((HEAD_TYPES >> 1) & 1).astype(np.float32)
E_BITS = (HEAD_TYPES & 1).astype(np.float32)
ROPE_CHUNKS = [c for c in range(NCH) if PI_BITS[2 * c] > 0]


class EB:
    ENGINES = ("pe", "act", "dve", "pool", "sp")

    def __init__(self, nc, ctx):
        self.nc = nc
        self.ctx = ctx
        self.eng = {"pe": nc.tensor, "act": nc.scalar, "dve": nc.vector,
                    "pool": nc.gpsimd, "sp": nc.sync}
        self._sems = {}
        self._count = {}
        self._waited = {e: {} for e in self.ENGINES}
        for e in self.ENGINES:
            self._mk("ev_" + e)

    def _mk(self, name):
        h = self.ctx.enter_context(self.nc.semaphore(name))
        self._sems[name] = h
        self._count[name] = 0
        return h

    def sem(self, name):
        if name not in self._sems:
            self._mk(name)
        return self._sems[name]

    def _wait(self, engine, deps):
        for dep in deps:
            if dep is None:
                continue
            sname, val = dep
            if self._waited[engine].get(sname, 0) >= val:
                continue
            self.eng[engine].wait_ge(self._sems[sname], val)
            self._waited[engine][sname] = val

    def op(self, engine, deps, fn, ev=False):
        self._wait(engine, [d for d in deps if d is not None])
        inst = fn(self.eng[engine])
        if ev:
            sname = "ev_" + engine
            self._count[sname] += 1
            inst.then_inc(self._sems[sname], 1)
            return (sname, self._count[sname])
        return None

    def dma(self, engine, deps, out, in_, sem_name, **kw):
        self._wait(engine, [d for d in deps if d is not None])
        s = self.sem(sem_name)
        inst = self.eng[engine].dma_start(out=out, in_=in_, **kw)
        self._count[sem_name] += 16
        inst.then_inc(s, 16)
        return (sem_name, self._count[sem_name])

    def idma(self, engine, deps, fn, sem_name):
        self._wait(engine, [d for d in deps if d is not None])
        sm = self.sem(sem_name)
        inst = fn(self.eng[engine])
        self._count[sem_name] += 16
        inst.then_inc(sm, 16)
        return (sem_name, self._count[sem_name])

    def last(self, engine):
        sname = "ev_" + engine
        return (sname, self._count[sname])

    def bar(self):
        return [(s, c) for s, c in self._count.items() if c > 0]


def ts(i, n):
    return slice(i * n, (i + 1) * n)


def ln_chain(b, stats, sc, e_sum, e_sq, n):
    e_m = b.op("dve", [e_sum], lambda e: e.tensor_scalar_mul(
        stats[:, sc + 2:sc + 3], stats[:, sc:sc + 1], 1.0 / n), ev=True)
    e_x2 = b.op("dve", [e_sq], lambda e: e.tensor_scalar_mul(
        stats[:, sc + 3:sc + 4], stats[:, sc + 1:sc + 2], 1.0 / n), ev=True)
    e_m2 = b.op("dve", [e_m], lambda e: e.tensor_mul(
        stats[:, sc + 4:sc + 5], stats[:, sc + 2:sc + 3],
        stats[:, sc + 2:sc + 3]), ev=True)
    e_v = b.op("dve", [e_x2, e_m2], lambda e: e.tensor_sub(
        stats[:, sc + 5:sc + 6], stats[:, sc + 3:sc + 4],
        stats[:, sc + 4:sc + 5]), ev=True)
    e_v = b.op("dve", [e_v], lambda e: e.tensor_scalar_add(
        stats[:, sc + 5:sc + 6], stats[:, sc + 5:sc + 6], 1e-5), ev=True)
    e_l = b.op("act", [e_v], lambda e: e.activation(
        stats[:, sc + 5:sc + 6], stats[:, sc + 5:sc + 6], AF.Ln), ev=True)
    e_r = b.op("act", [e_l], lambda e: e.activation(
        stats[:, sc + 6:sc + 7], stats[:, sc + 5:sc + 6], AF.Exp,
        scale=-0.5), ev=True)
    e_nm = b.op("dve", [e_m], lambda e: e.tensor_scalar_mul(
        stats[:, sc + 7:sc + 8], stats[:, sc + 2:sc + 3], -1.0), ev=True)
    return e_r, e_nm


MARKS = []


def build_program(NB, SEQ, GB, F):
    KT = SEQ // 128
    QG = SEQ // 512
    NT = SEQ // 128
    FT = F // 128
    NGR = NB // GB
    NTG = (SEQ * GB) // 512

    nc = bass.Bass("TRN2", target_bir_lowering=False)
    ap = lambda t: t.ap()
    x_in = ap(nc.dram_tensor("x", [NB, SEQ, D], F32, kind="ExternalInput"))
    wqkv = ap(nc.dram_tensor("wqkv", [D, 3 * D], F32, kind="ExternalInput"))
    wo = ap(nc.dram_tensor("wo", [D, D], F32, kind="ExternalInput"))
    gw = ap(nc.dram_tensor("gw", [D, E], F32, kind="ExternalInput"))
    w1 = ap(nc.dram_tensor("w1", [E, D, F], F32, kind="ExternalInput"))
    w2 = ap(nc.dram_tensor("w2", [E, F, D], F32, kind="ExternalInput"))
    lxi = ap(nc.dram_tensor("lxi", [1, H], F32, kind="ExternalInput"))
    c_id = ap(nc.dram_tensor("c_id", [128, 128], F32, kind="ExternalInput"))
    c_ones = ap(nc.dram_tensor("c_ones", [1, 128], F32, kind="ExternalInput"))
    c_sel = ap(nc.dram_tensor("c_sel", [128, 256], F32, kind="ExternalInput"))
    c_cos = ap(nc.dram_tensor("c_cos", [128, SEQ], F32, kind="ExternalInput"))
    c_sin = ap(nc.dram_tensor("c_sin", [128, SEQ], F32, kind="ExternalInput"))
    c_swp = ap(nc.dram_tensor("c_swp", [128, 128], F32, kind="ExternalInput"))
    c_mask = ap(nc.dram_tensor("c_mask", [128, 4 * 512], F32, kind="ExternalInput"))
    c_iota = ap(nc.dram_tensor("c_iota", [128, KT], F32, kind="ExternalInput"))
    c_ek = ap(nc.dram_tensor("c_ek", [1, H], F32, kind="ExternalInput"))
    c_triu = ap(nc.dram_tensor("c_triu", [128, 128], F32, kind="ExternalInput"))
    c_io8 = ap(nc.dram_tensor("c_io8", [128, 8], F32, kind="ExternalInput"))
    c_ids = ap(nc.dram_tensor("c_ids", [GB * SEQ, 120], BF16,
                              kind="ExternalInput"))
    c_pay = ap(nc.dram_tensor("c_pay", [128, (GB * SEQ) // 128], I32,
                              kind="ExternalInput"))
    out = ap(nc.dram_tensor("out", [NB * SEQ + 128, D], F32,
                             kind="ExternalOutput"))
    TOKG = GB * SEQ
    CAP = TOKG // 2
    CS = CAP + 128
    RW = 1152
    spill_tok = ap(nc.dram_tensor("spill_tok", [GB * SEQ + 128, RW], BF16,
                                  kind="Internal"))
    slotmap = ap(nc.dram_tensor("slotmap", [E * CS, 1], I32, kind="Internal"))
    outflat = out

    ctx = contextlib.ExitStack()
    _uq = [0]

    def U(name):
        _uq[0] += 1
        return f"{name}_{_uq[0]}"
    with ctx:
        b = EB(nc, ctx)
        T = ctx.enter_context
        R = {}

        id_sb = T(nc.sbuf_tensor("id_sb", [128, 128], BF16))
        swp_sb = T(nc.sbuf_tensor("swp_sb", [128, 128], BF16))
        ones_sb = T(nc.sbuf_tensor("ones_sb", [1, 128], F32))
        sel_sb = T(nc.sbuf_tensor("sel_sb", [128, 256], BF16))
        iota_sb = T(nc.sbuf_tensor("iota_sb", [128, KT], F32))
        ek_sb = T(nc.sbuf_tensor("ek_sb", [1, H], F32))
        lxi_sb = T(nc.sbuf_tensor("lxi_sb", [1, H], F32))
        rate_sb = T(nc.sbuf_tensor("rate_sb", [1, H], F32))
        rateB = T(nc.sbuf_tensor("rateB", [128, H], F32))
        gate_sb = T(nc.sbuf_tensor("gate_sb", [128, NCH * E], BF16))
        stats = T(nc.sbuf_tensor("stats", [128, 32], F32))
        gwrk = T(nc.sbuf_tensor("gwrk", [128, 48], F32))
        rcp_sb = T(nc.sbuf_tensor("rcp_sb", [1, 512], F32))
        avbf = T(nc.sbuf_tensor("avbf", [128, 512], BF16))
        onesb = T(nc.sbuf_tensor("onesb", [1, 128], BF16))
        rcpb_sb = T(nc.sbuf_tensor("rcpb_sb", [1, 512], BF16))
        den_sb = T(nc.sbuf_tensor("den_sb", [1, 512], BF16))
        rcpt_sb = T(nc.sbuf_tensor("rcpt_sb", [128, 4], F32))
        rcptb_sb = T(nc.sbuf_tensor("rcptb_sb", [128, 4], BF16))
        triu_sb = T(nc.sbuf_tensor("triu_sb", [128, 128], BF16))
        io8_sb = T(nc.sbuf_tensor("io8_sb", [128, 8], F32))
        idf_sb = T(nc.sbuf_tensor("idf_sb", [8, 8], F32))
        oor_sb = T(nc.sbuf_tensor("oor_sb", [128, RW], BF16))
        onescb = T(nc.sbuf_tensor("onescb", [128, 1], BF16))

        e_ob = b.op("dve", [], lambda e: e.memset(onesb[:, :], 1.0), ev=True)
        e_ocb = b.op("dve", [], lambda e: e.memset(onescb[:, :], 1.0), ev=True)
        e_oor = b.op("dve", [], lambda e: e.memset(oor_sb[:, :], 0.0), ev=True)
        pay_sb = T(nc.sbuf_tensor("pay_sb", [128, TOKG // 128], I32))
        czi_sb = T(nc.sbuf_tensor("czi_sb", [128, (E * CS) // 128], I32))
        e_czi = b.op("dve", [], lambda e: e.memset(czi_sb[:, :], TOKG),
                     ev=True)
        zsp = b.dma("sp", [e_oor], spill_tok[TOKG:TOKG + 128, :],
                    oor_sb[:, :], "zsp")
        ld_c = None
        for dst, src in ((ones_sb, c_ones), (iota_sb, c_iota),
                         (ek_sb, c_ek), (lxi_sb, lxi)):
            ld_c = b.dma("sp", [], dst[:, :], src[:, :], "ld_const")
        ld_cc = None
        for dst, src in ((id_sb, c_id), (sel_sb, c_sel), (swp_sb, c_swp),
                         (triu_sb, c_triu)):
            ld_cc = b.dma("pool", [], dst[:, :], src[:, :], "ld_constc")
        ld_c = b.dma("sp", [], io8_sb[:, :], c_io8[:, :], "ld_const")
        ld_c = b.dma("sp", [], idf_sb[:, :], c_id[0:8, 0:8], "ld_const")
        ld_c = b.dma("sp", [], pay_sb[:, :], c_pay[:, :], "ld_const")
        ld_ids = b.dma("sp", [],
                       spill_tok[0:TOKG, 1032:1152].rearrange(
                           "(t p) c -> p t c", p=128),
                       c_ids.rearrange("(t p) c -> p t c", p=128), "ld_ids")
        ld_gate = b.dma("pool", [],
                        gate_sb[:].rearrange("p (c g) -> p c g", c=NCH),
                        gw.rearrange("(c p) g -> p c g", p=128), "ld_gate")

        e0 = b.op("act", [ld_c], lambda e: e.activation(
            rate_sb[:, :], lxi_sb[:, :], AF.Exp, scale=-1.0), ev=True)
        e0 = b.op("dve", [e0, ld_c], lambda e: e.tensor_mul(
            rate_sb[:, :], rate_sb[:, :], ek_sb[:, :]), ev=True)
        with nc.psum_tensor("ps_init", [128, 512], F32) as ps_init:
            e0 = b.op("pe", [e0, ld_c], lambda e: e.matmul(
                ps_init[:, 0:H], ones_sb[0:1, :], rate_sb[0:1, :],
                start=True, stop=True), ev=True)
            e0 = b.op("dve", [e0], lambda e: e.tensor_copy(
                rateB[:, :], ps_init[:, 0:H]), ev=True)
        e_rateB = e0

        for g in range(NGR):
            gsc = contextlib.ExitStack()
            with gsc:
                Tg = gsc.enter_context
                comb_sb = Tg(nc.sbuf_tensor(U("combb"), [128, GB * NT * E], F32))
                selg = Tg(nc.sbuf_tensor(U("selg"), [128, GB * NT * E], BF16))
                idxg = Tg(nc.sbuf_tensor(U("idxg"), [128, GB * NT * 2], F32))
                gasc = contextlib.ExitStack()
                Ta = gasc.enter_context
                cos_sb = Ta(nc.sbuf_tensor(U("cosb"), [128, SEQ], BF16))
                sin_sb = Ta(nc.sbuf_tensor(U("sinb"), [128, SEQ], BF16))
                mask_sb = Ta(nc.sbuf_tensor(U("maskb"), [128, 4 * 512], BF16))
                qoff = Ta(nc.sbuf_tensor(U("qoff"), [128, H * QG], F32))
                biasHG = Ta(nc.sbuf_tensor(U("biasHG"), [128, H * QG * KT],
                                           F32))
                ld_cs = b.dma("pool", [], cos_sb[:, :], c_cos[:, :], "ld_cs")
                ld_cs = b.dma("pool", [], sin_sb[:, :], c_sin[:, :], "ld_cs")
                ld_cs = b.dma("pool", [], mask_sb[:, :], c_mask[:, :], "ld_cs")
                e_bias = e_rateB
                for h in range(H):
                    for qg_ in range(QG):
                        col = h * QG + qg_
                        e_q = b.op("dve", [e_rateB, R.get("biaswar")],
                                   lambda e, h=h, qg_=qg_, col=col:
                                   e.tensor_scalar_mul(
                            qoff[:, col:col + 1], rateB[:, h:h + 1],
                            float(512 * qg_ + 511)), ev=True)
                        e_bias = b.op("dve", [e_q], lambda e, h=h, col=col:
                                      e.tensor_scalar(
                            biasHG[:, ts(col, KT)], iota_sb[:, :],
                            rateB[:, h:h + 1], qoff[:, col:col + 1],
                            ALU.mult, ALU.subtract), ev=True)
                comb_last = None
                x2w_ev = None
                x2w_ev2 = None

                for bb in range(GB):
                    bidx = g * GB + bb
                    bsc = contextlib.ExitStack()
                    with bsc:
                        Tb = bsc.enter_context
                        fmj = Tb(nc.sbuf_tensor(U("fmj"), [128, NCH * SEQ], BF16))
                        q_sb = Tb(nc.sbuf_tensor(U("q_sb"), [128, NCH * SEQ], BF16))
                        k_sb = Tb(nc.sbuf_tensor(U("k_sb"), [128, NCH * SEQ], BF16))
                        v_sb = Tb(nc.sbuf_tensor(U("v_sb"), [128, NT * H * 65], BF16))

                        MARKS.append((f"A1 g{g}b{bb}", b._count["ev_pe"]))
                        a1 = contextlib.ExitStack()
                        with a1:
                            T1 = a1.enter_context
                            xrow = [T1(nc.sbuf_tensor(U(f"xr{i}"), [128, D], F32))
                                    for i in range(4)]
                            xln = [T1(nc.sbuf_tensor(U(f"xl{i}"), [128, D], BF16))
                                   for i in range(2)]
                            sq = T1(nc.sbuf_tensor(U("sq"), [128, D], BF16))
                            wsec = T1(nc.sbuf_tensor(U("wsec"), [128, NCH * D], BF16))
                            ps_t = [T1(nc.psum_tensor(U(f"pst{i}"), [128, 512], BF16))
                                    for i in range(2)]
                            qk_ps = [T1(nc.psum_tensor(U(f"qkp{i}"), [128, 512], F32))
                                     for i in range(2)]
                            v_ps = [T1(nc.psum_tensor(U(f"vps{i}"), [128, 1024], F32))
                                    for i in range(2)]

                            a1bar = b.bar()
                            e_vone = b.op("dve", [R.get("av_done")] + a1bar,
                                          lambda e: e.memset(
                                v_sb[:].rearrange("p (t c) -> p t c", c=65)
                                [:, :, 64:65], 1.0), ev=True)

                            prev_spill = R.get(("spill_war", bb))
                            ev_sq = R.get("sq_war")
                            for tt in range(NT):
                                xs = xrow[tt % 4]
                                lx = b.dma("sp",
                                           [R.get(("xr", tt % 4))] + a1bar,
                                           xs[:, :], x_in[bidx, ts(tt, 128), :],
                                           f"ldx{tt % 4}")
                                sc = (tt % 2) * 16
                                lnwar = R.get(("ln", sc))
                                e_sum = b.op("dve", [lx, lnwar],
                                             lambda e, xs=xs, sc=sc:
                                             e.reduce_sum(stats[:, sc:sc + 1],
                                                          xs[:, :], axis=AX.X),
                                             ev=True)
                                ev_sq = b.op("act", [lx, ev_sq, lnwar],
                                             lambda e, xs=xs, sc=sc:
                                             e.activation(
                                    sq[:, :], xs[:, :], AF.Square,
                                    accum_out=stats[:, sc + 1:sc + 2]), ev=True)
                                e_r, e_nm = ln_chain(b, stats, sc, e_sum, ev_sq, D)
                                xl = xln[tt % 2]
                                e_n = b.op("dve", [e_r, e_nm, R.get(("xl", tt % 2))],
                                           lambda e, xs=xs, xl=xl, sc=sc:
                                           e.tensor_scalar(
                                    xl[:, :], xs[:, :], stats[:, sc + 7:sc + 8],
                                    stats[:, sc + 6:sc + 7], ALU.add, ALU.mult),
                                    ev=True)
                                R[("xr", tt % 4)] = e_n
                                R[("ln", sc)] = e_n
                                for c2 in range(2):
                                    dep = [e_n, R.get(("pst", c2)), ld_cc,
                                           prev_spill]
                                    e_tp = None
                                    for cc in range(4):
                                        c = c2 * 4 + cc
                                        e_tp = b.op("pe", dep if cc == 0 else [],
                                                    lambda e, xl=xl, c=c, cc=cc,
                                                    c2=c2: e.transpose(
                                            ps_t[c2][:, ts(cc, 128)],
                                            xl[:, ts(c, 128)], id_sb[:, :]),
                                            ev=(cc == 3))
                                    e_ev = b.op("dve", [e_tp],
                                                lambda e, tt=tt, c2=c2:
                                                e.tensor_copy(
                                        fmj[:].rearrange("p (c t) -> p c t",
                                                         c=NCH)
                                        [:, slice(c2 * 4, c2 * 4 + 4),
                                         ts(tt, 128)],
                                        ps_t[c2][:, :].rearrange(
                                            "p (c t) -> p c t", c=4)), ev=True)
                                    R[("pst", c2)] = e_ev
                                R[("xl", tt % 2)] = b.last("pe")
                            R["sq_war"] = ev_sq
                            a1_fmj = b.last("dve")

                            ev_wsec_user = R.get("wsec_war")
                            MARKS.append((f"QKV g{g}b{bb}", b._count["ev_pe"]))
                            for sec, colbase in (("v", 2 * D), ("q", 0),
                                                 ("k", D)):
                                lw = b.dma("pool", [ev_wsec_user] + a1bar,
                                           wsec[:].rearrange(
                                               "p (c f) -> p c f", c=NCH),
                                           wqkv[:, colbase:colbase + D]
                                           .rearrange("(c p) f -> p c f", p=128),
                                           "ld_wsec")
                                if sec == "v":
                                    for tt in range(NT):
                                        vp = v_ps[tt % 2]
                                        dep0 = [lw, a1_fmj,
                                                R.get(("vps", tt % 2))]
                                        last_mm = None
                                        for vg in range(2):
                                            for c in range(NCH):
                                                last_mm = b.op(
                                                    "pe",
                                                    dep0 if (vg == 0 and c == 0)
                                                    else [],
                                                    lambda e, vp=vp, vg=vg, c=c,
                                                    tt=tt: e.matmul(
                                                        vp[:, ts(vg, 512)],
                                                        fmj[:].rearrange(
                                                            "p (c t) -> p c t",
                                                            c=NCH)
                                                        [:, c, ts(tt, 128)],
                                                        wsec[:, c * D + vg * 512:
                                                             c * D + vg * 512
                                                             + 512],
                                                        start=(c == 0),
                                                        stop=(c == NCH - 1)),
                                                    ev=(vg == 1 and c == NCH - 1))
                                        e_ve = b.op("dve", [last_mm, e_vone],
                                                    lambda e, vp=vp, tt=tt:
                                                    e.tensor_copy(
                                            v_sb[:, tt * H * 65:
                                                 (tt + 1) * H * 65]
                                            .rearrange("p (h c) -> p h c", h=H)
                                            [:, :, 0:64],
                                            vp[:, :].rearrange(
                                                "p (h c) -> p h c", c=64)),
                                            ev=True)
                                        R[("vps", tt % 2)] = e_ve
                                    ev_wsec_user = last_mm
                                else:
                                    dst = q_sb if sec == "q" else k_sb
                                    for ft in range(NCH):
                                        for tg in range(QG):
                                            i2 = (ft * QG + tg) % 2
                                            qp = qk_ps[i2]
                                            dep0 = [lw, a1_fmj,
                                                    R.get(("qkp", i2))]
                                            last_mm = None
                                            for c in range(NCH):
                                                last_mm = b.op(
                                                    "pe", dep0 if c == 0 else [],
                                                    lambda e, qp=qp, ft=ft,
                                                    tg=tg, c=c: e.matmul(
                                                        qp[:, :],
                                                        wsec[:, c * D + ft * 128:
                                                             c * D + ft * 128
                                                             + 128],
                                                        fmj[:].rearrange(
                                                            "p (c t) -> p c t",
                                                            c=NCH)
                                                        [:, c, ts(tg, 512)],
                                                        start=(c == 0),
                                                        stop=(c == NCH - 1)),
                                                    ev=(c == NCH - 1))
                                            e_qe = b.op("dve", [last_mm],
                                                        lambda e, qp=qp,
                                                        dst=dst, ft=ft, tg=tg:
                                                        e.tensor_copy(
                                                dst[:, ft * SEQ + tg * 512:
                                                    ft * SEQ + tg * 512 + 512],
                                                qp[:, :]), ev=True)
                                            R[("qkp", i2)] = e_qe
                                    ev_wsec_user = last_mm
                            R["wsec_war"] = ev_wsec_user
                            a1_done = b.last("dve")

                        MARKS.append((f"A2 g{g}b{bb}", b._count["ev_pe"]))
                        a2 = contextlib.ExitStack()
                        with a2:
                            T2 = a2.enter_context
                            probs = T2(nc.sbuf_tensor(U("probs"), [128, 8 * 512],
                                                      BF16))
                            kmask = T2(nc.sbuf_tensor(U("kmask"), [128, 2 * SEQ],
                                                      BF16))
                            rs = T2(nc.sbuf_tensor(U("rs"), [128, 2 * SEQ], BF16))
                            s_ps = [T2(nc.psum_tensor(U(f"sps{i}"), [128, 1024],
                                                      F32)) for i in range(2)]
                            av_ps = [T2(nc.psum_tensor(U(f"avp{i}"), [128, 512],
                                                       F32)) for i in range(2)]
                            bc_ps = [T2(nc.psum_tensor(U(f"bcp{i}"), [128, 512],
                                                       F32)) for i in range(2)]
                            km_ps = [sp[:, 0:128] for sp in s_ps]

                            ev_sps = {}
                            ev_rs = None
                            NSL = SEQ // 512
                            for ch in ROPE_CHUNKS:
                                for srcb in (q_sb, k_sb):
                                    chunk = srcb[:, ts(ch, SEQ)]
                                    swp = rs[:, 0:SEQ]
                                    m1 = rs[:, SEQ:2 * SEQ]
                                    e_cps = []
                                    last_smm = None
                                    for sl in range(NSL):
                                        sp = s_ps[sl % 2]
                                        last_smm = b.op(
                                            "pe",
                                            [a1_done, ld_cc, ld_cs,
                                             ev_sps.get(sl % 2)],
                                            lambda e, sp=sp, ch=ch, sl=sl,
                                            srcb=srcb: e.matmul(
                                                sp[:, 0:512], swp_sb[:, :],
                                                srcb[:, ch * SEQ + sl * 512:
                                                     ch * SEQ + sl * 512 + 512],
                                                start=True, stop=True), ev=True)
                                        e_c = b.op("dve", [last_smm, ev_rs],
                                                   lambda e, sp=sp, sl=sl:
                                                   e.tensor_copy(
                                            rs[:, ts(sl, 512)], sp[:, 0:512]),
                                            ev=True)
                                        ev_sps[sl % 2] = e_c
                                        e_cps.append(e_c)
                                    e_m1 = b.op("dve", e_cps + [ld_cs],
                                                lambda e, chunk=chunk, m1=m1:
                                                e.tensor_mul(
                                        m1, chunk, cos_sb[:, :]), ev=True)
                                    e_sw = b.op("dve", [e_m1],
                                                lambda e, swp=swp:
                                                e.tensor_mul(
                                        swp, swp, sin_sb[:, :]), ev=True)
                                    ev_rs = b.op("dve", [e_sw, last_smm],
                                                 lambda e, chunk=chunk, m1=m1,
                                                 swp=swp: e.tensor_add(
                                        chunk, m1, swp), ev=True)
                            rope_done = b.last("dve")

                            ev_probs = {}
                            pend_bc = [None]

                            def flush_bc():
                                if pend_bc[0] is None:
                                    return
                                (e_rc0, bcp0, i20, rb0, ch0, qg0, h0) = pend_bc[0]
                                pend_bc[0] = None
                                e_dt = None
                                for j in range(4):
                                    e_dt = b.op("pe", [e_rc0, e_ob,
                                                       R.get(("bcp", i20))]
                                                if j == 0 else [],
                                                lambda e, bcp0=bcp0, j=j:
                                                e.matmul(
                                        bcp0[:, j:j + 1],
                                        den_sb[0:1, ts(j, 128)],
                                        onesb[0:1, 0:1],
                                        start=True, stop=True),
                                        ev=(j == 3))
                                R["rcp"] = e_dt
                                e_rt = b.op("dve", [e_dt, R.get("rcpt")],
                                            lambda e, bcp0=bcp0:
                                            e.reciprocal(
                                    rcpt_sb[:, :], bcp0[:, 0:4]), ev=True)
                                e_rt = b.op("dve", [e_rt],
                                            lambda e: e.tensor_copy(
                                    rcptb_sb[:, :], rcpt_sb[:, :]), ev=True)
                                e_rr = None
                                for j in range(4):
                                    e_rr = b.op("pe", [e_rt] if j == 0 else [],
                                                lambda e, bcp0=bcp0, j=j:
                                                e.matmul(
                                        bcp0[0:1, ts(j, 128)],
                                        rcptb_sb[:, j:j + 1], id_sb[:, :],
                                        start=True, stop=True),
                                        ev=(j == 3))
                                R["rcpt"] = e_rr
                                e_cb = b.op("dve", [e_rr, R.get("rcpb")],
                                            lambda e, bcp0=bcp0:
                                            e.tensor_copy(
                                    rcpb_sb[0:1, :], bcp0[0:1, :]), ev=True)
                                e_bc = b.op("pe", [e_cb],
                                            lambda e, bcp0=bcp0: e.matmul(
                                    bcp0[0:64, :], onesb[0:1, 0:64],
                                    rcpb_sb[0:1, :], start=True, stop=True),
                                    ev=True)
                                R["rcpb"] = e_bc
                                e_cpa = b.op("dve", [R.get(("avpe", i20)),
                                                     R.get("avbf")],
                                             lambda e, i20=i20:
                                             e.tensor_copy(
                                    avbf[0:64, :], av_ps[i20][0:64, :]),
                                    ev=True)
                                R[("avp", i20)] = e_cpa
                                e_fin = b.op("dve", [e_cpa, e_bc],
                                             lambda e, bcp0=bcp0, rb0=rb0,
                                             ch0=ch0, qg0=qg0: e.tensor_mul(
                                    fmj[rb0:rb0 + 64,
                                        ch0 * SEQ + qg0 * 512:
                                        ch0 * SEQ + qg0 * 512 + 512],
                                    avbf[0:64, :], bcp0[0:64, :]), ev=True)
                                R["avbf"] = e_fin
                                R[("bcp", i20)] = e_fin
                                R[("attnT", h0, qg0)] = e_fin

                            for h in range(H):
                                ch = h // 2
                                rope = PI_BITS[h] > 0
                                decay = E_BITS[h] > 0
                                rb = (h % 2) * 64
                                kb_ev = [None] * KT
                                if rope:
                                    selap = sel_sb[:, ts(h % 2, 128)]
                                    for t in range(KT):
                                        kp = km_ps[t % 2]
                                        e_mm = b.op("pe",
                                                    [rope_done, ld_cc,
                                                     ev_sps.get(t % 2),
                                                     R.get(("kmp", t % 2))],
                                                    lambda e, kp=kp,
                                                    selap=selap, ch=ch, t=t:
                                                    e.matmul(
                                            kp, selap,
                                            k_sb[:, ch * SEQ + t * 128:
                                                 ch * SEQ + t * 128 + 128],
                                            start=True, stop=True), ev=True)
                                        e_cp = b.op("dve", [e_mm],
                                                    lambda e, kp=kp, h=h, t=t:
                                                    e.tensor_copy(
                                            kmask[:, (h % 2) * SEQ + t * 128:
                                                  (h % 2) * SEQ + t * 128 + 128],
                                            kp), ev=True)
                                        R[("kmp", t % 2)] = e_cp
                                        ev_sps[t % 2] = e_cp
                                        kb_ev[t] = e_cp

                                for qg in range(QG):
                                    nkt = 4 * (qg + 1)
                                    npr = nkt // 2
                                    i2 = (h * QG + qg) % 2
                                    avp = av_ps[i2]
                                    pend_av = []

                                    def emit_av_pair(pi, avdep, avp=None,
                                                     h=None, nkt=None, i2=None):
                                        e_av = None
                                        t0 = 2 * pi
                                        for tt_ in (t0, t0 + 1):
                                            dd = [avdep] if tt_ == t0 else []
                                            if tt_ == 0:
                                                dd += [R.get(("avp", i2)),
                                                       R.get(("avpe", i2))]
                                            e_av = b.op(
                                                "pe", dd,
                                                lambda e, tt_=tt_, avp=avp,
                                                h=h, nkt=nkt: e.matmul(
                                                    avp[0:65, :],
                                                    v_sb[:, tt_ * H * 65
                                                         + h * 65:
                                                         tt_ * H * 65
                                                         + h * 65 + 65],
                                                    probs[:, ts(tt_ % 8, 512)],
                                                    start=(tt_ == 0),
                                                    stop=(tt_ == nkt - 1)),
                                                ev=True)
                                            ev_probs[tt_ % 8] = e_av
                                        return e_av

                                    for pr in range(npr):
                                        t0 = 2 * pr
                                        sp = s_ps[pr % 2]
                                        e_s = None
                                        for ih in range(2):
                                            t = t0 + ih
                                            if rope:
                                                lhs = kmask[:, (h % 2) * SEQ
                                                            + t * 128:
                                                            (h % 2) * SEQ
                                                            + t * 128 + 128]
                                                rhs = q_sb[:, ch * SEQ
                                                           + qg * 512:
                                                           ch * SEQ
                                                           + qg * 512 + 512]
                                            else:
                                                lhs = k_sb[rb:rb + 64,
                                                           ch * SEQ + t * 128:
                                                           ch * SEQ + t * 128
                                                           + 128]
                                                rhs = q_sb[rb:rb + 64,
                                                           ch * SEQ + qg * 512:
                                                           ch * SEQ + qg * 512
                                                           + 512]
                                            e_s = b.op(
                                                "pe",
                                                [kb_ev[t] if rope
                                                 else rope_done,
                                                 a1_done,
                                                 ev_sps.get(pr % 2)
                                                 if ih == 0 else None],
                                                lambda e, sp=sp, lhs=lhs,
                                                rhs=rhs, ih=ih: e.matmul(
                                                    sp[:, ts(ih, 512)],
                                                    lhs, rhs,
                                                    start=True, stop=True),
                                                ev=True)
                                            if pr == 0 and ih == 0:
                                                flush_bc()
                                        slot = t0 % 8
                                        pdep = [e_s, ev_probs.get(slot),
                                                ev_probs.get(slot + 1),
                                                e_bias, a1_done]
                                        if decay:
                                            e_x = None
                                            for ih in range(2):
                                                t = t0 + ih
                                                bcol = (h * QG + qg) * KT + t
                                                e_x = b.op(
                                                    "act", pdep,
                                                    lambda e, sp=sp, ih=ih,
                                                    slot=slot, bcol=bcol:
                                                    e.activation(
                                                        probs[:, ts(slot + ih,
                                                                    512)],
                                                        sp[:, ts(ih, 512)],
                                                        AF.Exp, scale=0.125,
                                                        bias=biasHG[:, bcol:
                                                                    bcol + 1]),
                                                    ev=True)
                                        else:
                                            e_x = b.op(
                                                "act", pdep,
                                                lambda e, sp=sp, slot=slot:
                                                e.activation(
                                                    probs[:, slot * 512:
                                                          slot * 512 + 1024],
                                                    sp[:, :], AF.Exp,
                                                    scale=0.125), ev=True)
                                        ev_sps[pr % 2] = e_x
                                        for ih in range(2):
                                            t = t0 + ih
                                            if t >= 4 * qg:
                                                m = t - 4 * qg
                                                e_x = b.op(
                                                    "dve", [e_x, ld_cs],
                                                    lambda e, slot=slot,
                                                    ih=ih, m=m: e.tensor_mul(
                                                        probs[:, ts(slot + ih,
                                                                    512)],
                                                        probs[:, ts(slot + ih,
                                                                    512)],
                                                        mask_sb[:, ts(m, 512)]),
                                                    ev=True)
                                        pend_av.append((pr, e_x))
                                        if len(pend_av) > 2:
                                            pi, dep_ = pend_av.pop(0)
                                            emit_av_pair(pi, dep_, avp=avp,
                                                         h=h, nkt=nkt, i2=i2)
                                    e_av = None
                                    for pi, dep_ in pend_av:
                                        e_av = emit_av_pair(pi, dep_, avp=avp,
                                                            h=h, nkt=nkt,
                                                            i2=i2)
                                    R[("avpe", i2)] = e_av
                                    e_rc = b.op("dve", [e_av, R.get("rcp")],
                                                lambda e, avp=avp:
                                                e.tensor_copy(
                                        den_sb[0:1, :], avp[64:65, :]),
                                        ev=True)
                                    flush_bc()
                                    pend_bc[0] = (e_rc, bc_ps[i2], i2, rb,
                                                  ch, qg, h)
                            flush_bc()
                            R["av_done"] = b.last("pe")

                        MARKS.append((f"A3 g{g}b{bb}", b._count["ev_pe"]))
                        a3 = contextlib.ExitStack()
                        with a3:
                            T3 = a3.enter_context
                            wo_sb = T3(nc.sbuf_tensor(U("wo_sb"), [128, NCH * D],
                                                      BF16))
                            xrow3 = [T3(nc.sbuf_tensor(U(f"x3r{i}"), [128, D],
                                                       F32)) for i in range(2)]
                            x2r = [T3(nc.sbuf_tensor(U(f"x2r{i}"), [128, D], F32))
                                   for i in range(2)]
                            xln3 = [T3(nc.sbuf_tensor(U(f"x3l{i}"), [128, D],
                                                      BF16)) for i in range(2)]
                            sq3 = T3(nc.sbuf_tensor(U("sq3"), [128, D], BF16))
                            ps_t3 = [T3(nc.psum_tensor(U(f"pt3{i}"), [128, 512],
                                                       BF16)) for i in range(2)]
                            combbf = T3(nc.sbuf_tensor(U("combbf"), [128, 16],
                                                       BF16))
                            wo_ps = [T3(nc.psum_tensor(U(f"wop{i}"), [128, 1024],
                                                       F32)) for i in range(2)]
                            g_ps = T3(nc.psum_tensor(U("g_ps"), [128, 512], F32))

                            a3bar = b.bar()
                            lwo = b.dma("pool", [R.get("wo_war")] + a3bar,
                                        wo_sb[:].rearrange("p (c f) -> p c f",
                                                           c=NCH),
                                        wo.rearrange("(c p) f -> p c f", p=128),
                                        "ld_wo")
                            ev_sq3 = R.get("sq_war")
                            ev_gate = []
                            lag = [None]

                            def emit_post(state):
                                ttp, xlp, e_np, last_mmp = state
                                e_tps = []
                                for c2 in range(2):
                                    dep = [e_np, R.get(("pt3", c2)),
                                           last_mmp, ld_cc]
                                    e_tp = None
                                    for cc in range(4):
                                        c = c2 * 4 + cc
                                        e_tp = b.op("pe",
                                                    dep if cc == 0 else [],
                                                    lambda e, xlp=xlp, c=c,
                                                    cc=cc, c2=c2: e.transpose(
                                            ps_t3[c2][:, ts(cc, 128)],
                                            xlp[:, ts(c, 128)], id_sb[:, :]),
                                            ev=(cc == 3))
                                    e_ev = b.op("dve", [e_tp],
                                                lambda e, ttp=ttp, c2=c2:
                                                e.tensor_copy(
                                        fmj[:].rearrange("p (c t) -> p c t",
                                                         c=NCH)
                                        [:, slice(c2 * 4, c2 * 4 + 4),
                                         ts(ttp, 128)],
                                        ps_t3[c2][:, :].rearrange(
                                            "p (c t) -> p c t", c=4)), ev=True)
                                    R[("pt3", c2)] = e_ev
                                    e_tps.append(e_ev)
                                R[("x3l", ttp % 2)] = b.last("pe")
                                last_g = None
                                for c in range(NCH):
                                    last_g = b.op(
                                        "pe",
                                        (e_tps + [ld_gate] + ev_gate[0])
                                        if c == 0 else [],
                                        lambda e, c=c, ttp=ttp: e.matmul(
                                            g_ps[:, 0:E],
                                            fmj[:].rearrange(
                                                "p (c t) -> p c t", c=NCH)
                                            [:, c, ts(ttp, 128)],
                                            gate_sb[:, ts(c, E)],
                                            start=(c == 0),
                                            stop=(c == NCH - 1)),
                                        ev=(c == NCH - 1))
                                gti = bb * NT + ttp
                                evg, cl, _n7 = emit_top2(
                                    b, g_ps, gwrk, comb_sb,
                                    gti * E, last_g,
                                    R.get("gwrk_war"), selg=selg, idxg=idxg,
                                    io8=io8_sb, gti=gti)
                                ev_gate[0] = evg
                                R["gwrk_war"] = cl
                                cslot = (ttp % 2) * 8
                                e_cbf = b.op("dve",
                                             [cl, R.get(("cbf", ttp % 2))],
                                             lambda e, gti=gti, cslot=cslot:
                                             e.tensor_copy(
                                    combbf[:, cslot:cslot + 8],
                                    comb_sb[:, ts(gti, E)]), ev=True)
                                cw = b.dma("sp", [e_cbf],
                                           spill_tok[ts(gti, 128), 1024:1032],
                                           combbf[:, cslot:cslot + 8],
                                           f"csp{ttp % 2}")
                                R[("cbf", ttp % 2)] = cw
                                R[("rspl", gti)] = cw
                                return cl

                            ev_gate = [[]]
                            ev_gate = ev_gate
                            ev_gate[0] = []
                            for tt in range(NT):
                                qg = min(tt // 4, QG - 1)
                                wop = wo_ps[tt % 2]
                                xs = xrow3[tt % 2]
                                lx = b.dma("sp",
                                           [R.get(("x3r", tt % 2))] + a3bar,
                                           xs[:, :], x_in[bidx, ts(tt, 128), :],
                                           f"ldx3{tt % 2}")
                                dep0 = ([lwo, R.get(("wop", tt % 2))]
                                        + [R[("attnT", h, qg)]
                                           for h in range(H)])
                                last_mm = None
                                for dg in range(2):
                                    for c in range(NCH):
                                        last_mm = b.op(
                                            "pe",
                                            dep0 if (dg == 0 and c == 0)
                                            else [],
                                            lambda e, wop=wop, dg=dg, c=c,
                                            tt=tt: e.matmul(
                                                wop[:, ts(dg, 512)],
                                                fmj[:].rearrange(
                                                    "p (c t) -> p c t", c=NCH)
                                                [:, c, ts(tt, 128)],
                                                wo_sb[:, c * D + dg * 512:
                                                      c * D + dg * 512 + 512],
                                                start=(c == 0),
                                                stop=(c == NCH - 1)),
                                            ev=(dg == 1 and c == NCH - 1))
                                x2 = x2r[tt % 2]
                                e_x2 = b.op("dve",
                                            [last_mm, lx,
                                             R.get(("x2r", tt % 2)),
                                             R.get(("x2w", tt % 2))],
                                            lambda e, x2=x2, xs=xs, wop=wop:
                                            e.tensor_add(x2[:, :], xs[:, :],
                                                         wop[:, :]), ev=True)
                                R[("x3r", tt % 2)] = e_x2
                                R[("wop", tt % 2)] = e_x2
                                x2w = b.dma("sp", [e_x2],
                                            out[bidx * SEQ + tt * 128:
                                                bidx * SEQ + (tt + 1) * 128,
                                                :],
                                            x2[:, :], f"st_out{tt % 2}")
                                R[("x2w", tt % 2)] = x2w
                                if tt % 2 == 0:
                                    x2w_ev = x2w
                                else:
                                    x2w_ev2 = x2w
                                sc = (tt % 2) * 16
                                lnwar = R.get(("ln", sc))
                                e_sum = b.op("dve", [e_x2, lnwar],
                                             lambda e, x2=x2, sc=sc:
                                             e.reduce_sum(
                                    stats[:, sc:sc + 1], x2[:, :], axis=AX.X),
                                    ev=True)
                                ev_sq3 = b.op("act", [e_x2, ev_sq3, lnwar],
                                              lambda e, x2=x2, sc=sc:
                                              e.activation(
                                    sq3[:, :], x2[:, :], AF.Square,
                                    accum_out=stats[:, sc + 1:sc + 2]), ev=True)
                                e_r, e_nm = ln_chain(b, stats, sc, e_sum,
                                                     ev_sq3, D)
                                xl = xln3[tt % 2]
                                e_n = b.op("dve",
                                           [e_r, e_nm, R.get(("x3l", tt % 2)),
                                            R.get(("xsp", tt % 2))],
                                           lambda e, x2=x2, xl=xl, sc=sc:
                                           e.tensor_scalar(
                                    xl[:, :], x2[:, :],
                                    stats[:, sc + 7:sc + 8],
                                    stats[:, sc + 6:sc + 7],
                                    ALU.add, ALU.mult), ev=True)
                                R[("x2r", tt % 2)] = e_n
                                R[("ln", sc)] = e_n
                                gti0 = bb * NT + tt
                                xsp = b.dma("sp", [e_n],
                                            spill_tok[ts(gti0, 128), 0:1024],
                                            xl[:, :], f"xsp{tt % 2}")
                                R[("xsp", tt % 2)] = xsp
                                R[("rspl2", gti0)] = xsp
                                if lag[0] is not None:
                                    comb_last = emit_post(lag[0])
                                lag[0] = (tt, xl, e_n, last_mm)
                            comb_last = emit_post(lag[0])
                            lag[0] = None
                            R["sq_war"] = ev_sq3
                            R["wo_war"] = b.last("pe")

                MARKS.append((f"B g{g}", b._count["ev_pe"]))
                R["biaswar"] = b.last("act")
                gasc.close()
                GBNT = GB * NT
                NQ4 = GBNT // 4
                rsc = contextlib.ExitStack()
                with rsc:
                    TR = rsc.enter_context
                    cnt_sb = TR(nc.sbuf_tensor(U("cntb"), [8, GBNT], F32))
                    incl_sb = TR(nc.sbuf_tensor(U("inclb"), [8, GBNT], F32))
                    zz_sb = TR(nc.sbuf_tensor(U("zzb"), [8, GBNT], F32))
                    erow_sb = TR(nc.sbuf_tensor(U("erowb"), [1, 8], F32))
                    scr8 = TR(nc.sbuf_tensor(U("scr8b"), [128, 8], F32))
                    psl = TR(nc.sbuf_tensor(U("pslb"), [128, 2], F32))
                    dstf = TR(nc.sbuf_tensor(U("dstfb"), [128, 8 * NQ4], F32))
                    dsti32 = TR(nc.sbuf_tensor(U("dsti32"), [128, 8 * NQ4],
                                               I32))
                    ct_ps = TR(nc.psum_tensor(U("ctps"), [128, 512], F32))
                    po_ps = TR(nc.psum_tensor(U("pops"), [128, 512], F32))

                    rbar = b.bar()
                    b.dma("sp", [e_czi] + (R.get("sm_war") or []) + rbar,
                          slotmap[:, :].rearrange("(p j) one -> p (j one)",
                                                  p=128),
                          czi_sb[:, :], "rst_sm")
                    sel_done = b.last("dve")
                    e_cm = None
                    for t in range(GBNT):
                        e_cm = b.op("pe", [sel_done, e_ocb, ld_cc] +
                                    ([R.get("ct_war")] if t == 0 else []),
                                    lambda e, t=t: e.matmul(
                            ct_ps[0:8, t:t + 1], selg[:, ts(t, E)],
                            onescb[:, 0:1], start=True, stop=True),
                            ev=(t == GBNT - 1))
                    e_cc = b.op("dve", [e_cm], lambda e: e.tensor_copy(
                        cnt_sb[0:8, :], ct_ps[0:8, 0:GBNT]), ev=True)
                    R["ct_war"] = e_cc
                    e_zz = b.op("dve", [], lambda e: e.memset(
                        zz_sb[:, :], 0.0), ev=True)
                    e_sc = b.op("dve", [e_cc, e_zz],
                                lambda e: e.tensor_tensor_scan(
                        incl_sb[0:8, :], cnt_sb[0:8, :], zz_sb[0:8, :], 0.0,
                        ALU.add, ALU.add), ev=True)
                    e_ex = b.op("dve", [e_sc], lambda e: e.tensor_sub(
                        incl_sb[0:8, :], incl_sb[0:8, :], cnt_sb[0:8, :]),
                        ev=True)

                    e_er = None
                    e_pos = None
                    e_dst = {}
                    for t in range(GBNT):
                        e_e1 = b.op("pe", [e_ex, ld_c, e_er],
                                    lambda e, t=t: e.matmul(
                            ct_ps[0:1, 128:136], incl_sb[0:8, t:t + 1],
                            idf_sb[0:8, 0:8], start=True, stop=True), ev=True)
                        e_er = b.op("dve", [e_e1], lambda e: e.tensor_copy(
                            erow_sb[0:1, :], ct_ps[0:1, 128:136]), ev=True)
                        e_p1 = b.op("pe", [e_er, e_pos],
                                    lambda e, t=t: e.matmul(
                            po_ps[:, 0:8], triu_sb[:, :], selg[:, ts(t, E)],
                            start=True, stop=False), ev=False)
                        e_p2 = b.op("pe", [], lambda e: e.matmul(
                            po_ps[:, 0:8], ones_sb[0:1, 0:128],
                            erow_sb[0:1, 0:8], start=False, stop=True),
                            ev=True)
                        for k in range(2):
                            icol = idxg[:, 2 * t + k:2 * t + k + 1]
                            dcol = dstf[:, (t % 4) * 2 + k + 8 * (0):
                                        (t % 4) * 2 + k + 1] \
                                if False else dstf[:, (t // 4) * 8
                                                   + (t % 4) * 2 + k:
                                                   (t // 4) * 8
                                                   + (t % 4) * 2 + k + 1]
                            e_oh = b.op("dve",
                                        [e_p2, e_dst.get(k)] +
                                        ([R.get("dstf_war")] if t == 0
                                         else []),
                                        lambda e, icol=icol: e.tensor_scalar(
                                scr8[:, :], io8_sb[:, :], icol, None,
                                ALU.is_equal), ev=True)
                            e_mu = b.op("dve", [e_oh],
                                        lambda e: e.tensor_mul(
                                scr8[:, :], scr8[:, :], po_ps[:, 0:8]),
                                ev=True)
                            e_ps = b.op("dve", [e_mu],
                                        lambda e, k=k: e.reduce_sum(
                                psl[:, k:k + 1], scr8[:, :], axis=AX.X),
                                ev=True)
                            e_ps = b.op("dve", [e_ps],
                                        lambda e, k=k:
                                        e.tensor_scalar_min(
                                psl[:, k:k + 1], psl[:, k:k + 1],
                                float(CAP)), ev=True)
                            e_df = b.op("dve", [e_ps],
                                        lambda e, icol=icol, dcol=dcol:
                                        e.scalar_tensor_tensor(
                                dcol, icol, float(CS),
                                psl[:, k:k + 1],
                                ALU.mult, ALU.add), ev=True)
                            e_dst[k] = e_df
                        e_pos = b.last("dve")
                    e_conv = b.op("dve",
                                  [e_dst.get(0), e_dst.get(1),
                                   ("idsc", b._count.get("idsc", 0))
                                   if b._count.get("idsc", 0) else None],
                                  lambda e: e.tensor_copy(
                        dsti32[:, :], dstf[:, :]), ev=True)
                    R["dstf_war"] = e_conv
                    rst_sm_ev = ("rst_sm", b._count["rst_sm"])
                    for t in range(GBNT):
                        for k in range(2):
                            col = (t // 4) * 8 + (t % 4) * 2 + k
                            b.idma("pool",
                                   [e_conv, ld_c, rst_sm_ev],
                                   lambda e, t=t, col=col:
                                   e.indirect_dma_start(
                                slotmap[:, :],
                                IndirectOffsetOnAxis(
                                    dsti32[:, col:col + 1], 0),
                                pay_sb[:, t:t + 1],
                                None, bounds_check=None), "idsc")
                    route_done = ("idsc", b._count["idsc"])
                    spills_done = [
                        ("xsp0", b._count["xsp0"]),
                        ("xsp1", b._count["xsp1"]),
                        ("csp0", b._count["csp0"]),
                        ("csp1", b._count["csp1"]),
                        ("ld_ids", b._count["ld_ids"]),
                        ("zsp", b._count["zsp"])]

                bsc2 = contextlib.ExitStack()
                with bsc2:
                    TB = bsc2.enter_context
                    FH = F // 2
                    FTH = FT // 2
                    NTGS = CAP // 512
                    w1_sb = [TB(nc.sbuf_tensor(U("w1_sb"), [128, NCH * FH],
                                               BF16)) for _ in range(2)]
                    w2_sb = [TB(nc.sbuf_tensor(U("w2_sb"), [128, FTH * D],
                                               BF16)) for _ in range(2)]
                    hT = TB(nc.sbuf_tensor(U("hT"), [128, FTH * 512], BF16))
                    xsr = [TB(nc.sbuf_tensor(U("xsr"), [128, 4 * RW], BF16))
                           for i in range(2)]
                    xsT = TB(nc.sbuf_tensor(U("xsT"), [128, NCH * 512], BF16))
                    cfg_sb = [TB(nc.sbuf_tensor(U("cfg"), [128, 4], F32))
                              for i in range(2)]
                    idxB = [TB(nc.sbuf_tensor(U("idxB"), [128, 4], I32))
                            for i in range(2)]
                    sm_sb = [TB(nc.sbuf_tensor(U("smsb"), [128, 4], I32))
                             for i in range(2)]
                    idsc = [TB(nc.sbuf_tensor(U("idsc"), [128, 4], F32))
                            for i in range(2)]
                    tmp_sb = [TB(nc.sbuf_tensor(U("tmpb"), [128, D], F32))
                              for i in range(2)]
                    h_ps = [TB(nc.psum_tensor(U("hps"), [128, 512], F32))
                            for i in range(2)]
                    o_ps = [TB(nc.psum_tensor(U("ops"), [128, 512], F32))
                            for i in range(4)]
                    pt_ps = [TB(nc.psum_tensor(U("ptps"), [128, 512], BF16))
                             for i in range(2)]

                    bbar = b.bar()

                    def load_slot(sl, deps):
                        e_i, fh = divmod(sl, 2)
                        buf = sl % 2
                        lw1 = b.dma("pool", deps,
                                    w1_sb[buf][:].rearrange(
                                        "p (c f) -> p c f", c=NCH),
                                    w1[e_i, :, ts(fh, FH)].rearrange(
                                        "(c p) f -> p c f", p=128),
                                    f"ld_w1{buf}")
                        lw2 = b.dma("pool", [],
                                    w2_sb[buf][:].rearrange(
                                        "p (c d) -> p c d", c=FTH),
                                    w2[e_i, ts(fh, FH), :].rearrange(
                                        "(c p) d -> p c d", p=128),
                                    f"ld_w2{buf}")
                        return [lw1, lw2]

                    ev_hT = {}
                    ev_hps = {}
                    ev_op_war = {}
                    ev_xsr = {}
                    ev_cfg = {}
                    ev_cfg_use = {}
                    ev_dsg_use = {}
                    ev_tmp = {}
                    ev_pt = {}
                    out2_last = None
                    slot_pe_end = {}
                    wload = {0: load_slot(0, bbar)}
                    for sl in range(2 * E):
                        e_i, fh = divmod(sl, 2)
                        buf = sl % 2
                        if sl + 1 < 2 * E:
                            wload[sl + 1] = load_slot(
                                sl + 1, [slot_pe_end.get(sl - 1)] + bbar)
                        lw = wload[sl]
                        for tgi in range(NTGS):
                            xp = tgi % 2
                            xr = xsr[xp]
                            s0 = e_i * CS + tgi * 512
                            lsm = b.dma(
                                "sp",
                                [route_done]
                                + (R.get(("smR", xp)) or []) + bbar,
                                sm_sb[xp][:, :],
                                slotmap[s0:s0 + 512, :].rearrange(
                                    "(p j) one -> p (j one)", p=128),
                                f"ld_sm{xp}")
                            lxq = {}
                            for j4 in range(4):
                                hh = j4 // 2
                                lxq[j4] = b.idma(
                                    "pool",
                                    [lsm, ev_xsr.get((xp, hh)),
                                     ev_cfg.get((xp, j4))]
                                    + spills_done + bbar,
                                    lambda e, j4=j4, xr=xr, xp=xp:
                                    e.indirect_dma_start(
                                        xr[:, j4 * RW:(j4 + 1) * RW],
                                        None, spill_tok[:, :],
                                        IndirectOffsetOnAxis(
                                            sm_sb[xp][:, j4:j4 + 1], 0),
                                        bounds_check=None),
                                    f"gx{xp}{j4}")
                            R[("smR", xp)] = list(lxq.values())
                            e_tps = []
                            for j in range(4):
                                hh = j // 2
                                lxdep = [lxq[j]]
                                e_cf = b.op("dve",
                                            lxdep
                                            + [ev_cfg_use.get((xp, j))],
                                            lambda e, j=j, e_i=e_i, xp=xp,
                                            xr=xr: e.tensor_copy(
                                    cfg_sb[xp][:, j:j + 1],
                                    xr[:, j * RW + 1024 + e_i:
                                       j * RW + 1025 + e_i]), ev=True)
                                e_id = b.op("dve",
                                            [e_cf,
                                             ev_dsg_use.get((xp, j))],
                                            lambda e, j=j, xp=xp, xr=xr:
                                            e.scalar_tensor_tensor(
                                    idsc[xp][:, j:j + 1],
                                    xr[:, j * RW + 1032:j * RW + 1033],
                                    64.0,
                                    xr[:, j * RW + 1033:j * RW + 1034],
                                    ALU.mult, ALU.add), ev=True)
                                padoff = float(NB * SEQ - g * TOKG)
                                e_id = b.op("dve", [e_id],
                                            lambda e, j=j, xp=xp, xr=xr,
                                            padoff=padoff:
                                            e.scalar_tensor_tensor(
                                    idsc[xp][:, j:j + 1],
                                    xr[:, j * RW + 1034:j * RW + 1035],
                                    -padoff,
                                    idsc[xp][:, j:j + 1],
                                    ALU.mult, ALU.add), ev=True)
                                e_id = b.op("dve", [e_id],
                                            lambda e, j=j, xp=xp,
                                            padoff=padoff:
                                            e.tensor_single_scalar(
                                    idsc[xp][:, j:j + 1],
                                    idsc[xp][:, j:j + 1],
                                    padoff + g * TOKG, ALU.add), ev=True)
                                e_dv = b.op("dve", [e_id],
                                            lambda e, j=j, xp=xp:
                                            e.tensor_copy(
                                    idxB[xp][:, j:j + 1],
                                    idsc[xp][:, j:j + 1]), ev=True)
                                ev_cfg[(xp, j)] = e_dv
                                for c2 in range(2):
                                    dep = lxdep + [ev_pt.get(c2), ld_cc,
                                                   out2_last]
                                    e_tp = None
                                    for cc in range(4):
                                        c = c2 * 4 + cc
                                        e_tp = b.op(
                                            "pe", dep if cc == 0 else [],
                                            lambda e, j=j, c=c, cc=cc, c2=c2,
                                            xr=xr: e.transpose(
                                                pt_ps[c2][:, ts(cc, 128)],
                                                xr[:, j * RW + c * 128:
                                                   j * RW + c * 128 + 128],
                                                id_sb[:, :]),
                                            ev=(cc == 3))
                                    e_ev = b.op("dve", [e_tp],
                                                lambda e, j=j, c2=c2:
                                                e.tensor_copy(
                                        xsT[:].rearrange("p (c t) -> p c t",
                                                         c=NCH)
                                        [:, slice(c2 * 4, c2 * 4 + 4),
                                         ts(j, 128)],
                                        pt_ps[c2][:, :].rearrange(
                                            "p (c t) -> p c t", c=4)),
                                        ev=True)
                                    ev_pt[c2] = e_ev
                                    e_tps.append(e_ev)
                                if j % 2 == 1:
                                    ev_xsr[(xp, hh)] = b.last("pe")
                            last_h = None
                            for ft in range(FTH):
                                hp = h_ps[ft % 2]
                                dep0 = lw + e_tps + [ev_hps.get(ft % 2),
                                                     out2_last]
                                for c in range(NCH):
                                    last_h = b.op(
                                        "pe", dep0 if c == 0 else [],
                                        lambda e, hp=hp, ft=ft, c=c,
                                        buf=buf: e.matmul(
                                            hp[:, :],
                                            w1_sb[buf][:, c * FH + ft * 128:
                                                       c * FH + ft * 128
                                                       + 128],
                                            xsT[:, ts(c, 512)],
                                            start=(c == 0),
                                            stop=(c == NCH - 1)),
                                        ev=(c == NCH - 1))
                                e_g = b.op("act",
                                           [last_h, ev_hT.get(ft), out2_last],
                                           lambda e, hp=hp, ft=ft:
                                           e.activation(
                                    hT[:, ts(ft, 512)], hp[:, :],
                                    AF.Gelu_apprx_tanh), ev=True)
                                ev_hps[ft % 2] = e_g
                                ev_hT[ft] = e_g
                            for half in range(2):
                                last_o = {}
                                for ft in range(FTH):
                                    for j in range(2):
                                        for dg in range(2):
                                            pi = j * 2 + dg
                                            dd = [ev_hT[ft]]
                                            if ft == 0:
                                                dd.append(ev_op_war.get(pi))
                                            last_o[pi] = b.op(
                                                "pe", dd,
                                                lambda e, pi=pi, ft=ft, j=j,
                                                dg=dg, half=half, buf=buf:
                                                e.matmul(
                                                    o_ps[pi][:, :],
                                                    hT[:, ft * 512
                                                       + (half * 2 + j) * 128:
                                                       ft * 512
                                                       + (half * 2 + j) * 128
                                                       + 128],
                                                    w2_sb[buf][:, ft * D
                                                               + dg * 512:
                                                               ft * D
                                                               + dg * 512
                                                               + 512],
                                                    start=(ft == 0),
                                                    stop=(ft == FTH - 1)),
                                                ev=(ft == FTH - 1))
                                out2_last = last_o[3]
                                for j in range(2):
                                    jt = half * 2 + j
                                    tmp = tmp_sb[j]
                                    e_t = None
                                    for dg in range(2):
                                        pi = j * 2 + dg
                                        e_t = b.op(
                                            "dve",
                                            [last_o[pi], ev_tmp.get(j),
                                             ev_cfg.get((xp, jt))],
                                            lambda e, tmp=tmp, dg=dg, pi=pi,
                                            jt=jt, xp=xp:
                                            e.tensor_scalar_mul(
                                                tmp[:, ts(dg, 512)],
                                                o_ps[pi][:, :],
                                                cfg_sb[xp][:, jt:jt + 1]),
                                            ev=True)
                                        ev_op_war[pi] = e_t
                                    ev_cfg_use[(xp, jt)] = e_t
                                    e_a = b.idma(
                                        "pool",
                                        [e_t, ev_cfg[(xp, jt)],
                                         ("st_out0",
                                          b._count["st_out0"]),
                                         ("st_out1",
                                          b._count.get("st_out1", 0)),
                                         ("oacc", b._count.get("oacc", 0))],
                                        lambda e, jt=jt, xp=xp, tmp=tmp:
                                        e.indirect_dma_start(
                                            out[:, :],
                                            IndirectOffsetOnAxis(
                                                idxB[xp][:, jt:jt + 1], 0),
                                            tmp[:, :], None,
                                            bounds_check=None,
                                            compute_op=ALU.add), "oacc")
                                    ev_tmp[j] = e_a
                                    ev_dsg_use[(xp, jt)] = e_a
                        slot_pe_end[sl] = out2_last
                    R["sm_war"] = [
                        ("ld_sm0", b._count["ld_sm0"]),
                        ("ld_sm1", b._count.get("ld_sm1", 0))]
                    R["sm_war"] = [d for d in R["sm_war"] if d[1] > 0]

        final_deps = [("st_out0", b._count["st_out0"]),
                      ("st_out1", b._count.get("st_out1", 0)),
                      ("oacc", b._count.get("oacc", 0))]
        final_deps = [d for d in final_deps if d[1] > 0]
        b.op("sp", final_deps, lambda e: e.nop())
    return nc


def emit_top2(b, g_ps, gwrk, comb_sb, ccol, e_gmm, war,
              selg=None, idxg=None, io8=None, gti=0):
    e1 = b.op("dve", [e_gmm, war], lambda e: e.reduce_max(
        gwrk[:, 8:9], g_ps[:, 0:E], axis=AX.X, negate=True), ev=True)
    e2 = b.op("act", [e1, war], lambda e: e.activation(
        gwrk[:, 0:E], g_ps[:, 0:E], AF.Exp, bias=gwrk[:, 8:9]), ev=True)
    e3 = b.op("dve", [e2], lambda e: e.reduce_sum(
        gwrk[:, 9:10], gwrk[:, 0:E], axis=AX.X), ev=True)
    e4 = b.op("dve", [e3], lambda e: e.reciprocal(
        gwrk[:, 10:11], gwrk[:, 9:10]), ev=True)
    e5 = b.op("dve", [e4], lambda e: e.tensor_scalar_mul(
        gwrk[:, 0:E], gwrk[:, 0:E], gwrk[:, 10:11]), ev=True)
    e6 = b.op("dve", [e5], lambda e: e.reduce_max(
        gwrk[:, 11:12], gwrk[:, 0:E], axis=AX.X), ev=True)
    e7 = b.op("dve", [e6], lambda e: e.tensor_scalar(
        gwrk[:, 16:16 + E], gwrk[:, 0:E], gwrk[:, 11:12], None, ALU.is_lt),
        ev=True)
    e8 = b.op("dve", [e7], lambda e: e.tensor_mul(
        gwrk[:, 16:16 + E], gwrk[:, 0:E], gwrk[:, 16:16 + E]), ev=True)
    e9 = b.op("dve", [e8], lambda e: e.reduce_max(
        gwrk[:, 12:13], gwrk[:, 16:16 + E], axis=AX.X), ev=True)
    e10 = b.op("dve", [e9], lambda e: e.tensor_scalar(
        gwrk[:, 16:16 + E], gwrk[:, 0:E], gwrk[:, 12:13], None, ALU.is_ge),
        ev=True)
    n1 = b.op("dve", [e10], lambda e: e.tensor_copy(
        selg[:, ts(gti, E)], gwrk[:, 16:16 + E]), ev=True)
    n2 = b.op("dve", [e10], lambda e: e.tensor_scalar(
        gwrk[:, 24:32], gwrk[:, 0:E], gwrk[:, 11:12], None, ALU.is_ge),
        ev=True)
    n3 = b.op("dve", [n2], lambda e: e.tensor_sub(
        gwrk[:, 32:40], gwrk[:, 16:16 + E], gwrk[:, 24:32]), ev=True)
    n4 = b.op("dve", [n3], lambda e: e.tensor_mul(
        gwrk[:, 24:32], io8[:, :], gwrk[:, 24:32]), ev=True)
    n5 = b.op("dve", [n4], lambda e: e.reduce_sum(
        idxg[:, 2 * gti:2 * gti + 1], gwrk[:, 24:32], axis=AX.X), ev=True)
    n6 = b.op("dve", [n4], lambda e: e.tensor_mul(
        gwrk[:, 32:40], io8[:, :], gwrk[:, 32:40]), ev=True)
    n7 = b.op("dve", [n6], lambda e: e.reduce_sum(
        idxg[:, 2 * gti + 1:2 * gti + 2], gwrk[:, 32:40], axis=AX.X), ev=True)
    e11 = b.op("dve", [n1, n3], lambda e: e.tensor_mul(
        gwrk[:, 16:16 + E], gwrk[:, 0:E], gwrk[:, 16:16 + E]), ev=True)
    e12 = b.op("dve", [e11], lambda e: e.reduce_sum(
        gwrk[:, 13:14], gwrk[:, 16:16 + E], axis=AX.X), ev=True)
    e13 = b.op("dve", [e12], lambda e: e.reciprocal(
        gwrk[:, 14:15], gwrk[:, 13:14]), ev=True)
    e14 = b.op("dve", [e13], lambda e: e.tensor_scalar_mul(
        comb_sb[:, ccol:ccol + E], gwrk[:, 16:16 + E], gwrk[:, 14:15]),
        ev=True)
    return [e1, e2], e14, n7


def host_constants(SEQ, KT):
    ident = np.eye(128, dtype=np.float32)
    ones = np.ones((1, 128), dtype=np.float32)
    sel = np.zeros((128, 256), dtype=np.float32)
    for r in list(range(0, 32)) + list(range(64, 96)):
        sel[r, r] = 1.0
    for r in list(range(32, 64)) + list(range(96, 128)):
        sel[r, 128 + r] = 1.0
    inv_freq = ROPE_BASE ** (-(np.arange(0, DH, 2, dtype=np.float64) / DH))
    pos = np.arange(SEQ, dtype=np.float64)
    theta = inv_freq[:, None] * pos[None, :]
    c64 = np.concatenate([np.cos(theta)] * 2, 0)
    s64 = np.concatenate([np.sin(theta)] * 2, 0)
    cos = np.concatenate([c64, c64], 0).astype(np.float32)
    sin = np.concatenate([-s64, s64], 0).astype(np.float32)
    swp = np.zeros((128, 128), dtype=np.float32)
    swp[np.arange(64), 64 + np.arange(64)] = 1.0
    swp[64 + np.arange(64), np.arange(64)] = 1.0
    mask = np.zeros((128, 4 * 512), dtype=np.float32)
    p = np.arange(128)[:, None]
    q = np.arange(512)[None, :]
    for m in range(4):
        mask[:, m * 512:(m + 1) * 512] = (128 * m + p <= q).astype(np.float32)
    iota = (np.arange(128)[:, None]
            + 128 * np.arange(KT)[None, :]).astype(np.float32)
    return {"c_id": ident, "c_ones": ones, "c_sel": sel, "c_cos": cos,
            "c_sin": sin, "c_mask": mask, "c_iota": iota, "c_swp": swp,
            "c_ek": E_BITS.reshape(1, H).astype(np.float32),
            "c_triu": np.triu(np.ones((128, 128), np.float32), 1),
            "c_io8": np.broadcast_to(np.arange(8, dtype=np.float32),
                                     (128, 8)).copy()}


def host_ids(tokg):
    t = np.arange(tokg)
    ids = np.zeros((tokg, 120), np.float32)
    ids[:, 0] = t // 64
    ids[:, 1] = t % 64
    ids[:, 2] = 1.0
    return np.ascontiguousarray(ids.astype(ml_dtypes.bfloat16))


def host_pay(tokg):
    t = np.arange(tokg // 128)
    p = np.arange(128)
    return np.ascontiguousarray(
        (t[None, :] * 128 + p[:, None]).astype(np.int32))


def permute_wqkv(w):
    w = np.asarray(w)
    perm = np.arange(3 * D)
    within = np.concatenate([np.arange(0, 64, 2), 64 + np.arange(0, 64, 2),
                             np.arange(1, 64, 2), 64 + np.arange(1, 64, 2)])
    for s in range(2):
        for c in ROPE_CHUNKS:
            base = s * D + c * 128
            perm[base:base + 128] = base + within
    return np.ascontiguousarray(w[:, perm])


_NC_CACHE = {}


def kernel(L0_attn_x, L0_attn_w_qkv, L0_attn_w_o, L0_attn_log_xi,
           L0_ffn_gate_w, L0_ffn_w1, L0_ffn_w2):
    NB_TOT, SEQ, _ = L0_attn_x.shape
    NB = NB_TOT // N_CORES
    GB = 2 if NB % 2 == 0 else 1
    F = L0_ffn_w1.shape[2]
    key = (NB, SEQ, GB, F)
    if key not in _NC_CACHE:
        _NC_CACHE[key] = build_program(NB, SEQ, GB, F)
    nc = _NC_CACHE[key]
    consts = host_constants(SEQ, SEQ // 128)
    consts["c_ids"] = host_ids(GB * SEQ)
    consts["c_pay"] = host_pay(GB * SEQ)
    shared = {
        "wqkv": permute_wqkv(L0_attn_w_qkv),
        "wo": np.ascontiguousarray(np.asarray(L0_attn_w_o)),
        "gw": np.ascontiguousarray(np.asarray(L0_ffn_gate_w)),
        "w1": np.ascontiguousarray(np.asarray(L0_ffn_w1)),
        "w2": np.ascontiguousarray(np.asarray(L0_ffn_w2)),
        "lxi": np.ascontiguousarray(
            np.asarray(L0_attn_log_xi).reshape(1, H)),
        **consts,
    }
    x = np.ascontiguousarray(np.asarray(L0_attn_x))
    in_maps = [{"x": x[i * NB:(i + 1) * NB], **shared}
               for i in range(N_CORES)]
    res = run_bass_kernel_spmd(nc, in_maps, core_ids=list(range(N_CORES)))
    return np.concatenate(
        [res.results[i]["out"][:NB * SEQ].reshape(NB, SEQ, D)
         for i in range(N_CORES)],
        axis=0)


# version 10
# speedup vs baseline: 1.4713x; 1.1566x over previous
import contextlib

import numpy as np
import ml_dtypes
import concourse.bass as bass
import concourse.mybir as mybir
from concourse.bass_utils import run_bass_kernel_spmd
from concourse.bass_core_rust import IndirectOffsetOnAxis

F32 = mybir.dt.float32
BF16 = mybir.dt.bfloat16
I32 = mybir.dt.int32
I16 = mybir.dt.int16
AF = mybir.ActivationFunctionType
ALU = mybir.AluOpType
AX = mybir.AxisListType

N_CORES = 8
D = 1024
NCH = 8
H = 16
DH = 64
E = 8
ROPE_BASE = 10000.0
HEAD_TYPES = np.arange(H) % 4
PI_BITS = ((HEAD_TYPES >> 1) & 1).astype(np.float32)
E_BITS = (HEAD_TYPES & 1).astype(np.float32)
ROPE_CHUNKS = [c for c in range(NCH) if PI_BITS[2 * c] > 0]


class EB:
    ENGINES = ("pe", "act", "dve", "pool", "sp")

    def __init__(self, nc, ctx):
        self.nc = nc
        self.ctx = ctx
        self.eng = {"pe": nc.tensor, "act": nc.scalar, "dve": nc.vector,
                    "pool": nc.gpsimd, "sp": nc.sync}
        self._sems = {}
        self._count = {}
        self._waited = {e: {} for e in self.ENGINES}
        for e in self.ENGINES:
            self._mk("ev_" + e)

    def _mk(self, name):
        h = self.ctx.enter_context(self.nc.semaphore(name))
        self._sems[name] = h
        self._count[name] = 0
        return h

    def sem(self, name):
        if name not in self._sems:
            self._mk(name)
        return self._sems[name]

    def _wait(self, engine, deps):
        for dep in deps:
            if dep is None:
                continue
            sname, val = dep
            if self._waited[engine].get(sname, 0) >= val:
                continue
            self.eng[engine].wait_ge(self._sems[sname], val)
            self._waited[engine][sname] = val

    def op(self, engine, deps, fn, ev=False):
        self._wait(engine, [d for d in deps if d is not None])
        inst = fn(self.eng[engine])
        if ev:
            sname = "ev_" + engine
            self._count[sname] += 1
            inst.then_inc(self._sems[sname], 1)
            return (sname, self._count[sname])
        return None

    def dma(self, engine, deps, out, in_, sem_name, **kw):
        self._wait(engine, [d for d in deps if d is not None])
        s = self.sem(sem_name)
        inst = self.eng[engine].dma_start(out=out, in_=in_, **kw)
        self._count[sem_name] += 16
        inst.then_inc(s, 16)
        return (sem_name, self._count[sem_name])

    def idma(self, engine, deps, fn, sem_name):
        self._wait(engine, [d for d in deps if d is not None])
        sm = self.sem(sem_name)
        inst = fn(self.eng[engine])
        self._count[sem_name] += 16
        inst.then_inc(sm, 16)
        return (sem_name, self._count[sem_name])

    def last(self, engine):
        sname = "ev_" + engine
        return (sname, self._count[sname])

    def bar(self):
        return [(s, c) for s, c in self._count.items() if c > 0]


def ts(i, n):
    return slice(i * n, (i + 1) * n)


def ln_chain(b, stats, sc, e_sum, e_sq, n):
    e_m = b.op("dve", [e_sum], lambda e: e.tensor_scalar_mul(
        stats[:, sc + 2:sc + 3], stats[:, sc:sc + 1], 1.0 / n), ev=True)
    e_x2 = b.op("dve", [e_sq], lambda e: e.tensor_scalar_mul(
        stats[:, sc + 3:sc + 4], stats[:, sc + 1:sc + 2], 1.0 / n), ev=True)
    e_m2 = b.op("dve", [e_m], lambda e: e.tensor_mul(
        stats[:, sc + 4:sc + 5], stats[:, sc + 2:sc + 3],
        stats[:, sc + 2:sc + 3]), ev=True)
    e_v = b.op("dve", [e_x2, e_m2], lambda e: e.tensor_sub(
        stats[:, sc + 5:sc + 6], stats[:, sc + 3:sc + 4],
        stats[:, sc + 4:sc + 5]), ev=True)
    e_v = b.op("dve", [e_v], lambda e: e.tensor_scalar_add(
        stats[:, sc + 5:sc + 6], stats[:, sc + 5:sc + 6], 1e-5), ev=True)
    e_l = b.op("act", [e_v], lambda e: e.activation(
        stats[:, sc + 5:sc + 6], stats[:, sc + 5:sc + 6], AF.Ln), ev=True)
    e_r = b.op("act", [e_l], lambda e: e.activation(
        stats[:, sc + 6:sc + 7], stats[:, sc + 5:sc + 6], AF.Exp,
        scale=-0.5), ev=True)
    e_nm = b.op("dve", [e_m], lambda e: e.tensor_scalar_mul(
        stats[:, sc + 7:sc + 8], stats[:, sc + 2:sc + 3], -1.0), ev=True)
    return e_r, e_nm


MARKS = []


def build_program(NB, SEQ, GB, F):
    KT = SEQ // 128
    QG = SEQ // 512
    NT = SEQ // 128
    FT = F // 128
    NGR = NB // GB
    NTG = (SEQ * GB) // 512

    nc = bass.Bass("TRN2", target_bir_lowering=False)
    ap = lambda t: t.ap()
    x_in = ap(nc.dram_tensor("x", [NB, SEQ, D], F32, kind="ExternalInput"))
    wqkv = ap(nc.dram_tensor("wqkv", [D, 3 * D], F32, kind="ExternalInput"))
    wo = ap(nc.dram_tensor("wo", [D, D], F32, kind="ExternalInput"))
    gw = ap(nc.dram_tensor("gw", [D, E], F32, kind="ExternalInput"))
    w1 = ap(nc.dram_tensor("w1", [E, D, F], F32, kind="ExternalInput"))
    w2 = ap(nc.dram_tensor("w2", [E, F, D], F32, kind="ExternalInput"))
    lxi = ap(nc.dram_tensor("lxi", [1, H], F32, kind="ExternalInput"))
    c_id = ap(nc.dram_tensor("c_id", [128, 128], F32, kind="ExternalInput"))
    c_ones = ap(nc.dram_tensor("c_ones", [1, 128], F32, kind="ExternalInput"))
    c_sel = ap(nc.dram_tensor("c_sel", [128, 256], F32, kind="ExternalInput"))
    c_cos = ap(nc.dram_tensor("c_cos", [128, SEQ], F32, kind="ExternalInput"))
    c_sin = ap(nc.dram_tensor("c_sin", [128, SEQ], F32, kind="ExternalInput"))
    c_swp = ap(nc.dram_tensor("c_swp", [128, 128], F32, kind="ExternalInput"))
    c_mask = ap(nc.dram_tensor("c_mask", [128, 4 * 512], F32, kind="ExternalInput"))
    c_iota = ap(nc.dram_tensor("c_iota", [128, KT], F32, kind="ExternalInput"))
    c_ek = ap(nc.dram_tensor("c_ek", [1, H], F32, kind="ExternalInput"))
    c_triu = ap(nc.dram_tensor("c_triu", [128, 128], F32, kind="ExternalInput"))
    c_io8 = ap(nc.dram_tensor("c_io8", [128, 8], F32, kind="ExternalInput"))
    c_ids = ap(nc.dram_tensor("c_ids", [GB * SEQ, 120], BF16,
                              kind="ExternalInput"))
    c_pay = ap(nc.dram_tensor("c_pay", [128, (GB * SEQ) // 128], I32,
                              kind="ExternalInput"))
    out = ap(nc.dram_tensor("out", [NB * SEQ + 128, D], F32,
                             kind="ExternalOutput"))
    TOKG = GB * SEQ
    CAP = max((3 * TOKG) // 8 // 512, 1) * 512
    CS = CAP + 128
    RW = 1152
    spill_tok = ap(nc.dram_tensor("spill_tok", [GB * SEQ + 128, RW], BF16,
                                  kind="Internal"))
    slotmap = ap(nc.dram_tensor("slotmap", [E * CS, 1], I32, kind="Internal"))
    outflat = out

    ctx = contextlib.ExitStack()
    _uq = [0]

    def U(name):
        _uq[0] += 1
        return f"{name}_{_uq[0]}"
    with ctx:
        b = EB(nc, ctx)
        T = ctx.enter_context
        R = {}

        id_sb = T(nc.sbuf_tensor("id_sb", [128, 128], BF16))
        swp_sb = T(nc.sbuf_tensor("swp_sb", [128, 128], BF16))
        ones_sb = T(nc.sbuf_tensor("ones_sb", [1, 128], F32))
        sel_sb = T(nc.sbuf_tensor("sel_sb", [128, 256], BF16))
        iota_sb = T(nc.sbuf_tensor("iota_sb", [128, KT], F32))
        ek_sb = T(nc.sbuf_tensor("ek_sb", [1, H], F32))
        lxi_sb = T(nc.sbuf_tensor("lxi_sb", [1, H], F32))
        rate_sb = T(nc.sbuf_tensor("rate_sb", [1, H], F32))
        rateB = T(nc.sbuf_tensor("rateB", [128, H], F32))
        gate_sb = T(nc.sbuf_tensor("gate_sb", [128, NCH * E], BF16))
        stats = T(nc.sbuf_tensor("stats", [128, 32], F32))
        gwrk = T(nc.sbuf_tensor("gwrk", [128, 48], F32))
        rcp_sb = T(nc.sbuf_tensor("rcp_sb", [1, 512], F32))
        avbf = T(nc.sbuf_tensor("avbf", [128, 512], BF16))
        onesb = T(nc.sbuf_tensor("onesb", [1, 128], BF16))
        rcpb_sb = T(nc.sbuf_tensor("rcpb_sb", [1, 512], BF16))
        den_sb = T(nc.sbuf_tensor("den_sb", [1, 512], BF16))
        rcpt_sb = T(nc.sbuf_tensor("rcpt_sb", [128, 4], F32))
        rcptb_sb = T(nc.sbuf_tensor("rcptb_sb", [128, 4], BF16))
        triu_sb = T(nc.sbuf_tensor("triu_sb", [128, 128], BF16))
        io8_sb = T(nc.sbuf_tensor("io8_sb", [128, 8], F32))
        idf_sb = T(nc.sbuf_tensor("idf_sb", [8, 8], F32))
        oor_sb = T(nc.sbuf_tensor("oor_sb", [128, RW], BF16))
        onescb = T(nc.sbuf_tensor("onescb", [128, 1], BF16))

        e_ob = b.op("dve", [], lambda e: e.memset(onesb[:, :], 1.0), ev=True)
        e_ocb = b.op("dve", [], lambda e: e.memset(onescb[:, :], 1.0), ev=True)
        e_oor = b.op("dve", [], lambda e: e.memset(oor_sb[:, :], 0.0), ev=True)
        pay_sb = T(nc.sbuf_tensor("pay_sb", [128, TOKG // 128], I32))
        czi_sb = T(nc.sbuf_tensor("czi_sb", [128, (E * CS) // 128], I32))
        e_czi = b.op("dve", [], lambda e: e.memset(czi_sb[:, :], TOKG),
                     ev=True)
        zsp = b.dma("sp", [e_oor], spill_tok[TOKG:TOKG + 128, :],
                    oor_sb[:, :], "zsp")
        ld_c = None
        for dst, src in ((ones_sb, c_ones), (iota_sb, c_iota),
                         (ek_sb, c_ek), (lxi_sb, lxi)):
            ld_c = b.dma("sp", [], dst[:, :], src[:, :], "ld_const")
        ld_cc = None
        for dst, src in ((id_sb, c_id), (sel_sb, c_sel), (swp_sb, c_swp),
                         (triu_sb, c_triu)):
            ld_cc = b.dma("pool", [], dst[:, :], src[:, :], "ld_constc")
        ld_c = b.dma("sp", [], io8_sb[:, :], c_io8[:, :], "ld_const")
        ld_c = b.dma("sp", [], idf_sb[:, :], c_id[0:8, 0:8], "ld_const")
        ld_c = b.dma("sp", [], pay_sb[:, :], c_pay[:, :], "ld_const")
        ld_ids = b.dma("sp", [],
                       spill_tok[0:TOKG, 1032:1152].rearrange(
                           "(t p) c -> p t c", p=128),
                       c_ids.rearrange("(t p) c -> p t c", p=128), "ld_ids")
        ld_gate = b.dma("pool", [],
                        gate_sb[:].rearrange("p (c g) -> p c g", c=NCH),
                        gw.rearrange("(c p) g -> p c g", p=128), "ld_gate")

        e0 = b.op("act", [ld_c], lambda e: e.activation(
            rate_sb[:, :], lxi_sb[:, :], AF.Exp, scale=-1.0), ev=True)
        e0 = b.op("dve", [e0, ld_c], lambda e: e.tensor_mul(
            rate_sb[:, :], rate_sb[:, :], ek_sb[:, :]), ev=True)
        with nc.psum_tensor("ps_init", [128, 512], F32) as ps_init:
            e0 = b.op("pe", [e0, ld_c], lambda e: e.matmul(
                ps_init[:, 0:H], ones_sb[0:1, :], rate_sb[0:1, :],
                start=True, stop=True), ev=True)
            e0 = b.op("dve", [e0], lambda e: e.tensor_copy(
                rateB[:, :], ps_init[:, 0:H]), ev=True)
        e_rateB = e0

        for g in range(NGR):
            gsc = contextlib.ExitStack()
            with gsc:
                Tg = gsc.enter_context
                comb_sb = Tg(nc.sbuf_tensor(U("combb"), [128, GB * NT * E], F32))
                selg = Tg(nc.sbuf_tensor(U("selg"), [128, GB * NT * E], BF16))
                idxg = Tg(nc.sbuf_tensor(U("idxg"), [128, GB * NT * 2], F32))
                gasc = contextlib.ExitStack()
                Ta = gasc.enter_context
                cos_sb = Ta(nc.sbuf_tensor(U("cosb"), [128, SEQ], BF16))
                sin_sb = Ta(nc.sbuf_tensor(U("sinb"), [128, SEQ], BF16))
                mask_sb = Ta(nc.sbuf_tensor(U("maskb"), [128, 4 * 512], BF16))
                qoff = Ta(nc.sbuf_tensor(U("qoff"), [128, H * QG], F32))
                biasHG = Ta(nc.sbuf_tensor(U("biasHG"), [128, H * QG * KT],
                                           F32))
                ld_cs = b.dma("pool", [], cos_sb[:, :], c_cos[:, :], "ld_cs")
                ld_cs = b.dma("pool", [], sin_sb[:, :], c_sin[:, :], "ld_cs")
                ld_cs = b.dma("pool", [], mask_sb[:, :], c_mask[:, :], "ld_cs")
                e_bias = e_rateB
                for h in range(H):
                    for qg_ in range(QG):
                        col = h * QG + qg_
                        e_q = b.op("dve", [e_rateB, R.get("biaswar")],
                                   lambda e, h=h, qg_=qg_, col=col:
                                   e.tensor_scalar_mul(
                            qoff[:, col:col + 1], rateB[:, h:h + 1],
                            float(512 * qg_ + 511)), ev=True)
                        e_bias = b.op("dve", [e_q], lambda e, h=h, col=col:
                                      e.tensor_scalar(
                            biasHG[:, ts(col, KT)], iota_sb[:, :],
                            rateB[:, h:h + 1], qoff[:, col:col + 1],
                            ALU.mult, ALU.subtract), ev=True)
                comb_last = None
                x2w_ev = None
                x2w_ev2 = None

                for bb in range(GB):
                    bidx = g * GB + bb
                    bsc = contextlib.ExitStack()
                    with bsc:
                        Tb = bsc.enter_context
                        fmj = Tb(nc.sbuf_tensor(U("fmj"), [128, NCH * SEQ], BF16))
                        q_sb = Tb(nc.sbuf_tensor(U("q_sb"), [128, NCH * SEQ], BF16))
                        k_sb = Tb(nc.sbuf_tensor(U("k_sb"), [128, NCH * SEQ], BF16))
                        v_sb = Tb(nc.sbuf_tensor(U("v_sb"), [128, NT * H * 65], BF16))

                        MARKS.append((f"A1 g{g}b{bb}", b._count["ev_pe"]))
                        a1 = contextlib.ExitStack()
                        with a1:
                            T1 = a1.enter_context
                            xrow = [T1(nc.sbuf_tensor(U(f"xr{i}"), [128, D], F32))
                                    for i in range(4)]
                            xln = [T1(nc.sbuf_tensor(U(f"xl{i}"), [128, D], BF16))
                                   for i in range(2)]
                            sq = T1(nc.sbuf_tensor(U("sq"), [128, D], BF16))
                            wsec = T1(nc.sbuf_tensor(U("wsec"), [128, NCH * D], BF16))
                            ps_t = [T1(nc.psum_tensor(U(f"pst{i}"), [128, 512], BF16))
                                    for i in range(2)]
                            qk_ps = [T1(nc.psum_tensor(U(f"qkp{i}"), [128, 512], F32))
                                     for i in range(2)]
                            v_ps = [T1(nc.psum_tensor(U(f"vps{i}"), [128, 1024], F32))
                                    for i in range(2)]

                            a1bar = b.bar()
                            e_vone = b.op("dve", [R.get("av_done")] + a1bar,
                                          lambda e: e.memset(
                                v_sb[:].rearrange("p (t c) -> p t c", c=65)
                                [:, :, 64:65], 1.0), ev=True)

                            prev_spill = R.get(("spill_war", bb))
                            ev_sq = R.get("sq_war")
                            for tt in range(NT):
                                xs = xrow[tt % 4]
                                lx = b.dma("sp",
                                           [R.get(("xr", tt % 4))] + a1bar,
                                           xs[:, :], x_in[bidx, ts(tt, 128), :],
                                           f"ldx{tt % 4}")
                                sc = (tt % 2) * 16
                                lnwar = R.get(("ln", sc))
                                e_sum = b.op("dve", [lx, lnwar],
                                             lambda e, xs=xs, sc=sc:
                                             e.reduce_sum(stats[:, sc:sc + 1],
                                                          xs[:, :], axis=AX.X),
                                             ev=True)
                                ev_sq = b.op("act", [lx, ev_sq, lnwar],
                                             lambda e, xs=xs, sc=sc:
                                             e.activation(
                                    sq[:, :], xs[:, :], AF.Square,
                                    accum_out=stats[:, sc + 1:sc + 2]), ev=True)
                                e_r, e_nm = ln_chain(b, stats, sc, e_sum, ev_sq, D)
                                xl = xln[tt % 2]
                                e_n = b.op("dve", [e_r, e_nm, R.get(("xl", tt % 2))],
                                           lambda e, xs=xs, xl=xl, sc=sc:
                                           e.tensor_scalar(
                                    xl[:, :], xs[:, :], stats[:, sc + 7:sc + 8],
                                    stats[:, sc + 6:sc + 7], ALU.add, ALU.mult),
                                    ev=True)
                                R[("xr", tt % 4)] = e_n
                                R[("ln", sc)] = e_n
                                for c2 in range(2):
                                    dep = [e_n, R.get(("pst", c2)), ld_cc,
                                           prev_spill]
                                    e_tp = None
                                    for cc in range(4):
                                        c = c2 * 4 + cc
                                        e_tp = b.op("pe", dep if cc == 0 else [],
                                                    lambda e, xl=xl, c=c, cc=cc,
                                                    c2=c2: e.transpose(
                                            ps_t[c2][:, ts(cc, 128)],
                                            xl[:, ts(c, 128)], id_sb[:, :]),
                                            ev=(cc == 3))
                                    e_ev = b.op("dve", [e_tp],
                                                lambda e, tt=tt, c2=c2:
                                                e.tensor_copy(
                                        fmj[:].rearrange("p (c t) -> p c t",
                                                         c=NCH)
                                        [:, slice(c2 * 4, c2 * 4 + 4),
                                         ts(tt, 128)],
                                        ps_t[c2][:, :].rearrange(
                                            "p (c t) -> p c t", c=4)), ev=True)
                                    R[("pst", c2)] = e_ev
                                R[("xl", tt % 2)] = b.last("pe")
                            R["sq_war"] = ev_sq
                            a1_fmj = b.last("dve")

                            ev_wsec_user = R.get("wsec_war")
                            MARKS.append((f"QKV g{g}b{bb}", b._count["ev_pe"]))
                            for sec, colbase in (("v", 2 * D), ("q", 0),
                                                 ("k", D)):
                                lw = b.dma("pool", [ev_wsec_user] + a1bar,
                                           wsec[:].rearrange(
                                               "p (c f) -> p c f", c=NCH),
                                           wqkv[:, colbase:colbase + D]
                                           .rearrange("(c p) f -> p c f", p=128),
                                           "ld_wsec")
                                if sec == "v":
                                    for tt in range(NT):
                                        vp = v_ps[tt % 2]
                                        dep0 = [lw, a1_fmj,
                                                R.get(("vps", tt % 2))]
                                        last_mm = None
                                        for vg in range(2):
                                            for c in range(NCH):
                                                last_mm = b.op(
                                                    "pe",
                                                    dep0 if (vg == 0 and c == 0)
                                                    else [],
                                                    lambda e, vp=vp, vg=vg, c=c,
                                                    tt=tt: e.matmul(
                                                        vp[:, ts(vg, 512)],
                                                        fmj[:].rearrange(
                                                            "p (c t) -> p c t",
                                                            c=NCH)
                                                        [:, c, ts(tt, 128)],
                                                        wsec[:, c * D + vg * 512:
                                                             c * D + vg * 512
                                                             + 512],
                                                        start=(c == 0),
                                                        stop=(c == NCH - 1)),
                                                    ev=(vg == 1 and c == NCH - 1))
                                        e_ve = b.op("dve", [last_mm, e_vone],
                                                    lambda e, vp=vp, tt=tt:
                                                    e.tensor_copy(
                                            v_sb[:, tt * H * 65:
                                                 (tt + 1) * H * 65]
                                            .rearrange("p (h c) -> p h c", h=H)
                                            [:, :, 0:64],
                                            vp[:, :].rearrange(
                                                "p (h c) -> p h c", c=64)),
                                            ev=True)
                                        R[("vps", tt % 2)] = e_ve
                                    ev_wsec_user = last_mm
                                else:
                                    dst = q_sb if sec == "q" else k_sb
                                    for ft in range(NCH):
                                        for tg in range(QG):
                                            i2 = (ft * QG + tg) % 2
                                            qp = qk_ps[i2]
                                            dep0 = [lw, a1_fmj,
                                                    R.get(("qkp", i2))]
                                            last_mm = None
                                            for c in range(NCH):
                                                last_mm = b.op(
                                                    "pe", dep0 if c == 0 else [],
                                                    lambda e, qp=qp, ft=ft,
                                                    tg=tg, c=c: e.matmul(
                                                        qp[:, :],
                                                        wsec[:, c * D + ft * 128:
                                                             c * D + ft * 128
                                                             + 128],
                                                        fmj[:].rearrange(
                                                            "p (c t) -> p c t",
                                                            c=NCH)
                                                        [:, c, ts(tg, 512)],
                                                        start=(c == 0),
                                                        stop=(c == NCH - 1)),
                                                    ev=(c == NCH - 1))
                                            e_qe = b.op("dve", [last_mm],
                                                        lambda e, qp=qp,
                                                        dst=dst, ft=ft, tg=tg:
                                                        e.tensor_copy(
                                                dst[:, ft * SEQ + tg * 512:
                                                    ft * SEQ + tg * 512 + 512],
                                                qp[:, :]), ev=True)
                                            R[("qkp", i2)] = e_qe
                                    ev_wsec_user = last_mm
                            R["wsec_war"] = ev_wsec_user
                            a1_done = b.last("dve")

                        MARKS.append((f"A2 g{g}b{bb}", b._count["ev_pe"]))
                        a2 = contextlib.ExitStack()
                        with a2:
                            T2 = a2.enter_context
                            probs = T2(nc.sbuf_tensor(U("probs"), [128, 8 * 512],
                                                      BF16))
                            kmask = T2(nc.sbuf_tensor(U("kmask"), [128, 2 * SEQ],
                                                      BF16))
                            rs = T2(nc.sbuf_tensor(U("rs"), [128, 2 * SEQ], BF16))
                            s_ps = [T2(nc.psum_tensor(U(f"sps{i}"), [128, 1024],
                                                      F32)) for i in range(2)]
                            av_ps = [T2(nc.psum_tensor(U(f"avp{i}"), [128, 512],
                                                       F32)) for i in range(2)]
                            bc_ps = [T2(nc.psum_tensor(U(f"bcp{i}"), [128, 512],
                                                       F32)) for i in range(2)]
                            km_ps = [sp[:, 0:128] for sp in s_ps]

                            ev_sps = {}
                            ev_rs = None
                            NSL = SEQ // 512
                            for ch in ROPE_CHUNKS:
                                for srcb in (q_sb, k_sb):
                                    chunk = srcb[:, ts(ch, SEQ)]
                                    swp = rs[:, 0:SEQ]
                                    m1 = rs[:, SEQ:2 * SEQ]
                                    e_cps = []
                                    last_smm = None
                                    for sl in range(NSL):
                                        sp = s_ps[sl % 2]
                                        last_smm = b.op(
                                            "pe",
                                            [a1_done, ld_cc, ld_cs,
                                             ev_sps.get(sl % 2)],
                                            lambda e, sp=sp, ch=ch, sl=sl,
                                            srcb=srcb: e.matmul(
                                                sp[:, 0:512], swp_sb[:, :],
                                                srcb[:, ch * SEQ + sl * 512:
                                                     ch * SEQ + sl * 512 + 512],
                                                start=True, stop=True), ev=True)
                                        e_c = b.op("dve", [last_smm, ev_rs],
                                                   lambda e, sp=sp, sl=sl:
                                                   e.tensor_copy(
                                            rs[:, ts(sl, 512)], sp[:, 0:512]),
                                            ev=True)
                                        ev_sps[sl % 2] = e_c
                                        e_cps.append(e_c)
                                    e_m1 = b.op("dve", e_cps + [ld_cs],
                                                lambda e, chunk=chunk, m1=m1:
                                                e.tensor_mul(
                                        m1, chunk, cos_sb[:, :]), ev=True)
                                    e_sw = b.op("dve", [e_m1],
                                                lambda e, swp=swp:
                                                e.tensor_mul(
                                        swp, swp, sin_sb[:, :]), ev=True)
                                    ev_rs = b.op("dve", [e_sw, last_smm],
                                                 lambda e, chunk=chunk, m1=m1,
                                                 swp=swp: e.tensor_add(
                                        chunk, m1, swp), ev=True)
                            rope_done = b.last("dve")

                            ev_probs = {}
                            pend_bc = [None]

                            def flush_bc():
                                if pend_bc[0] is None:
                                    return
                                (e_rc0, bcp0, i20, rb0, ch0, qg0, h0) = pend_bc[0]
                                pend_bc[0] = None
                                e_dt = None
                                for j in range(4):
                                    e_dt = b.op("pe", [e_rc0, e_ob,
                                                       R.get(("bcp", i20))]
                                                if j == 0 else [],
                                                lambda e, bcp0=bcp0, j=j:
                                                e.matmul(
                                        bcp0[:, j:j + 1],
                                        den_sb[0:1, ts(j, 128)],
                                        onesb[0:1, 0:1],
                                        start=True, stop=True),
                                        ev=(j == 3))
                                R["rcp"] = e_dt
                                e_rt = b.op("dve", [e_dt, R.get("rcpt")],
                                            lambda e, bcp0=bcp0:
                                            e.reciprocal(
                                    rcpt_sb[:, :], bcp0[:, 0:4]), ev=True)
                                e_rt = b.op("dve", [e_rt],
                                            lambda e: e.tensor_copy(
                                    rcptb_sb[:, :], rcpt_sb[:, :]), ev=True)
                                e_rr = None
                                for j in range(4):
                                    e_rr = b.op("pe", [e_rt] if j == 0 else [],
                                                lambda e, bcp0=bcp0, j=j:
                                                e.matmul(
                                        bcp0[0:1, ts(j, 128)],
                                        rcptb_sb[:, j:j + 1], id_sb[:, :],
                                        start=True, stop=True),
                                        ev=(j == 3))
                                R["rcpt"] = e_rr
                                e_cb = b.op("dve", [e_rr, R.get("rcpb")],
                                            lambda e, bcp0=bcp0:
                                            e.tensor_copy(
                                    rcpb_sb[0:1, :], bcp0[0:1, :]), ev=True)
                                e_bc = b.op("pe", [e_cb],
                                            lambda e, bcp0=bcp0: e.matmul(
                                    bcp0[0:64, :], onesb[0:1, 0:64],
                                    rcpb_sb[0:1, :], start=True, stop=True),
                                    ev=True)
                                R["rcpb"] = e_bc
                                e_cpa = b.op("dve", [R.get(("avpe", i20)),
                                                     R.get("avbf")],
                                             lambda e, i20=i20:
                                             e.tensor_copy(
                                    avbf[0:64, :], av_ps[i20][0:64, :]),
                                    ev=True)
                                R[("avp", i20)] = e_cpa
                                e_fin = b.op("dve", [e_cpa, e_bc],
                                             lambda e, bcp0=bcp0, rb0=rb0,
                                             ch0=ch0, qg0=qg0: e.tensor_mul(
                                    fmj[rb0:rb0 + 64,
                                        ch0 * SEQ + qg0 * 512:
                                        ch0 * SEQ + qg0 * 512 + 512],
                                    avbf[0:64, :], bcp0[0:64, :]), ev=True)
                                R["avbf"] = e_fin
                                R[("bcp", i20)] = e_fin
                                R[("attnT", h0, qg0)] = e_fin

                            for h in range(H):
                                ch = h // 2
                                rope = PI_BITS[h] > 0
                                decay = E_BITS[h] > 0
                                rb = (h % 2) * 64
                                kb_ev = [None] * KT
                                if rope:
                                    selap = sel_sb[:, ts(h % 2, 128)]
                                    for t in range(KT):
                                        kp = km_ps[t % 2]
                                        e_mm = b.op("pe",
                                                    [rope_done, ld_cc,
                                                     ev_sps.get(t % 2),
                                                     R.get(("kmp", t % 2))],
                                                    lambda e, kp=kp,
                                                    selap=selap, ch=ch, t=t:
                                                    e.matmul(
                                            kp, selap,
                                            k_sb[:, ch * SEQ + t * 128:
                                                 ch * SEQ + t * 128 + 128],
                                            start=True, stop=True), ev=True)
                                        e_cp = b.op("dve", [e_mm],
                                                    lambda e, kp=kp, h=h, t=t:
                                                    e.tensor_copy(
                                            kmask[:, (h % 2) * SEQ + t * 128:
                                                  (h % 2) * SEQ + t * 128 + 128],
                                            kp), ev=True)
                                        R[("kmp", t % 2)] = e_cp
                                        ev_sps[t % 2] = e_cp
                                        kb_ev[t] = e_cp

                                for qg in range(QG):
                                    nkt = 4 * (qg + 1)
                                    npr = nkt // 2
                                    i2 = (h * QG + qg) % 2
                                    avp = av_ps[i2]
                                    pend_av = []

                                    def emit_av_pair(pi, avdep, avp=None,
                                                     h=None, nkt=None, i2=None):
                                        e_av = None
                                        t0 = 2 * pi
                                        for tt_ in (t0, t0 + 1):
                                            dd = [avdep] if tt_ == t0 else []
                                            if tt_ == 0:
                                                dd += [R.get(("avp", i2)),
                                                       R.get(("avpe", i2))]
                                            e_av = b.op(
                                                "pe", dd,
                                                lambda e, tt_=tt_, avp=avp,
                                                h=h, nkt=nkt: e.matmul(
                                                    avp[0:65, :],
                                                    v_sb[:, tt_ * H * 65
                                                         + h * 65:
                                                         tt_ * H * 65
                                                         + h * 65 + 65],
                                                    probs[:, ts(tt_ % 8, 512)],
                                                    start=(tt_ == 0),
                                                    stop=(tt_ == nkt - 1)),
                                                ev=True)
                                            ev_probs[tt_ % 8] = e_av
                                        return e_av

                                    for pr in range(npr):
                                        t0 = 2 * pr
                                        sp = s_ps[pr % 2]
                                        e_s = None
                                        for ih in range(2):
                                            t = t0 + ih
                                            if rope:
                                                lhs = kmask[:, (h % 2) * SEQ
                                                            + t * 128:
                                                            (h % 2) * SEQ
                                                            + t * 128 + 128]
                                                rhs = q_sb[:, ch * SEQ
                                                           + qg * 512:
                                                           ch * SEQ
                                                           + qg * 512 + 512]
                                            else:
                                                lhs = k_sb[rb:rb + 64,
                                                           ch * SEQ + t * 128:
                                                           ch * SEQ + t * 128
                                                           + 128]
                                                rhs = q_sb[rb:rb + 64,
                                                           ch * SEQ + qg * 512:
                                                           ch * SEQ + qg * 512
                                                           + 512]
                                            e_s = b.op(
                                                "pe",
                                                [kb_ev[t] if rope
                                                 else rope_done,
                                                 a1_done,
                                                 ev_sps.get(pr % 2)
                                                 if ih == 0 else None],
                                                lambda e, sp=sp, lhs=lhs,
                                                rhs=rhs, ih=ih: e.matmul(
                                                    sp[:, ts(ih, 512)],
                                                    lhs, rhs,
                                                    start=True, stop=True),
                                                ev=True)
                                            if pr == 0 and ih == 0:
                                                flush_bc()
                                        slot = t0 % 8
                                        pdep = [e_s, ev_probs.get(slot),
                                                ev_probs.get(slot + 1),
                                                e_bias, a1_done]
                                        if decay:
                                            e_x = None
                                            for ih in range(2):
                                                t = t0 + ih
                                                bcol = (h * QG + qg) * KT + t
                                                e_x = b.op(
                                                    "act", pdep,
                                                    lambda e, sp=sp, ih=ih,
                                                    slot=slot, bcol=bcol:
                                                    e.activation(
                                                        probs[:, ts(slot + ih,
                                                                    512)],
                                                        sp[:, ts(ih, 512)],
                                                        AF.Exp, scale=0.125,
                                                        bias=biasHG[:, bcol:
                                                                    bcol + 1]),
                                                    ev=True)
                                        else:
                                            e_x = b.op(
                                                "act", pdep,
                                                lambda e, sp=sp, slot=slot:
                                                e.activation(
                                                    probs[:, slot * 512:
                                                          slot * 512 + 1024],
                                                    sp[:, :], AF.Exp,
                                                    scale=0.125), ev=True)
                                        ev_sps[pr % 2] = e_x
                                        for ih in range(2):
                                            t = t0 + ih
                                            if t >= 4 * qg:
                                                m = t - 4 * qg
                                                e_x = b.op(
                                                    "dve", [e_x, ld_cs],
                                                    lambda e, slot=slot,
                                                    ih=ih, m=m: e.tensor_mul(
                                                        probs[:, ts(slot + ih,
                                                                    512)],
                                                        probs[:, ts(slot + ih,
                                                                    512)],
                                                        mask_sb[:, ts(m, 512)]),
                                                    ev=True)
                                        pend_av.append((pr, e_x))
                                        if len(pend_av) > 2:
                                            pi, dep_ = pend_av.pop(0)
                                            emit_av_pair(pi, dep_, avp=avp,
                                                         h=h, nkt=nkt, i2=i2)
                                    e_av = None
                                    for pi, dep_ in pend_av:
                                        e_av = emit_av_pair(pi, dep_, avp=avp,
                                                            h=h, nkt=nkt,
                                                            i2=i2)
                                    R[("avpe", i2)] = e_av
                                    e_rc = b.op("dve", [e_av, R.get("rcp")],
                                                lambda e, avp=avp:
                                                e.tensor_copy(
                                        den_sb[0:1, :], avp[64:65, :]),
                                        ev=True)
                                    flush_bc()
                                    pend_bc[0] = (e_rc, bc_ps[i2], i2, rb,
                                                  ch, qg, h)
                            flush_bc()
                            R["av_done"] = b.last("pe")

                        MARKS.append((f"A3 g{g}b{bb}", b._count["ev_pe"]))
                        a3 = contextlib.ExitStack()
                        with a3:
                            T3 = a3.enter_context
                            wo_sb = T3(nc.sbuf_tensor(U("wo_sb"), [128, NCH * D],
                                                      BF16))
                            xrow3 = [T3(nc.sbuf_tensor(U(f"x3r{i}"), [128, D],
                                                       F32)) for i in range(2)]
                            x2r = [T3(nc.sbuf_tensor(U(f"x2r{i}"), [128, D], F32))
                                   for i in range(2)]
                            xln3 = [T3(nc.sbuf_tensor(U(f"x3l{i}"), [128, D],
                                                      BF16)) for i in range(2)]
                            sq3 = T3(nc.sbuf_tensor(U("sq3"), [128, D], BF16))
                            ps_t3 = [T3(nc.psum_tensor(U(f"pt3{i}"), [128, 512],
                                                       BF16)) for i in range(2)]
                            combbf = T3(nc.sbuf_tensor(U("combbf"), [128, 16],
                                                       BF16))
                            wo_ps = [T3(nc.psum_tensor(U(f"wop{i}"), [128, 1024],
                                                       F32)) for i in range(2)]
                            g_ps = T3(nc.psum_tensor(U("g_ps"), [128, 512], F32))

                            a3bar = b.bar()
                            lwo = b.dma("pool", [R.get("wo_war")] + a3bar,
                                        wo_sb[:].rearrange("p (c f) -> p c f",
                                                           c=NCH),
                                        wo.rearrange("(c p) f -> p c f", p=128),
                                        "ld_wo")
                            ev_sq3 = R.get("sq_war")
                            ev_gate = []
                            lag = [None]

                            def emit_post(state):
                                ttp, xlp, e_np, last_mmp = state
                                e_tps = []
                                for c2 in range(2):
                                    dep = [e_np, R.get(("pt3", c2)),
                                           last_mmp, ld_cc]
                                    e_tp = None
                                    for cc in range(4):
                                        c = c2 * 4 + cc
                                        e_tp = b.op("pe",
                                                    dep if cc == 0 else [],
                                                    lambda e, xlp=xlp, c=c,
                                                    cc=cc, c2=c2: e.transpose(
                                            ps_t3[c2][:, ts(cc, 128)],
                                            xlp[:, ts(c, 128)], id_sb[:, :]),
                                            ev=(cc == 3))
                                    e_ev = b.op("dve", [e_tp],
                                                lambda e, ttp=ttp, c2=c2:
                                                e.tensor_copy(
                                        fmj[:].rearrange("p (c t) -> p c t",
                                                         c=NCH)
                                        [:, slice(c2 * 4, c2 * 4 + 4),
                                         ts(ttp, 128)],
                                        ps_t3[c2][:, :].rearrange(
                                            "p (c t) -> p c t", c=4)), ev=True)
                                    R[("pt3", c2)] = e_ev
                                    e_tps.append(e_ev)
                                R[("x3l", ttp % 2)] = b.last("pe")
                                last_g = None
                                for c in range(NCH):
                                    last_g = b.op(
                                        "pe",
                                        (e_tps + [ld_gate] + ev_gate[0])
                                        if c == 0 else [],
                                        lambda e, c=c, ttp=ttp: e.matmul(
                                            g_ps[:, 0:E],
                                            fmj[:].rearrange(
                                                "p (c t) -> p c t", c=NCH)
                                            [:, c, ts(ttp, 128)],
                                            gate_sb[:, ts(c, E)],
                                            start=(c == 0),
                                            stop=(c == NCH - 1)),
                                        ev=(c == NCH - 1))
                                gti = bb * NT + ttp
                                evg, cl, _n7 = emit_top2(
                                    b, g_ps, gwrk, comb_sb,
                                    gti * E, last_g,
                                    R.get("gwrk_war"), selg=selg, idxg=idxg,
                                    io8=io8_sb, gti=gti)
                                ev_gate[0] = evg
                                R["gwrk_war"] = cl
                                cslot = (ttp % 2) * 8
                                e_cbf = b.op("dve",
                                             [cl, R.get(("cbf", ttp % 2))],
                                             lambda e, gti=gti, cslot=cslot:
                                             e.tensor_copy(
                                    combbf[:, cslot:cslot + 8],
                                    comb_sb[:, ts(gti, E)]), ev=True)
                                cw = b.dma("sp", [e_cbf],
                                           spill_tok[ts(gti, 128), 1024:1032],
                                           combbf[:, cslot:cslot + 8],
                                           f"csp{ttp % 2}")
                                R[("cbf", ttp % 2)] = cw
                                R[("rspl", gti)] = cw
                                return cl

                            ev_gate = [[]]
                            ev_gate = ev_gate
                            ev_gate[0] = []
                            for tt in range(NT):
                                qg = min(tt // 4, QG - 1)
                                wop = wo_ps[tt % 2]
                                xs = xrow3[tt % 2]
                                lx = b.dma("sp",
                                           [R.get(("x3r", tt % 2))] + a3bar,
                                           xs[:, :], x_in[bidx, ts(tt, 128), :],
                                           f"ldx3{tt % 2}")
                                dep0 = ([lwo, R.get(("wop", tt % 2))]
                                        + [R[("attnT", h, qg)]
                                           for h in range(H)])
                                last_mm = None
                                for dg in range(2):
                                    for c in range(NCH):
                                        last_mm = b.op(
                                            "pe",
                                            dep0 if (dg == 0 and c == 0)
                                            else [],
                                            lambda e, wop=wop, dg=dg, c=c,
                                            tt=tt: e.matmul(
                                                wop[:, ts(dg, 512)],
                                                fmj[:].rearrange(
                                                    "p (c t) -> p c t", c=NCH)
                                                [:, c, ts(tt, 128)],
                                                wo_sb[:, c * D + dg * 512:
                                                      c * D + dg * 512 + 512],
                                                start=(c == 0),
                                                stop=(c == NCH - 1)),
                                            ev=(dg == 1 and c == NCH - 1))
                                x2 = x2r[tt % 2]
                                e_x2 = b.op("dve",
                                            [last_mm, lx,
                                             R.get(("x2r", tt % 2)),
                                             R.get(("x2w", tt % 2))],
                                            lambda e, x2=x2, xs=xs, wop=wop:
                                            e.tensor_add(x2[:, :], xs[:, :],
                                                         wop[:, :]), ev=True)
                                R[("x3r", tt % 2)] = e_x2
                                R[("wop", tt % 2)] = e_x2
                                x2w = b.dma("sp", [e_x2],
                                            out[bidx * SEQ + tt * 128:
                                                bidx * SEQ + (tt + 1) * 128,
                                                :],
                                            x2[:, :], f"st_out{tt % 2}")
                                R[("x2w", tt % 2)] = x2w
                                if tt % 2 == 0:
                                    x2w_ev = x2w
                                else:
                                    x2w_ev2 = x2w
                                sc = (tt % 2) * 16
                                lnwar = R.get(("ln", sc))
                                e_sum = b.op("dve", [e_x2, lnwar],
                                             lambda e, x2=x2, sc=sc:
                                             e.reduce_sum(
                                    stats[:, sc:sc + 1], x2[:, :], axis=AX.X),
                                    ev=True)
                                ev_sq3 = b.op("act", [e_x2, ev_sq3, lnwar],
                                              lambda e, x2=x2, sc=sc:
                                              e.activation(
                                    sq3[:, :], x2[:, :], AF.Square,
                                    accum_out=stats[:, sc + 1:sc + 2]), ev=True)
                                e_r, e_nm = ln_chain(b, stats, sc, e_sum,
                                                     ev_sq3, D)
                                xl = xln3[tt % 2]
                                e_n = b.op("dve",
                                           [e_r, e_nm, R.get(("x3l", tt % 2)),
                                            R.get(("xsp", tt % 2))],
                                           lambda e, x2=x2, xl=xl, sc=sc:
                                           e.tensor_scalar(
                                    xl[:, :], x2[:, :],
                                    stats[:, sc + 7:sc + 8],
                                    stats[:, sc + 6:sc + 7],
                                    ALU.add, ALU.mult), ev=True)
                                R[("x2r", tt % 2)] = e_n
                                R[("ln", sc)] = e_n
                                gti0 = bb * NT + tt
                                xsp = b.dma("sp", [e_n],
                                            spill_tok[ts(gti0, 128), 0:1024],
                                            xl[:, :], f"xsp{tt % 2}")
                                R[("xsp", tt % 2)] = xsp
                                R[("rspl2", gti0)] = xsp
                                if lag[0] is not None:
                                    comb_last = emit_post(lag[0])
                                lag[0] = (tt, xl, e_n, last_mm)
                            comb_last = emit_post(lag[0])
                            lag[0] = None
                            R["sq_war"] = ev_sq3
                            R["wo_war"] = b.last("pe")

                MARKS.append((f"B g{g}", b._count["ev_pe"]))
                R["biaswar"] = b.last("act")
                gasc.close()
                GBNT = GB * NT
                NQ4 = GBNT // 4
                rsc = contextlib.ExitStack()
                with rsc:
                    TR = rsc.enter_context
                    cnt_sb = TR(nc.sbuf_tensor(U("cntb"), [8, GBNT], F32))
                    incl_sb = TR(nc.sbuf_tensor(U("inclb"), [8, GBNT], F32))
                    zz_sb = TR(nc.sbuf_tensor(U("zzb"), [8, GBNT], F32))
                    erow_sb = TR(nc.sbuf_tensor(U("erowb"), [1, 8], F32))
                    scr8 = TR(nc.sbuf_tensor(U("scr8b"), [128, 8], F32))
                    psl = TR(nc.sbuf_tensor(U("pslb"), [128, 2], F32))
                    dstf = TR(nc.sbuf_tensor(U("dstfb"), [128, 8 * NQ4], F32))
                    dsti32 = TR(nc.sbuf_tensor(U("dsti32"), [128, 8 * NQ4],
                                               I32))
                    ct_ps = TR(nc.psum_tensor(U("ctps"), [128, 512], F32))
                    po_ps = TR(nc.psum_tensor(U("pops"), [128, 512], F32))

                    rbar = b.bar()
                    b.dma("sp", [e_czi] + (R.get("sm_war") or []) + rbar,
                          slotmap[:, :].rearrange("(p j) one -> p (j one)",
                                                  p=128),
                          czi_sb[:, :], "rst_sm")
                    sel_done = b.last("dve")
                    e_cm = None
                    for t in range(GBNT):
                        e_cm = b.op("pe", [sel_done, e_ocb, ld_cc] +
                                    ([R.get("ct_war")] if t == 0 else []),
                                    lambda e, t=t: e.matmul(
                            ct_ps[0:8, t:t + 1], selg[:, ts(t, E)],
                            onescb[:, 0:1], start=True, stop=True),
                            ev=(t == GBNT - 1))
                    e_cc = b.op("dve", [e_cm], lambda e: e.tensor_copy(
                        cnt_sb[0:8, :], ct_ps[0:8, 0:GBNT]), ev=True)
                    R["ct_war"] = e_cc
                    e_zz = b.op("dve", [], lambda e: e.memset(
                        zz_sb[:, :], 0.0), ev=True)
                    e_sc = b.op("dve", [e_cc, e_zz],
                                lambda e: e.tensor_tensor_scan(
                        incl_sb[0:8, :], cnt_sb[0:8, :], zz_sb[0:8, :], 0.0,
                        ALU.add, ALU.add), ev=True)
                    e_ex = b.op("dve", [e_sc], lambda e: e.tensor_sub(
                        incl_sb[0:8, :], incl_sb[0:8, :], cnt_sb[0:8, :]),
                        ev=True)

                    e_er = None
                    e_pos = None
                    e_dst = {}
                    for t in range(GBNT):
                        e_e1 = b.op("pe", [e_ex, ld_c, e_er],
                                    lambda e, t=t: e.matmul(
                            ct_ps[0:1, 128:136], incl_sb[0:8, t:t + 1],
                            idf_sb[0:8, 0:8], start=True, stop=True), ev=True)
                        e_er = b.op("dve", [e_e1], lambda e: e.tensor_copy(
                            erow_sb[0:1, :], ct_ps[0:1, 128:136]), ev=True)
                        e_p1 = b.op("pe", [e_er, e_pos],
                                    lambda e, t=t: e.matmul(
                            po_ps[:, 0:8], triu_sb[:, :], selg[:, ts(t, E)],
                            start=True, stop=False), ev=False)
                        e_p2 = b.op("pe", [], lambda e: e.matmul(
                            po_ps[:, 0:8], ones_sb[0:1, 0:128],
                            erow_sb[0:1, 0:8], start=False, stop=True),
                            ev=True)
                        for k in range(2):
                            icol = idxg[:, 2 * t + k:2 * t + k + 1]
                            dcol = dstf[:, (t % 4) * 2 + k + 8 * (0):
                                        (t % 4) * 2 + k + 1] \
                                if False else dstf[:, (t // 4) * 8
                                                   + (t % 4) * 2 + k:
                                                   (t // 4) * 8
                                                   + (t % 4) * 2 + k + 1]
                            e_oh = b.op("dve",
                                        [e_p2, e_dst.get(k)] +
                                        ([R.get("dstf_war")] if t == 0
                                         else []),
                                        lambda e, icol=icol: e.tensor_scalar(
                                scr8[:, :], io8_sb[:, :], icol, None,
                                ALU.is_equal), ev=True)
                            e_mu = b.op("dve", [e_oh],
                                        lambda e: e.tensor_mul(
                                scr8[:, :], scr8[:, :], po_ps[:, 0:8]),
                                ev=True)
                            e_ps = b.op("dve", [e_mu],
                                        lambda e, k=k: e.reduce_sum(
                                psl[:, k:k + 1], scr8[:, :], axis=AX.X),
                                ev=True)
                            e_ps = b.op("dve", [e_ps],
                                        lambda e, k=k:
                                        e.tensor_scalar_min(
                                psl[:, k:k + 1], psl[:, k:k + 1],
                                float(CAP)), ev=True)
                            e_df = b.op("dve", [e_ps],
                                        lambda e, icol=icol, dcol=dcol:
                                        e.scalar_tensor_tensor(
                                dcol, icol, float(CS),
                                psl[:, k:k + 1],
                                ALU.mult, ALU.add), ev=True)
                            e_dst[k] = e_df
                        e_pos = b.last("dve")
                    e_conv = b.op("dve",
                                  [e_dst.get(0), e_dst.get(1),
                                   ("idsc", b._count.get("idsc", 0))
                                   if b._count.get("idsc", 0) else None],
                                  lambda e: e.tensor_copy(
                        dsti32[:, :], dstf[:, :]), ev=True)
                    R["dstf_war"] = e_conv
                    rst_sm_ev = ("rst_sm", b._count["rst_sm"])
                    for t in range(GBNT):
                        for k in range(2):
                            col = (t // 4) * 8 + (t % 4) * 2 + k
                            b.idma("pool",
                                   [e_conv, ld_c, rst_sm_ev],
                                   lambda e, t=t, col=col:
                                   e.indirect_dma_start(
                                slotmap[:, :],
                                IndirectOffsetOnAxis(
                                    dsti32[:, col:col + 1], 0),
                                pay_sb[:, t:t + 1],
                                None, bounds_check=None), "idsc")
                    route_done = ("idsc", b._count["idsc"])
                    spills_done = [
                        ("xsp0", b._count["xsp0"]),
                        ("xsp1", b._count["xsp1"]),
                        ("csp0", b._count["csp0"]),
                        ("csp1", b._count["csp1"]),
                        ("ld_ids", b._count["ld_ids"]),
                        ("zsp", b._count["zsp"])]

                bsc2 = contextlib.ExitStack()
                with bsc2:
                    TB = bsc2.enter_context
                    FH = F // 2
                    FTH = FT // 2
                    NTGS = CAP // 512
                    w1_sb = [TB(nc.sbuf_tensor(U("w1_sb"), [128, NCH * FH],
                                               BF16)) for _ in range(2)]
                    w2_sb = [TB(nc.sbuf_tensor(U("w2_sb"), [128, FTH * D],
                                               BF16)) for _ in range(2)]
                    hT = TB(nc.sbuf_tensor(U("hT"), [128, FTH * 512], BF16))
                    xsr = [TB(nc.sbuf_tensor(U("xsr"), [128, 4 * RW], BF16))
                           for i in range(2)]
                    xsT = TB(nc.sbuf_tensor(U("xsT"), [128, NCH * 512], BF16))
                    cfg_sb = [TB(nc.sbuf_tensor(U("cfg"), [128, 4], F32))
                              for i in range(2)]
                    idxB = [TB(nc.sbuf_tensor(U("idxB"), [128, 4], I32))
                            for i in range(2)]
                    sm_sb = [TB(nc.sbuf_tensor(U("smsb"), [128, 4], I32))
                             for i in range(2)]
                    idsc = [TB(nc.sbuf_tensor(U("idsc"), [128, 4], F32))
                            for i in range(2)]
                    tmp_sb = [TB(nc.sbuf_tensor(U("tmpb"), [128, D], F32))
                              for i in range(2)]
                    h_ps = [TB(nc.psum_tensor(U("hps"), [128, 512], F32))
                            for i in range(2)]
                    o_ps = [TB(nc.psum_tensor(U("ops"), [128, 512], F32))
                            for i in range(4)]
                    pt_ps = [TB(nc.psum_tensor(U("ptps"), [128, 512], BF16))
                             for i in range(2)]

                    bbar = b.bar()

                    def load_slot(sl, deps):
                        e_i, fh = divmod(sl, 2)
                        buf = sl % 2
                        lw1 = b.dma("pool", deps,
                                    w1_sb[buf][:].rearrange(
                                        "p (c f) -> p c f", c=NCH),
                                    w1[e_i, :, ts(fh, FH)].rearrange(
                                        "(c p) f -> p c f", p=128),
                                    f"ld_w1{buf}")
                        lw2 = b.dma("pool", [],
                                    w2_sb[buf][:].rearrange(
                                        "p (c d) -> p c d", c=FTH),
                                    w2[e_i, ts(fh, FH), :].rearrange(
                                        "(c p) d -> p c d", p=128),
                                    f"ld_w2{buf}")
                        return [lw1, lw2]

                    ev_hT = {}
                    ev_hps = {}
                    ev_op_war = {}
                    ev_xsr = {}
                    ev_cfg = {}
                    ev_cfg_use = {}
                    ev_dsg_use = {}
                    ev_tmp = {}
                    ev_pt = {}
                    out2_last = None
                    slot_pe_end = {}
                    wload = {0: load_slot(0, bbar)}
                    for sl in range(2 * E):
                        e_i, fh = divmod(sl, 2)
                        buf = sl % 2
                        if sl + 1 < 2 * E:
                            wload[sl + 1] = load_slot(
                                sl + 1, [slot_pe_end.get(sl - 1)] + bbar)
                        lw = wload[sl]
                        for tgi in range(NTGS):
                            xp = tgi % 2
                            xr = xsr[xp]
                            s0 = e_i * CS + tgi * 512
                            lsm = b.dma(
                                "sp",
                                [route_done]
                                + (R.get(("smR", xp)) or []) + bbar,
                                sm_sb[xp][:, :],
                                slotmap[s0:s0 + 512, :].rearrange(
                                    "(p j) one -> p (j one)", p=128),
                                f"ld_sm{xp}")
                            lxq = {}
                            for j4 in range(4):
                                hh = j4 // 2
                                lxq[j4] = b.idma(
                                    "pool",
                                    [lsm, ev_xsr.get((xp, hh)),
                                     ev_cfg.get((xp, j4))]
                                    + spills_done + bbar,
                                    lambda e, j4=j4, xr=xr, xp=xp:
                                    e.indirect_dma_start(
                                        xr[:, j4 * RW:(j4 + 1) * RW],
                                        None, spill_tok[:, :],
                                        IndirectOffsetOnAxis(
                                            sm_sb[xp][:, j4:j4 + 1], 0),
                                        bounds_check=None),
                                    f"gx{xp}{j4}")
                            R[("smR", xp)] = list(lxq.values())
                            e_tps = []
                            for j in range(4):
                                hh = j // 2
                                lxdep = [lxq[j]]
                                e_cf = b.op("dve",
                                            lxdep
                                            + [ev_cfg_use.get((xp, j))],
                                            lambda e, j=j, e_i=e_i, xp=xp,
                                            xr=xr: e.tensor_copy(
                                    cfg_sb[xp][:, j:j + 1],
                                    xr[:, j * RW + 1024 + e_i:
                                       j * RW + 1025 + e_i]), ev=True)
                                e_id = b.op("dve",
                                            [e_cf,
                                             ev_dsg_use.get((xp, j))],
                                            lambda e, j=j, xp=xp, xr=xr:
                                            e.scalar_tensor_tensor(
                                    idsc[xp][:, j:j + 1],
                                    xr[:, j * RW + 1032:j * RW + 1033],
                                    64.0,
                                    xr[:, j * RW + 1033:j * RW + 1034],
                                    ALU.mult, ALU.add), ev=True)
                                padoff = float(NB * SEQ - g * TOKG)
                                e_id = b.op("dve", [e_id],
                                            lambda e, j=j, xp=xp, xr=xr,
                                            padoff=padoff:
                                            e.scalar_tensor_tensor(
                                    idsc[xp][:, j:j + 1],
                                    xr[:, j * RW + 1034:j * RW + 1035],
                                    -padoff,
                                    idsc[xp][:, j:j + 1],
                                    ALU.mult, ALU.add), ev=True)
                                e_id = b.op("dve", [e_id],
                                            lambda e, j=j, xp=xp,
                                            padoff=padoff:
                                            e.tensor_single_scalar(
                                    idsc[xp][:, j:j + 1],
                                    idsc[xp][:, j:j + 1],
                                    padoff + g * TOKG, ALU.add), ev=True)
                                e_dv = b.op("dve", [e_id],
                                            lambda e, j=j, xp=xp:
                                            e.tensor_copy(
                                    idxB[xp][:, j:j + 1],
                                    idsc[xp][:, j:j + 1]), ev=True)
                                ev_cfg[(xp, j)] = e_dv
                                for c2 in range(2):
                                    dep = lxdep + [ev_pt.get(c2), ld_cc,
                                                   out2_last]
                                    e_tp = None
                                    for cc in range(4):
                                        c = c2 * 4 + cc
                                        e_tp = b.op(
                                            "pe", dep if cc == 0 else [],
                                            lambda e, j=j, c=c, cc=cc, c2=c2,
                                            xr=xr: e.transpose(
                                                pt_ps[c2][:, ts(cc, 128)],
                                                xr[:, j * RW + c * 128:
                                                   j * RW + c * 128 + 128],
                                                id_sb[:, :]),
                                            ev=(cc == 3))
                                    e_ev = b.op("dve", [e_tp],
                                                lambda e, j=j, c2=c2:
                                                e.tensor_copy(
                                        xsT[:].rearrange("p (c t) -> p c t",
                                                         c=NCH)
                                        [:, slice(c2 * 4, c2 * 4 + 4),
                                         ts(j, 128)],
                                        pt_ps[c2][:, :].rearrange(
                                            "p (c t) -> p c t", c=4)),
                                        ev=True)
                                    ev_pt[c2] = e_ev
                                    e_tps.append(e_ev)
                                if j % 2 == 1:
                                    ev_xsr[(xp, hh)] = b.last("pe")
                            last_h = None
                            for ft in range(FTH):
                                hp = h_ps[ft % 2]
                                dep0 = lw + e_tps + [ev_hps.get(ft % 2),
                                                     out2_last]
                                for c in range(NCH):
                                    last_h = b.op(
                                        "pe", dep0 if c == 0 else [],
                                        lambda e, hp=hp, ft=ft, c=c,
                                        buf=buf: e.matmul(
                                            hp[:, :],
                                            w1_sb[buf][:, c * FH + ft * 128:
                                                       c * FH + ft * 128
                                                       + 128],
                                            xsT[:, ts(c, 512)],
                                            start=(c == 0),
                                            stop=(c == NCH - 1)),
                                        ev=(c == NCH - 1))
                                e_g = b.op("act",
                                           [last_h, ev_hT.get(ft), out2_last],
                                           lambda e, hp=hp, ft=ft:
                                           e.activation(
                                    hT[:, ts(ft, 512)], hp[:, :],
                                    AF.Gelu_apprx_tanh), ev=True)
                                ev_hps[ft % 2] = e_g
                                ev_hT[ft] = e_g
                            for half in range(2):
                                last_o = {}
                                for ft in range(FTH):
                                    for j in range(2):
                                        for dg in range(2):
                                            pi = j * 2 + dg
                                            dd = [ev_hT[ft]]
                                            if ft == 0:
                                                dd.append(ev_op_war.get(pi))
                                            last_o[pi] = b.op(
                                                "pe", dd,
                                                lambda e, pi=pi, ft=ft, j=j,
                                                dg=dg, half=half, buf=buf:
                                                e.matmul(
                                                    o_ps[pi][:, :],
                                                    hT[:, ft * 512
                                                       + (half * 2 + j) * 128:
                                                       ft * 512
                                                       + (half * 2 + j) * 128
                                                       + 128],
                                                    w2_sb[buf][:, ft * D
                                                               + dg * 512:
                                                               ft * D
                                                               + dg * 512
                                                               + 512],
                                                    start=(ft == 0),
                                                    stop=(ft == FTH - 1)),
                                                ev=(ft == FTH - 1))
                                out2_last = last_o[3]
                                for j in range(2):
                                    jt = half * 2 + j
                                    tmp = tmp_sb[j]
                                    e_t = None
                                    for dg in range(2):
                                        pi = j * 2 + dg
                                        e_t = b.op(
                                            "dve",
                                            [last_o[pi], ev_tmp.get(j),
                                             ev_cfg.get((xp, jt))],
                                            lambda e, tmp=tmp, dg=dg, pi=pi,
                                            jt=jt, xp=xp:
                                            e.tensor_scalar_mul(
                                                tmp[:, ts(dg, 512)],
                                                o_ps[pi][:, :],
                                                cfg_sb[xp][:, jt:jt + 1]),
                                            ev=True)
                                        ev_op_war[pi] = e_t
                                    ev_cfg_use[(xp, jt)] = e_t
                                    e_a = b.idma(
                                        "pool",
                                        [e_t, ev_cfg[(xp, jt)],
                                         ("st_out0",
                                          b._count["st_out0"]),
                                         ("st_out1",
                                          b._count.get("st_out1", 0)),
                                         ("oacc", b._count.get("oacc", 0))],
                                        lambda e, jt=jt, xp=xp, tmp=tmp:
                                        e.indirect_dma_start(
                                            out[:, :],
                                            IndirectOffsetOnAxis(
                                                idxB[xp][:, jt:jt + 1], 0),
                                            tmp[:, :], None,
                                            bounds_check=None,
                                            compute_op=ALU.add), "oacc")
                                    ev_tmp[j] = e_a
                                    ev_dsg_use[(xp, jt)] = e_a
                        slot_pe_end[sl] = out2_last
                    R["sm_war"] = [
                        ("ld_sm0", b._count["ld_sm0"]),
                        ("ld_sm1", b._count.get("ld_sm1", 0))]
                    R["sm_war"] = [d for d in R["sm_war"] if d[1] > 0]

        final_deps = [("st_out0", b._count["st_out0"]),
                      ("st_out1", b._count.get("st_out1", 0)),
                      ("oacc", b._count.get("oacc", 0))]
        final_deps = [d for d in final_deps if d[1] > 0]
        b.op("sp", final_deps, lambda e: e.nop())
    return nc


def emit_top2(b, g_ps, gwrk, comb_sb, ccol, e_gmm, war,
              selg=None, idxg=None, io8=None, gti=0):
    e1 = b.op("dve", [e_gmm, war], lambda e: e.reduce_max(
        gwrk[:, 8:9], g_ps[:, 0:E], axis=AX.X, negate=True), ev=True)
    e2 = b.op("act", [e1, war], lambda e: e.activation(
        gwrk[:, 0:E], g_ps[:, 0:E], AF.Exp, bias=gwrk[:, 8:9]), ev=True)
    e3 = b.op("dve", [e2], lambda e: e.reduce_sum(
        gwrk[:, 9:10], gwrk[:, 0:E], axis=AX.X), ev=True)
    e4 = b.op("dve", [e3], lambda e: e.reciprocal(
        gwrk[:, 10:11], gwrk[:, 9:10]), ev=True)
    e5 = b.op("dve", [e4], lambda e: e.tensor_scalar_mul(
        gwrk[:, 0:E], gwrk[:, 0:E], gwrk[:, 10:11]), ev=True)
    e6 = b.op("dve", [e5], lambda e: e.reduce_max(
        gwrk[:, 11:12], gwrk[:, 0:E], axis=AX.X), ev=True)
    e7 = b.op("dve", [e6], lambda e: e.tensor_scalar(
        gwrk[:, 16:16 + E], gwrk[:, 0:E], gwrk[:, 11:12], None, ALU.is_lt),
        ev=True)
    e8 = b.op("dve", [e7], lambda e: e.tensor_mul(
        gwrk[:, 16:16 + E], gwrk[:, 0:E], gwrk[:, 16:16 + E]), ev=True)
    e9 = b.op("dve", [e8], lambda e: e.reduce_max(
        gwrk[:, 12:13], gwrk[:, 16:16 + E], axis=AX.X), ev=True)
    e10 = b.op("dve", [e9], lambda e: e.tensor_scalar(
        gwrk[:, 16:16 + E], gwrk[:, 0:E], gwrk[:, 12:13], None, ALU.is_ge),
        ev=True)
    n1 = b.op("dve", [e10], lambda e: e.tensor_copy(
        selg[:, ts(gti, E)], gwrk[:, 16:16 + E]), ev=True)
    n2 = b.op("dve", [e10], lambda e: e.tensor_scalar(
        gwrk[:, 24:32], gwrk[:, 0:E], gwrk[:, 11:12], None, ALU.is_ge),
        ev=True)
    n3 = b.op("dve", [n2], lambda e: e.tensor_sub(
        gwrk[:, 32:40], gwrk[:, 16:16 + E], gwrk[:, 24:32]), ev=True)
    n4 = b.op("dve", [n3], lambda e: e.tensor_mul(
        gwrk[:, 24:32], io8[:, :], gwrk[:, 24:32]), ev=True)
    n5 = b.op("dve", [n4], lambda e: e.reduce_sum(
        idxg[:, 2 * gti:2 * gti + 1], gwrk[:, 24:32], axis=AX.X), ev=True)
    n6 = b.op("dve", [n4], lambda e: e.tensor_mul(
        gwrk[:, 32:40], io8[:, :], gwrk[:, 32:40]), ev=True)
    n7 = b.op("dve", [n6], lambda e: e.reduce_sum(
        idxg[:, 2 * gti + 1:2 * gti + 2], gwrk[:, 32:40], axis=AX.X), ev=True)
    e11 = b.op("dve", [n1, n3], lambda e: e.tensor_mul(
        gwrk[:, 16:16 + E], gwrk[:, 0:E], gwrk[:, 16:16 + E]), ev=True)
    e12 = b.op("dve", [e11], lambda e: e.reduce_sum(
        gwrk[:, 13:14], gwrk[:, 16:16 + E], axis=AX.X), ev=True)
    e13 = b.op("dve", [e12], lambda e: e.reciprocal(
        gwrk[:, 14:15], gwrk[:, 13:14]), ev=True)
    e14 = b.op("dve", [e13], lambda e: e.tensor_scalar_mul(
        comb_sb[:, ccol:ccol + E], gwrk[:, 16:16 + E], gwrk[:, 14:15]),
        ev=True)
    return [e1, e2], e14, n7


def host_constants(SEQ, KT):
    ident = np.eye(128, dtype=np.float32)
    ones = np.ones((1, 128), dtype=np.float32)
    sel = np.zeros((128, 256), dtype=np.float32)
    for r in list(range(0, 32)) + list(range(64, 96)):
        sel[r, r] = 1.0
    for r in list(range(32, 64)) + list(range(96, 128)):
        sel[r, 128 + r] = 1.0
    inv_freq = ROPE_BASE ** (-(np.arange(0, DH, 2, dtype=np.float64) / DH))
    pos = np.arange(SEQ, dtype=np.float64)
    theta = inv_freq[:, None] * pos[None, :]
    c64 = np.concatenate([np.cos(theta)] * 2, 0)
    s64 = np.concatenate([np.sin(theta)] * 2, 0)
    cos = np.concatenate([c64, c64], 0).astype(np.float32)
    sin = np.concatenate([-s64, s64], 0).astype(np.float32)
    swp = np.zeros((128, 128), dtype=np.float32)
    swp[np.arange(64), 64 + np.arange(64)] = 1.0
    swp[64 + np.arange(64), np.arange(64)] = 1.0
    mask = np.zeros((128, 4 * 512), dtype=np.float32)
    p = np.arange(128)[:, None]
    q = np.arange(512)[None, :]
    for m in range(4):
        mask[:, m * 512:(m + 1) * 512] = (128 * m + p <= q).astype(np.float32)
    iota = (np.arange(128)[:, None]
            + 128 * np.arange(KT)[None, :]).astype(np.float32)
    return {"c_id": ident, "c_ones": ones, "c_sel": sel, "c_cos": cos,
            "c_sin": sin, "c_mask": mask, "c_iota": iota, "c_swp": swp,
            "c_ek": E_BITS.reshape(1, H).astype(np.float32),
            "c_triu": np.triu(np.ones((128, 128), np.float32), 1),
            "c_io8": np.broadcast_to(np.arange(8, dtype=np.float32),
                                     (128, 8)).copy()}


def host_ids(tokg):
    t = np.arange(tokg)
    ids = np.zeros((tokg, 120), np.float32)
    ids[:, 0] = t // 64
    ids[:, 1] = t % 64
    ids[:, 2] = 1.0
    return np.ascontiguousarray(ids.astype(ml_dtypes.bfloat16))


def host_pay(tokg):
    t = np.arange(tokg // 128)
    p = np.arange(128)
    return np.ascontiguousarray(
        (t[None, :] * 128 + p[:, None]).astype(np.int32))


def permute_wqkv(w):
    w = np.asarray(w)
    perm = np.arange(3 * D)
    within = np.concatenate([np.arange(0, 64, 2), 64 + np.arange(0, 64, 2),
                             np.arange(1, 64, 2), 64 + np.arange(1, 64, 2)])
    for s in range(2):
        for c in ROPE_CHUNKS:
            base = s * D + c * 128
            perm[base:base + 128] = base + within
    return np.ascontiguousarray(w[:, perm])


_NC_CACHE = {}


def kernel(L0_attn_x, L0_attn_w_qkv, L0_attn_w_o, L0_attn_log_xi,
           L0_ffn_gate_w, L0_ffn_w1, L0_ffn_w2):
    NB_TOT, SEQ, _ = L0_attn_x.shape
    NB = NB_TOT // N_CORES
    GB = 2 if NB % 2 == 0 else 1
    F = L0_ffn_w1.shape[2]
    key = (NB, SEQ, GB, F)
    if key not in _NC_CACHE:
        _NC_CACHE[key] = build_program(NB, SEQ, GB, F)
    nc = _NC_CACHE[key]
    consts = host_constants(SEQ, SEQ // 128)
    consts["c_ids"] = host_ids(GB * SEQ)
    consts["c_pay"] = host_pay(GB * SEQ)
    shared = {
        "wqkv": permute_wqkv(L0_attn_w_qkv),
        "wo": np.ascontiguousarray(np.asarray(L0_attn_w_o)),
        "gw": np.ascontiguousarray(np.asarray(L0_ffn_gate_w)),
        "w1": np.ascontiguousarray(np.asarray(L0_ffn_w1)),
        "w2": np.ascontiguousarray(np.asarray(L0_ffn_w2)),
        "lxi": np.ascontiguousarray(
            np.asarray(L0_attn_log_xi).reshape(1, H)),
        **consts,
    }
    x = np.ascontiguousarray(np.asarray(L0_attn_x))
    in_maps = [{"x": x[i * NB:(i + 1) * NB], **shared}
               for i in range(N_CORES)]
    res = run_bass_kernel_spmd(nc, in_maps, core_ids=list(range(N_CORES)))
    return np.concatenate(
        [res.results[i]["out"][:NB * SEQ].reshape(NB, SEQ, D)
         for i in range(N_CORES)],
        axis=0)


# version 11
# speedup vs baseline: 1.4921x; 1.0141x over previous
import contextlib

import numpy as np
import ml_dtypes
import concourse.bass as bass
import concourse.mybir as mybir
from concourse.bass_utils import run_bass_kernel_spmd
from concourse.bass_core_rust import IndirectOffsetOnAxis

F32 = mybir.dt.float32
BF16 = mybir.dt.bfloat16
I32 = mybir.dt.int32
I16 = mybir.dt.int16
AF = mybir.ActivationFunctionType
ALU = mybir.AluOpType
AX = mybir.AxisListType

N_CORES = 8
D = 1024
NCH = 8
H = 16
DH = 64
E = 8
ROPE_BASE = 10000.0
HEAD_TYPES = np.arange(H) % 4
PI_BITS = ((HEAD_TYPES >> 1) & 1).astype(np.float32)
E_BITS = (HEAD_TYPES & 1).astype(np.float32)
ROPE_CHUNKS = [c for c in range(NCH) if PI_BITS[2 * c] > 0]


class EB:
    ENGINES = ("pe", "act", "dve", "pool", "sp")

    def __init__(self, nc, ctx):
        self.nc = nc
        self.ctx = ctx
        self.eng = {"pe": nc.tensor, "act": nc.scalar, "dve": nc.vector,
                    "pool": nc.gpsimd, "sp": nc.sync}
        self._sems = {}
        self._count = {}
        self._waited = {e: {} for e in self.ENGINES}
        for e in self.ENGINES:
            self._mk("ev_" + e)

    def _mk(self, name):
        h = self.ctx.enter_context(self.nc.semaphore(name))
        self._sems[name] = h
        self._count[name] = 0
        return h

    def sem(self, name):
        if name not in self._sems:
            self._mk(name)
        return self._sems[name]

    def _wait(self, engine, deps):
        for dep in deps:
            if dep is None:
                continue
            sname, val = dep
            if self._waited[engine].get(sname, 0) >= val:
                continue
            self.eng[engine].wait_ge(self._sems[sname], val)
            self._waited[engine][sname] = val

    def op(self, engine, deps, fn, ev=False):
        self._wait(engine, [d for d in deps if d is not None])
        inst = fn(self.eng[engine])
        if ev:
            sname = "ev_" + engine
            self._count[sname] += 1
            inst.then_inc(self._sems[sname], 1)
            return (sname, self._count[sname])
        return None

    def dma(self, engine, deps, out, in_, sem_name, **kw):
        self._wait(engine, [d for d in deps if d is not None])
        s = self.sem(sem_name)
        inst = self.eng[engine].dma_start(out=out, in_=in_, **kw)
        self._count[sem_name] += 16
        inst.then_inc(s, 16)
        return (sem_name, self._count[sem_name])

    def idma(self, engine, deps, fn, sem_name):
        self._wait(engine, [d for d in deps if d is not None])
        sm = self.sem(sem_name)
        inst = fn(self.eng[engine])
        self._count[sem_name] += 16
        inst.then_inc(sm, 16)
        return (sem_name, self._count[sem_name])

    def last(self, engine):
        sname = "ev_" + engine
        return (sname, self._count[sname])

    def bar(self):
        return [(s, c) for s, c in self._count.items() if c > 0]


def ts(i, n):
    return slice(i * n, (i + 1) * n)


def ln_chain(b, stats, sc, e_sum, e_sq, n):
    e_m = b.op("dve", [e_sum], lambda e: e.tensor_scalar_mul(
        stats[:, sc + 2:sc + 3], stats[:, sc:sc + 1], 1.0 / n), ev=True)
    e_x2 = b.op("dve", [e_sq], lambda e: e.tensor_scalar_mul(
        stats[:, sc + 3:sc + 4], stats[:, sc + 1:sc + 2], 1.0 / n), ev=True)
    e_m2 = b.op("dve", [e_m], lambda e: e.tensor_mul(
        stats[:, sc + 4:sc + 5], stats[:, sc + 2:sc + 3],
        stats[:, sc + 2:sc + 3]), ev=True)
    e_v = b.op("dve", [e_x2, e_m2], lambda e: e.tensor_sub(
        stats[:, sc + 5:sc + 6], stats[:, sc + 3:sc + 4],
        stats[:, sc + 4:sc + 5]), ev=True)
    e_v = b.op("dve", [e_v], lambda e: e.tensor_scalar_add(
        stats[:, sc + 5:sc + 6], stats[:, sc + 5:sc + 6], 1e-5), ev=True)
    e_l = b.op("act", [e_v], lambda e: e.activation(
        stats[:, sc + 5:sc + 6], stats[:, sc + 5:sc + 6], AF.Ln), ev=True)
    e_r = b.op("act", [e_l], lambda e: e.activation(
        stats[:, sc + 6:sc + 7], stats[:, sc + 5:sc + 6], AF.Exp,
        scale=-0.5), ev=True)
    e_nm = b.op("dve", [e_m], lambda e: e.tensor_scalar_mul(
        stats[:, sc + 7:sc + 8], stats[:, sc + 2:sc + 3], -1.0), ev=True)
    return e_r, e_nm


MARKS = []


def build_program(NB, SEQ, GB, F):
    KT = SEQ // 128
    QG = SEQ // 512
    NT = SEQ // 128
    FT = F // 128
    NGR = NB // GB
    NTG = (SEQ * GB) // 512

    nc = bass.Bass("TRN2", target_bir_lowering=False)
    ap = lambda t: t.ap()
    x_in = ap(nc.dram_tensor("x", [NB, SEQ, D], F32, kind="ExternalInput"))
    wqkv = ap(nc.dram_tensor("wqkv", [D, 3 * D], F32, kind="ExternalInput"))
    wo = ap(nc.dram_tensor("wo", [D, D], F32, kind="ExternalInput"))
    gw = ap(nc.dram_tensor("gw", [D, E], F32, kind="ExternalInput"))
    w1 = ap(nc.dram_tensor("w1", [E, D, F], F32, kind="ExternalInput"))
    w2 = ap(nc.dram_tensor("w2", [E, F, D], F32, kind="ExternalInput"))
    lxi = ap(nc.dram_tensor("lxi", [1, H], F32, kind="ExternalInput"))
    c_id = ap(nc.dram_tensor("c_id", [128, 128], F32, kind="ExternalInput"))
    c_ones = ap(nc.dram_tensor("c_ones", [1, 128], F32, kind="ExternalInput"))
    c_sel = ap(nc.dram_tensor("c_sel", [128, 256], F32, kind="ExternalInput"))
    c_cos = ap(nc.dram_tensor("c_cos", [128, SEQ], F32, kind="ExternalInput"))
    c_sin = ap(nc.dram_tensor("c_sin", [128, SEQ], F32, kind="ExternalInput"))
    c_swp = ap(nc.dram_tensor("c_swp", [128, 128], F32, kind="ExternalInput"))
    c_mask = ap(nc.dram_tensor("c_mask", [128, 4 * 512], F32, kind="ExternalInput"))
    c_iota = ap(nc.dram_tensor("c_iota", [128, KT], F32, kind="ExternalInput"))
    c_ek = ap(nc.dram_tensor("c_ek", [1, H], F32, kind="ExternalInput"))
    c_triu = ap(nc.dram_tensor("c_triu", [128, 128], F32, kind="ExternalInput"))
    c_io8 = ap(nc.dram_tensor("c_io8", [128, 8], F32, kind="ExternalInput"))
    c_ids = ap(nc.dram_tensor("c_ids", [GB * SEQ, 120], BF16,
                              kind="ExternalInput"))
    c_pay = ap(nc.dram_tensor("c_pay", [128, (GB * SEQ) // 128], I32,
                              kind="ExternalInput"))
    out = ap(nc.dram_tensor("out", [NB * SEQ + 128, D], F32,
                             kind="ExternalOutput"))
    TOKG = GB * SEQ
    CAP = max((3 * TOKG) // 8 // 512, 1) * 512
    CS = CAP + 128
    RW = 1152
    spill_tok = ap(nc.dram_tensor("spill_tok", [GB * SEQ + 128, RW], BF16,
                                  kind="Internal"))
    slotmap = ap(nc.dram_tensor("slotmap", [E * CS, 1], I32, kind="Internal"))
    outflat = out

    ctx = contextlib.ExitStack()
    _uq = [0]

    def U(name):
        _uq[0] += 1
        return f"{name}_{_uq[0]}"
    with ctx:
        b = EB(nc, ctx)
        T = ctx.enter_context
        R = {}

        id_sb = T(nc.sbuf_tensor("id_sb", [128, 128], BF16))
        swp_sb = T(nc.sbuf_tensor("swp_sb", [128, 128], BF16))
        ones_sb = T(nc.sbuf_tensor("ones_sb", [1, 128], F32))
        sel_sb = T(nc.sbuf_tensor("sel_sb", [128, 256], BF16))
        iota_sb = T(nc.sbuf_tensor("iota_sb", [128, KT], F32))
        ek_sb = T(nc.sbuf_tensor("ek_sb", [1, H], F32))
        lxi_sb = T(nc.sbuf_tensor("lxi_sb", [1, H], F32))
        rate_sb = T(nc.sbuf_tensor("rate_sb", [1, H], F32))
        rateB = T(nc.sbuf_tensor("rateB", [128, H], F32))
        gate_sb = T(nc.sbuf_tensor("gate_sb", [128, NCH * E], BF16))
        stats = T(nc.sbuf_tensor("stats", [128, 32], F32))
        gwrk = T(nc.sbuf_tensor("gwrk", [128, 48], F32))
        rcp_sb = T(nc.sbuf_tensor("rcp_sb", [1, 512], F32))
        avbf = T(nc.sbuf_tensor("avbf", [128, 512], BF16))
        onesb = T(nc.sbuf_tensor("onesb", [1, 128], BF16))
        rcpb_sb = T(nc.sbuf_tensor("rcpb_sb", [1, 512], BF16))
        den_sb = T(nc.sbuf_tensor("den_sb", [1, 512], BF16))
        rcpt_sb = T(nc.sbuf_tensor("rcpt_sb", [128, 4], F32))
        rcptb_sb = T(nc.sbuf_tensor("rcptb_sb", [128, 4], BF16))
        triu_sb = T(nc.sbuf_tensor("triu_sb", [128, 128], BF16))
        io8_sb = T(nc.sbuf_tensor("io8_sb", [128, 8], F32))
        idf_sb = T(nc.sbuf_tensor("idf_sb", [8, 8], F32))
        oor_sb = T(nc.sbuf_tensor("oor_sb", [128, RW], BF16))
        onescb = T(nc.sbuf_tensor("onescb", [128, 1], BF16))

        e_ob = b.op("dve", [], lambda e: e.memset(onesb[:, :], 1.0), ev=True)
        e_ocb = b.op("dve", [], lambda e: e.memset(onescb[:, :], 1.0), ev=True)
        e_oor = b.op("dve", [], lambda e: e.memset(oor_sb[:, :], 0.0), ev=True)
        pay_sb = T(nc.sbuf_tensor("pay_sb", [128, TOKG // 128], I32))
        czi_sb = T(nc.sbuf_tensor("czi_sb", [128, (E * CS) // 128], I32))
        e_czi = b.op("dve", [], lambda e: e.memset(czi_sb[:, :], TOKG),
                     ev=True)
        zsp = b.dma("sp", [e_oor], spill_tok[TOKG:TOKG + 128, :],
                    oor_sb[:, :], "zsp")
        ld_c = None
        for dst, src in ((ones_sb, c_ones), (iota_sb, c_iota),
                         (ek_sb, c_ek), (lxi_sb, lxi)):
            ld_c = b.dma("sp", [], dst[:, :], src[:, :], "ld_const")
        ld_cc = None
        for dst, src in ((id_sb, c_id), (sel_sb, c_sel), (swp_sb, c_swp),
                         (triu_sb, c_triu)):
            ld_cc = b.dma("pool", [], dst[:, :], src[:, :], "ld_constc")
        ld_c = b.dma("sp", [], io8_sb[:, :], c_io8[:, :], "ld_const")
        ld_c = b.dma("sp", [], idf_sb[:, :], c_id[0:8, 0:8], "ld_const")
        ld_c = b.dma("sp", [], pay_sb[:, :], c_pay[:, :], "ld_const")
        ld_ids = b.dma("sp", [],
                       spill_tok[0:TOKG, 1032:1152].rearrange(
                           "(t p) c -> p t c", p=128),
                       c_ids.rearrange("(t p) c -> p t c", p=128), "ld_ids")
        ld_gate = b.dma("pool", [],
                        gate_sb[:].rearrange("p (c g) -> p c g", c=NCH),
                        gw.rearrange("(c p) g -> p c g", p=128), "ld_gate")

        e0 = b.op("act", [ld_c], lambda e: e.activation(
            rate_sb[:, :], lxi_sb[:, :], AF.Exp, scale=-1.0), ev=True)
        e0 = b.op("dve", [e0, ld_c], lambda e: e.tensor_mul(
            rate_sb[:, :], rate_sb[:, :], ek_sb[:, :]), ev=True)
        with nc.psum_tensor("ps_init", [128, 512], F32) as ps_init:
            e0 = b.op("pe", [e0, ld_c], lambda e: e.matmul(
                ps_init[:, 0:H], ones_sb[0:1, :], rate_sb[0:1, :],
                start=True, stop=True), ev=True)
            e0 = b.op("dve", [e0], lambda e: e.tensor_copy(
                rateB[:, :], ps_init[:, 0:H]), ev=True)
        e_rateB = e0

        for g in range(NGR):
            gsc = contextlib.ExitStack()
            with gsc:
                Tg = gsc.enter_context
                comb_sb = Tg(nc.sbuf_tensor(U("combb"), [128, GB * NT * E], F32))
                selg = Tg(nc.sbuf_tensor(U("selg"), [128, GB * NT * E], BF16))
                idxg = Tg(nc.sbuf_tensor(U("idxg"), [128, GB * NT * 2], F32))
                gasc = contextlib.ExitStack()
                Ta = gasc.enter_context
                cos_sb = Ta(nc.sbuf_tensor(U("cosb"), [128, SEQ], BF16))
                sin_sb = Ta(nc.sbuf_tensor(U("sinb"), [128, SEQ], BF16))
                mask_sb = Ta(nc.sbuf_tensor(U("maskb"), [128, 4 * 512], BF16))
                qoff = Ta(nc.sbuf_tensor(U("qoff"), [128, H * QG], F32))
                biasHG = Ta(nc.sbuf_tensor(U("biasHG"), [128, H * QG * KT],
                                           F32))
                ld_cs = b.dma("pool", [], cos_sb[:, :], c_cos[:, :], "ld_cs")
                ld_cs = b.dma("pool", [], sin_sb[:, :], c_sin[:, :], "ld_cs")
                ld_cs = b.dma("pool", [], mask_sb[:, :], c_mask[:, :], "ld_cs")
                e_bias = e_rateB
                for h in range(H):
                    for qg_ in range(QG):
                        col = h * QG + qg_
                        e_q = b.op("dve", [e_rateB, R.get("biaswar")],
                                   lambda e, h=h, qg_=qg_, col=col:
                                   e.tensor_scalar_mul(
                            qoff[:, col:col + 1], rateB[:, h:h + 1],
                            float(512 * qg_ + 511)), ev=True)
                        e_bias = b.op("dve", [e_q], lambda e, h=h, col=col:
                                      e.tensor_scalar(
                            biasHG[:, ts(col, KT)], iota_sb[:, :],
                            rateB[:, h:h + 1], qoff[:, col:col + 1],
                            ALU.mult, ALU.subtract), ev=True)
                comb_last = None
                x2w_ev = None
                x2w_ev2 = None

                for bb in range(GB):
                    bidx = g * GB + bb
                    bsc = contextlib.ExitStack()
                    with bsc:
                        Tb = bsc.enter_context
                        fmj = Tb(nc.sbuf_tensor(U("fmj"), [128, NCH * SEQ], BF16))
                        q_sb = Tb(nc.sbuf_tensor(U("q_sb"), [128, NCH * SEQ], BF16))
                        k_sb = Tb(nc.sbuf_tensor(U("k_sb"), [128, NCH * SEQ], BF16))
                        v_sb = Tb(nc.sbuf_tensor(U("v_sb"), [128, NT * H * 65], BF16))

                        MARKS.append((f"A1 g{g}b{bb}", b._count["ev_pe"]))
                        a1 = contextlib.ExitStack()
                        with a1:
                            T1 = a1.enter_context
                            xrow = [T1(nc.sbuf_tensor(U(f"xr{i}"), [128, D], F32))
                                    for i in range(4)]
                            xln = [T1(nc.sbuf_tensor(U(f"xl{i}"), [128, D], BF16))
                                   for i in range(2)]
                            sq = T1(nc.sbuf_tensor(U("sq"), [128, D], BF16))
                            wsec = T1(nc.sbuf_tensor(U("wsec"), [128, NCH * D], BF16))
                            ps_t = [T1(nc.psum_tensor(U(f"pst{i}"), [128, 512], BF16))
                                    for i in range(2)]
                            qk_ps = [T1(nc.psum_tensor(U(f"qkp{i}"), [128, 512], F32))
                                     for i in range(2)]
                            v_ps = [T1(nc.psum_tensor(U(f"vps{i}"), [128, 1024], F32))
                                    for i in range(2)]

                            a1bar = b.bar()
                            e_vone = b.op("dve", [R.get("av_done")] + a1bar,
                                          lambda e: e.memset(
                                v_sb[:].rearrange("p (t c) -> p t c", c=65)
                                [:, :, 64:65], 1.0), ev=True)

                            prev_spill = R.get(("spill_war", bb))
                            ev_sq = R.get("sq_war")
                            for tt in range(NT):
                                xs = xrow[tt % 4]
                                lx = b.dma("sp",
                                           [R.get(("xr", tt % 4))] + a1bar,
                                           xs[:, :], x_in[bidx, ts(tt, 128), :],
                                           f"ldx{tt % 4}")
                                sc = (tt % 2) * 16
                                lnwar = R.get(("ln", sc))
                                e_sum = b.op("dve", [lx, lnwar],
                                             lambda e, xs=xs, sc=sc:
                                             e.reduce_sum(stats[:, sc:sc + 1],
                                                          xs[:, :], axis=AX.X),
                                             ev=True)
                                ev_sq = b.op("act", [lx, ev_sq, lnwar],
                                             lambda e, xs=xs, sc=sc:
                                             e.activation(
                                    sq[:, :], xs[:, :], AF.Square,
                                    accum_out=stats[:, sc + 1:sc + 2]), ev=True)
                                e_r, e_nm = ln_chain(b, stats, sc, e_sum, ev_sq, D)
                                xl = xln[tt % 2]
                                e_n = b.op("dve", [e_r, e_nm, R.get(("xl", tt % 2))],
                                           lambda e, xs=xs, xl=xl, sc=sc:
                                           e.tensor_scalar(
                                    xl[:, :], xs[:, :], stats[:, sc + 7:sc + 8],
                                    stats[:, sc + 6:sc + 7], ALU.add, ALU.mult),
                                    ev=True)
                                R[("xr", tt % 4)] = e_n
                                R[("ln", sc)] = e_n
                                for c2 in range(2):
                                    dep = [e_n, R.get(("pst", c2)), ld_cc,
                                           prev_spill]
                                    e_tp = None
                                    for cc in range(4):
                                        c = c2 * 4 + cc
                                        e_tp = b.op("pe", dep if cc == 0 else [],
                                                    lambda e, xl=xl, c=c, cc=cc,
                                                    c2=c2: e.transpose(
                                            ps_t[c2][:, ts(cc, 128)],
                                            xl[:, ts(c, 128)], id_sb[:, :]),
                                            ev=(cc == 3))
                                    e_ev = b.op("dve", [e_tp],
                                                lambda e, tt=tt, c2=c2:
                                                e.tensor_copy(
                                        fmj[:].rearrange("p (c t) -> p c t",
                                                         c=NCH)
                                        [:, slice(c2 * 4, c2 * 4 + 4),
                                         ts(tt, 128)],
                                        ps_t[c2][:, :].rearrange(
                                            "p (c t) -> p c t", c=4)), ev=True)
                                    R[("pst", c2)] = e_ev
                                R[("xl", tt % 2)] = b.last("pe")
                            R["sq_war"] = ev_sq
                            a1_fmj = b.last("dve")

                            ev_wsec_user = R.get("wsec_war")
                            MARKS.append((f"QKV g{g}b{bb}", b._count["ev_pe"]))
                            for sec, colbase in (("v", 2 * D), ("q", 0),
                                                 ("k", D)):
                                lw = b.dma("pool", [ev_wsec_user] + a1bar,
                                           wsec[:].rearrange(
                                               "p (c f) -> p c f", c=NCH),
                                           wqkv[:, colbase:colbase + D]
                                           .rearrange("(c p) f -> p c f", p=128),
                                           "ld_wsec")
                                if sec == "v":
                                    for tt in range(NT):
                                        vp = v_ps[tt % 2]
                                        dep0 = [lw, a1_fmj,
                                                R.get(("vps", tt % 2))]
                                        last_mm = None
                                        for vg in range(2):
                                            for c in range(NCH):
                                                last_mm = b.op(
                                                    "pe",
                                                    dep0 if (vg == 0 and c == 0)
                                                    else [],
                                                    lambda e, vp=vp, vg=vg, c=c,
                                                    tt=tt: e.matmul(
                                                        vp[:, ts(vg, 512)],
                                                        fmj[:].rearrange(
                                                            "p (c t) -> p c t",
                                                            c=NCH)
                                                        [:, c, ts(tt, 128)],
                                                        wsec[:, c * D + vg * 512:
                                                             c * D + vg * 512
                                                             + 512],
                                                        start=(c == 0),
                                                        stop=(c == NCH - 1)),
                                                    ev=(vg == 1 and c == NCH - 1))
                                        e_ve = b.op("dve", [last_mm, e_vone],
                                                    lambda e, vp=vp, tt=tt:
                                                    e.tensor_copy(
                                            v_sb[:, tt * H * 65:
                                                 (tt + 1) * H * 65]
                                            .rearrange("p (h c) -> p h c", h=H)
                                            [:, :, 0:64],
                                            vp[:, :].rearrange(
                                                "p (h c) -> p h c", c=64)),
                                            ev=True)
                                        R[("vps", tt % 2)] = e_ve
                                    ev_wsec_user = last_mm
                                else:
                                    dst = q_sb if sec == "q" else k_sb
                                    for ft in range(NCH):
                                        for tg in range(QG):
                                            i2 = (ft * QG + tg) % 2
                                            qp = qk_ps[i2]
                                            dep0 = [lw, a1_fmj,
                                                    R.get(("qkp", i2))]
                                            last_mm = None
                                            for c in range(NCH):
                                                last_mm = b.op(
                                                    "pe", dep0 if c == 0 else [],
                                                    lambda e, qp=qp, ft=ft,
                                                    tg=tg, c=c: e.matmul(
                                                        qp[:, :],
                                                        wsec[:, c * D + ft * 128:
                                                             c * D + ft * 128
                                                             + 128],
                                                        fmj[:].rearrange(
                                                            "p (c t) -> p c t",
                                                            c=NCH)
                                                        [:, c, ts(tg, 512)],
                                                        start=(c == 0),
                                                        stop=(c == NCH - 1)),
                                                    ev=(c == NCH - 1))
                                            e_qe = b.op("dve", [last_mm],
                                                        lambda e, qp=qp,
                                                        dst=dst, ft=ft, tg=tg:
                                                        e.tensor_copy(
                                                dst[:, ft * SEQ + tg * 512:
                                                    ft * SEQ + tg * 512 + 512],
                                                qp[:, :]), ev=True)
                                            R[("qkp", i2)] = e_qe
                                    ev_wsec_user = last_mm
                            R["wsec_war"] = ev_wsec_user
                            a1_done = b.last("dve")

                        MARKS.append((f"A2 g{g}b{bb}", b._count["ev_pe"]))
                        a2 = contextlib.ExitStack()
                        with a2:
                            T2 = a2.enter_context
                            probs = T2(nc.sbuf_tensor(U("probs"), [128, 8 * 512],
                                                      BF16))
                            kmask = T2(nc.sbuf_tensor(U("kmask"), [128, 2 * SEQ],
                                                      BF16))
                            rs = T2(nc.sbuf_tensor(U("rs"), [128, 2 * SEQ], BF16))
                            s_ps = [T2(nc.psum_tensor(U(f"sps{i}"), [128, 1024],
                                                      F32)) for i in range(2)]
                            av_ps = [T2(nc.psum_tensor(U(f"avp{i}"), [128, 512],
                                                       F32)) for i in range(2)]
                            bc_ps = [T2(nc.psum_tensor(U(f"bcp{i}"), [128, 512],
                                                       F32)) for i in range(2)]
                            km_ps = [sp[:, 0:128] for sp in s_ps]

                            ev_sps = {}
                            ev_rs = None
                            NSL = SEQ // 512
                            for ch in ROPE_CHUNKS:
                                for srcb in (q_sb, k_sb):
                                    chunk = srcb[:, ts(ch, SEQ)]
                                    swp = rs[:, 0:SEQ]
                                    m1 = rs[:, SEQ:2 * SEQ]
                                    e_cps = []
                                    last_smm = None
                                    for sl in range(NSL):
                                        sp = s_ps[sl % 2]
                                        last_smm = b.op(
                                            "pe",
                                            [a1_done, ld_cc, ld_cs,
                                             ev_sps.get(sl % 2)],
                                            lambda e, sp=sp, ch=ch, sl=sl,
                                            srcb=srcb: e.matmul(
                                                sp[:, 0:512], swp_sb[:, :],
                                                srcb[:, ch * SEQ + sl * 512:
                                                     ch * SEQ + sl * 512 + 512],
                                                start=True, stop=True), ev=True)
                                        e_c = b.op("dve", [last_smm, ev_rs],
                                                   lambda e, sp=sp, sl=sl:
                                                   e.tensor_copy(
                                            rs[:, ts(sl, 512)], sp[:, 0:512]),
                                            ev=True)
                                        ev_sps[sl % 2] = e_c
                                        e_cps.append(e_c)
                                    e_m1 = b.op("dve", e_cps + [ld_cs],
                                                lambda e, chunk=chunk, m1=m1:
                                                e.tensor_mul(
                                        m1, chunk, cos_sb[:, :]), ev=True)
                                    e_sw = b.op("dve", [e_m1],
                                                lambda e, swp=swp:
                                                e.tensor_mul(
                                        swp, swp, sin_sb[:, :]), ev=True)
                                    ev_rs = b.op("dve", [e_sw, last_smm],
                                                 lambda e, chunk=chunk, m1=m1,
                                                 swp=swp: e.tensor_add(
                                        chunk, m1, swp), ev=True)
                            rope_done = b.last("dve")

                            ev_probs = {}
                            pend_bc = [None]

                            def flush_bc():
                                if pend_bc[0] is None:
                                    return
                                (e_rc0, bcp0, i20, rb0, ch0, qg0, h0) = pend_bc[0]
                                pend_bc[0] = None
                                e_dt = None
                                for j in range(4):
                                    e_dt = b.op("pe", [e_rc0, e_ob,
                                                       R.get(("bcp", i20))]
                                                if j == 0 else [],
                                                lambda e, bcp0=bcp0, j=j:
                                                e.matmul(
                                        bcp0[:, j:j + 1],
                                        den_sb[0:1, ts(j, 128)],
                                        onesb[0:1, 0:1],
                                        start=True, stop=True),
                                        ev=(j == 3))
                                R["rcp"] = e_dt
                                e_rt = b.op("dve", [e_dt, R.get("rcpt")],
                                            lambda e, bcp0=bcp0:
                                            e.reciprocal(
                                    rcpt_sb[:, :], bcp0[:, 0:4]), ev=True)
                                e_rt = b.op("dve", [e_rt],
                                            lambda e: e.tensor_copy(
                                    rcptb_sb[:, :], rcpt_sb[:, :]), ev=True)
                                e_rr = None
                                for j in range(4):
                                    e_rr = b.op("pe", [e_rt] if j == 0 else [],
                                                lambda e, bcp0=bcp0, j=j:
                                                e.matmul(
                                        bcp0[0:1, ts(j, 128)],
                                        rcptb_sb[:, j:j + 1], id_sb[:, :],
                                        start=True, stop=True),
                                        ev=(j == 3))
                                R["rcpt"] = e_rr
                                e_cb = b.op("dve", [e_rr, R.get("rcpb")],
                                            lambda e, bcp0=bcp0:
                                            e.tensor_copy(
                                    rcpb_sb[0:1, :], bcp0[0:1, :]), ev=True)
                                e_bc = b.op("pe", [e_cb],
                                            lambda e, bcp0=bcp0: e.matmul(
                                    bcp0[0:64, :], onesb[0:1, 0:64],
                                    rcpb_sb[0:1, :], start=True, stop=True),
                                    ev=True)
                                R["rcpb"] = e_bc
                                e_cpa = b.op("dve", [R.get(("avpe", i20)),
                                                     R.get("avbf")],
                                             lambda e, i20=i20:
                                             e.tensor_copy(
                                    avbf[0:64, :], av_ps[i20][0:64, :]),
                                    ev=True)
                                R[("avp", i20)] = e_cpa
                                e_fin = b.op("dve", [e_cpa, e_bc],
                                             lambda e, bcp0=bcp0, rb0=rb0,
                                             ch0=ch0, qg0=qg0: e.tensor_mul(
                                    fmj[rb0:rb0 + 64,
                                        ch0 * SEQ + qg0 * 512:
                                        ch0 * SEQ + qg0 * 512 + 512],
                                    avbf[0:64, :], bcp0[0:64, :]), ev=True)
                                R["avbf"] = e_fin
                                R[("bcp", i20)] = e_fin
                                R[("attnT", h0, qg0)] = e_fin

                            for h in range(H):
                                ch = h // 2
                                rope = PI_BITS[h] > 0
                                decay = E_BITS[h] > 0
                                rb = (h % 2) * 64
                                kb_ev = [None] * KT
                                if rope:
                                    selap = sel_sb[:, ts(h % 2, 128)]
                                    for t in range(KT):
                                        kp = km_ps[t % 2]
                                        e_mm = b.op("pe",
                                                    [rope_done, ld_cc,
                                                     ev_sps.get(t % 2),
                                                     R.get(("kmp", t % 2))],
                                                    lambda e, kp=kp,
                                                    selap=selap, ch=ch, t=t:
                                                    e.matmul(
                                            kp, selap,
                                            k_sb[:, ch * SEQ + t * 128:
                                                 ch * SEQ + t * 128 + 128],
                                            start=True, stop=True), ev=True)
                                        e_cp = b.op("dve", [e_mm],
                                                    lambda e, kp=kp, h=h, t=t:
                                                    e.tensor_copy(
                                            kmask[:, (h % 2) * SEQ + t * 128:
                                                  (h % 2) * SEQ + t * 128 + 128],
                                            kp), ev=True)
                                        R[("kmp", t % 2)] = e_cp
                                        ev_sps[t % 2] = e_cp
                                        kb_ev[t] = e_cp

                                for qg in range(QG):
                                    nkt = 4 * (qg + 1)
                                    npr = nkt // 2
                                    i2 = (h * QG + qg) % 2
                                    avp = av_ps[i2]
                                    pend_av = []

                                    def emit_av_pair(pi, avdep, avp=None,
                                                     h=None, nkt=None, i2=None):
                                        e_av = None
                                        t0 = 2 * pi
                                        for tt_ in (t0, t0 + 1):
                                            dd = [avdep] if tt_ == t0 else []
                                            if tt_ == 0:
                                                dd += [R.get(("avp", i2)),
                                                       R.get(("avpe", i2))]
                                            e_av = b.op(
                                                "pe", dd,
                                                lambda e, tt_=tt_, avp=avp,
                                                h=h, nkt=nkt: e.matmul(
                                                    avp[0:65, :],
                                                    v_sb[:, tt_ * H * 65
                                                         + h * 65:
                                                         tt_ * H * 65
                                                         + h * 65 + 65],
                                                    probs[:, ts(tt_ % 8, 512)],
                                                    start=(tt_ == 0),
                                                    stop=(tt_ == nkt - 1)),
                                                ev=True)
                                            ev_probs[tt_ % 8] = e_av
                                        return e_av

                                    for pr in range(npr):
                                        t0 = 2 * pr
                                        sp = s_ps[pr % 2]
                                        e_s = None
                                        for ih in range(2):
                                            t = t0 + ih
                                            if rope:
                                                lhs = kmask[:, (h % 2) * SEQ
                                                            + t * 128:
                                                            (h % 2) * SEQ
                                                            + t * 128 + 128]
                                                rhs = q_sb[:, ch * SEQ
                                                           + qg * 512:
                                                           ch * SEQ
                                                           + qg * 512 + 512]
                                            else:
                                                lhs = k_sb[rb:rb + 64,
                                                           ch * SEQ + t * 128:
                                                           ch * SEQ + t * 128
                                                           + 128]
                                                rhs = q_sb[rb:rb + 64,
                                                           ch * SEQ + qg * 512:
                                                           ch * SEQ + qg * 512
                                                           + 512]
                                            e_s = b.op(
                                                "pe",
                                                [kb_ev[t] if rope
                                                 else rope_done,
                                                 a1_done,
                                                 ev_sps.get(pr % 2)
                                                 if ih == 0 else None],
                                                lambda e, sp=sp, lhs=lhs,
                                                rhs=rhs, ih=ih: e.matmul(
                                                    sp[:, ts(ih, 512)],
                                                    lhs, rhs,
                                                    start=True, stop=True),
                                                ev=True)
                                            if pr == 0 and ih == 0:
                                                flush_bc()
                                        slot = t0 % 8
                                        pdep = [e_s, ev_probs.get(slot),
                                                ev_probs.get(slot + 1),
                                                e_bias, a1_done]
                                        if decay:
                                            e_x = None
                                            for ih in range(2):
                                                t = t0 + ih
                                                bcol = (h * QG + qg) * KT + t
                                                e_x = b.op(
                                                    "act", pdep,
                                                    lambda e, sp=sp, ih=ih,
                                                    slot=slot, bcol=bcol:
                                                    e.activation(
                                                        probs[:, ts(slot + ih,
                                                                    512)],
                                                        sp[:, ts(ih, 512)],
                                                        AF.Exp, scale=0.125,
                                                        bias=biasHG[:, bcol:
                                                                    bcol + 1]),
                                                    ev=True)
                                        else:
                                            e_x = b.op(
                                                "act", pdep,
                                                lambda e, sp=sp, slot=slot:
                                                e.activation(
                                                    probs[:, slot * 512:
                                                          slot * 512 + 1024],
                                                    sp[:, :], AF.Exp,
                                                    scale=0.125), ev=True)
                                        ev_sps[pr % 2] = e_x
                                        for ih in range(2):
                                            t = t0 + ih
                                            if t >= 4 * qg:
                                                m = t - 4 * qg
                                                e_x = b.op(
                                                    "dve", [e_x, ld_cs],
                                                    lambda e, slot=slot,
                                                    ih=ih, m=m: e.tensor_mul(
                                                        probs[:, ts(slot + ih,
                                                                    512)],
                                                        probs[:, ts(slot + ih,
                                                                    512)],
                                                        mask_sb[:, ts(m, 512)]),
                                                    ev=True)
                                        pend_av.append((pr, e_x))
                                        if len(pend_av) > 2:
                                            pi, dep_ = pend_av.pop(0)
                                            emit_av_pair(pi, dep_, avp=avp,
                                                         h=h, nkt=nkt, i2=i2)
                                    e_av = None
                                    for pi, dep_ in pend_av:
                                        e_av = emit_av_pair(pi, dep_, avp=avp,
                                                            h=h, nkt=nkt,
                                                            i2=i2)
                                    R[("avpe", i2)] = e_av
                                    e_rc = b.op("dve", [e_av, R.get("rcp")],
                                                lambda e, avp=avp:
                                                e.tensor_copy(
                                        den_sb[0:1, :], avp[64:65, :]),
                                        ev=True)
                                    flush_bc()
                                    pend_bc[0] = (e_rc, bc_ps[i2], i2, rb,
                                                  ch, qg, h)
                            flush_bc()
                            R["av_done"] = b.last("pe")

                        MARKS.append((f"A3 g{g}b{bb}", b._count["ev_pe"]))
                        a3 = contextlib.ExitStack()
                        with a3:
                            T3 = a3.enter_context
                            wo_sb = T3(nc.sbuf_tensor(U("wo_sb"), [128, NCH * D],
                                                      BF16))
                            xrow3 = [T3(nc.sbuf_tensor(U(f"x3r{i}"), [128, D],
                                                       F32)) for i in range(2)]
                            x2r = [T3(nc.sbuf_tensor(U(f"x2r{i}"), [128, D], F32))
                                   for i in range(2)]
                            xln3 = [T3(nc.sbuf_tensor(U(f"x3l{i}"), [128, D],
                                                      BF16)) for i in range(2)]
                            sq3 = T3(nc.sbuf_tensor(U("sq3"), [128, D], BF16))
                            ps_t3 = [T3(nc.psum_tensor(U(f"pt3{i}"), [128, 512],
                                                       BF16)) for i in range(2)]
                            combbf = T3(nc.sbuf_tensor(U("combbf"), [128, 16],
                                                       BF16))
                            wo_ps = [T3(nc.psum_tensor(U(f"wop{i}"), [128, 1024],
                                                       F32)) for i in range(2)]
                            g_ps = T3(nc.psum_tensor(U("g_ps"), [128, 512], F32))

                            a3bar = b.bar()
                            lwo = b.dma("pool", [R.get("wo_war")] + a3bar,
                                        wo_sb[:].rearrange("p (c f) -> p c f",
                                                           c=NCH),
                                        wo.rearrange("(c p) f -> p c f", p=128),
                                        "ld_wo")
                            ev_sq3 = R.get("sq_war")
                            ev_gate = []
                            lag = [None]

                            def emit_post(state):
                                ttp, xlp, e_np, last_mmp = state
                                e_tps = []
                                for c2 in range(2):
                                    dep = [e_np, R.get(("pt3", c2)),
                                           last_mmp, ld_cc]
                                    e_tp = None
                                    for cc in range(4):
                                        c = c2 * 4 + cc
                                        e_tp = b.op("pe",
                                                    dep if cc == 0 else [],
                                                    lambda e, xlp=xlp, c=c,
                                                    cc=cc, c2=c2: e.transpose(
                                            ps_t3[c2][:, ts(cc, 128)],
                                            xlp[:, ts(c, 128)], id_sb[:, :]),
                                            ev=(cc == 3))
                                    e_ev = b.op("dve", [e_tp],
                                                lambda e, ttp=ttp, c2=c2:
                                                e.tensor_copy(
                                        fmj[:].rearrange("p (c t) -> p c t",
                                                         c=NCH)
                                        [:, slice(c2 * 4, c2 * 4 + 4),
                                         ts(ttp, 128)],
                                        ps_t3[c2][:, :].rearrange(
                                            "p (c t) -> p c t", c=4)), ev=True)
                                    R[("pt3", c2)] = e_ev
                                    e_tps.append(e_ev)
                                R[("x3l", ttp % 2)] = b.last("pe")
                                last_g = None
                                for c in range(NCH):
                                    last_g = b.op(
                                        "pe",
                                        (e_tps + [ld_gate] + ev_gate[0])
                                        if c == 0 else [],
                                        lambda e, c=c, ttp=ttp: e.matmul(
                                            g_ps[:, 0:E],
                                            fmj[:].rearrange(
                                                "p (c t) -> p c t", c=NCH)
                                            [:, c, ts(ttp, 128)],
                                            gate_sb[:, ts(c, E)],
                                            start=(c == 0),
                                            stop=(c == NCH - 1)),
                                        ev=(c == NCH - 1))
                                gti = bb * NT + ttp
                                evg, cl, _n7 = emit_top2(
                                    b, g_ps, gwrk, comb_sb,
                                    gti * E, last_g,
                                    R.get("gwrk_war"), selg=selg, idxg=idxg,
                                    io8=io8_sb, gti=gti)
                                ev_gate[0] = evg
                                R["gwrk_war"] = cl
                                cslot = (ttp % 2) * 8
                                e_cbf = b.op("dve",
                                             [cl, R.get(("cbf", ttp % 2))],
                                             lambda e, gti=gti, cslot=cslot:
                                             e.tensor_copy(
                                    combbf[:, cslot:cslot + 8],
                                    comb_sb[:, ts(gti, E)]), ev=True)
                                cw = b.dma("sp", [e_cbf],
                                           spill_tok[ts(gti, 128), 1024:1032],
                                           combbf[:, cslot:cslot + 8],
                                           f"csp{ttp % 2}")
                                R[("cbf", ttp % 2)] = cw
                                R[("rspl", gti)] = cw
                                return cl

                            ev_gate = [[]]
                            ev_gate = ev_gate
                            ev_gate[0] = []
                            for tt in range(NT):
                                qg = min(tt // 4, QG - 1)
                                wop = wo_ps[tt % 2]
                                xs = xrow3[tt % 2]
                                lx = b.dma("sp",
                                           [R.get(("x3r", tt % 2))] + a3bar,
                                           xs[:, :], x_in[bidx, ts(tt, 128), :],
                                           f"ldx3{tt % 2}")
                                dep0 = ([lwo, R.get(("wop", tt % 2))]
                                        + [R[("attnT", h, qg)]
                                           for h in range(H)])
                                last_mm = None
                                for dg in range(2):
                                    for c in range(NCH):
                                        last_mm = b.op(
                                            "pe",
                                            dep0 if (dg == 0 and c == 0)
                                            else [],
                                            lambda e, wop=wop, dg=dg, c=c,
                                            tt=tt: e.matmul(
                                                wop[:, ts(dg, 512)],
                                                fmj[:].rearrange(
                                                    "p (c t) -> p c t", c=NCH)
                                                [:, c, ts(tt, 128)],
                                                wo_sb[:, c * D + dg * 512:
                                                      c * D + dg * 512 + 512],
                                                start=(c == 0),
                                                stop=(c == NCH - 1)),
                                            ev=(dg == 1 and c == NCH - 1))
                                x2 = x2r[tt % 2]
                                e_x2 = b.op("dve",
                                            [last_mm, lx,
                                             R.get(("x2r", tt % 2)),
                                             R.get(("x2w", tt % 2))],
                                            lambda e, x2=x2, xs=xs, wop=wop:
                                            e.tensor_add(x2[:, :], xs[:, :],
                                                         wop[:, :]), ev=True)
                                R[("x3r", tt % 2)] = e_x2
                                R[("wop", tt % 2)] = e_x2
                                x2w = b.dma("sp", [e_x2],
                                            out[bidx * SEQ + tt * 128:
                                                bidx * SEQ + (tt + 1) * 128,
                                                :],
                                            x2[:, :], f"st_out{tt % 2}")
                                R[("x2w", tt % 2)] = x2w
                                if tt % 2 == 0:
                                    x2w_ev = x2w
                                else:
                                    x2w_ev2 = x2w
                                sc = (tt % 2) * 16
                                lnwar = R.get(("ln", sc))
                                e_sum = b.op("dve", [e_x2, lnwar],
                                             lambda e, x2=x2, sc=sc:
                                             e.reduce_sum(
                                    stats[:, sc:sc + 1], x2[:, :], axis=AX.X),
                                    ev=True)
                                ev_sq3 = b.op("act", [e_x2, ev_sq3, lnwar],
                                              lambda e, x2=x2, sc=sc:
                                              e.activation(
                                    sq3[:, :], x2[:, :], AF.Square,
                                    accum_out=stats[:, sc + 1:sc + 2]), ev=True)
                                e_r, e_nm = ln_chain(b, stats, sc, e_sum,
                                                     ev_sq3, D)
                                xl = xln3[tt % 2]
                                e_n = b.op("dve",
                                           [e_r, e_nm, R.get(("x3l", tt % 2)),
                                            R.get(("xsp", tt % 2))],
                                           lambda e, x2=x2, xl=xl, sc=sc:
                                           e.tensor_scalar(
                                    xl[:, :], x2[:, :],
                                    stats[:, sc + 7:sc + 8],
                                    stats[:, sc + 6:sc + 7],
                                    ALU.add, ALU.mult), ev=True)
                                R[("x2r", tt % 2)] = e_n
                                R[("ln", sc)] = e_n
                                gti0 = bb * NT + tt
                                xsp = b.dma("sp", [e_n],
                                            spill_tok[ts(gti0, 128), 0:1024],
                                            xl[:, :], f"xsp{tt % 2}")
                                R[("xsp", tt % 2)] = xsp
                                R[("rspl2", gti0)] = xsp
                                if lag[0] is not None:
                                    comb_last = emit_post(lag[0])
                                lag[0] = (tt, xl, e_n, last_mm)
                            comb_last = emit_post(lag[0])
                            lag[0] = None
                            R["sq_war"] = ev_sq3
                            R["wo_war"] = b.last("pe")

                MARKS.append((f"B g{g}", b._count["ev_pe"]))
                R["biaswar"] = b.last("act")
                gasc.close()
                GBNT = GB * NT
                NQ4 = GBNT // 4
                rsc = contextlib.ExitStack()
                with rsc:
                    TR = rsc.enter_context
                    cnt_sb = TR(nc.sbuf_tensor(U("cntb"), [8, GBNT], F32))
                    incl_sb = TR(nc.sbuf_tensor(U("inclb"), [8, GBNT], F32))
                    zz_sb = TR(nc.sbuf_tensor(U("zzb"), [8, GBNT], F32))
                    erow_sb = TR(nc.sbuf_tensor(U("erowb"), [1, 8], F32))
                    scr8 = TR(nc.sbuf_tensor(U("scr8b"), [128, 8], F32))
                    psl = TR(nc.sbuf_tensor(U("pslb"), [128, 2], F32))
                    dstf = TR(nc.sbuf_tensor(U("dstfb"), [128, 8 * NQ4], F32))
                    dsti32 = TR(nc.sbuf_tensor(U("dsti32"), [128, 8 * NQ4],
                                               I32))
                    ct_ps = TR(nc.psum_tensor(U("ctps"), [128, 512], F32))
                    po_ps = TR(nc.psum_tensor(U("pops"), [128, 512], F32))

                    rbar = b.bar()
                    b.dma("sp", [e_czi] + (R.get("sm_war") or []) + rbar,
                          slotmap[:, :].rearrange("(p j) one -> p (j one)",
                                                  p=128),
                          czi_sb[:, :], "rst_sm")
                    sel_done = b.last("dve")
                    e_cm = None
                    for t in range(GBNT):
                        e_cm = b.op("pe", [sel_done, e_ocb, ld_cc] +
                                    ([R.get("ct_war")] if t == 0 else []),
                                    lambda e, t=t: e.matmul(
                            ct_ps[0:8, t:t + 1], selg[:, ts(t, E)],
                            onescb[:, 0:1], start=True, stop=True),
                            ev=(t == GBNT - 1))
                    e_cc = b.op("dve", [e_cm], lambda e: e.tensor_copy(
                        cnt_sb[0:8, :], ct_ps[0:8, 0:GBNT]), ev=True)
                    R["ct_war"] = e_cc
                    e_zz = b.op("dve", [], lambda e: e.memset(
                        zz_sb[:, :], 0.0), ev=True)
                    e_sc = b.op("dve", [e_cc, e_zz],
                                lambda e: e.tensor_tensor_scan(
                        incl_sb[0:8, :], cnt_sb[0:8, :], zz_sb[0:8, :], 0.0,
                        ALU.add, ALU.add), ev=True)
                    e_ex = b.op("dve", [e_sc], lambda e: e.tensor_sub(
                        incl_sb[0:8, :], incl_sb[0:8, :], cnt_sb[0:8, :]),
                        ev=True)

                    e_er = None
                    e_pos = None
                    e_dst = {}
                    idsc_war = (("idsc", b._count.get("idsc", 0))
                                if b._count.get("idsc", 0) else None)
                    rst_sm_ev = ("rst_sm", b._count["rst_sm"])
                    for t in range(GBNT):
                        e_e1 = b.op("pe", [e_ex, ld_c, e_er],
                                    lambda e, t=t: e.matmul(
                            ct_ps[0:1, 128:136], incl_sb[0:8, t:t + 1],
                            idf_sb[0:8, 0:8], start=True, stop=True), ev=True)
                        e_er = b.op("dve", [e_e1], lambda e: e.tensor_copy(
                            erow_sb[0:1, :], ct_ps[0:1, 128:136]), ev=True)
                        e_p1 = b.op("pe", [e_er, e_pos],
                                    lambda e, t=t: e.matmul(
                            po_ps[:, 0:8], triu_sb[:, :], selg[:, ts(t, E)],
                            start=True, stop=False), ev=False)
                        e_p2 = b.op("pe", [], lambda e: e.matmul(
                            po_ps[:, 0:8], ones_sb[0:1, 0:128],
                            erow_sb[0:1, 0:8], start=False, stop=True),
                            ev=True)
                        for k in range(2):
                            icol = idxg[:, 2 * t + k:2 * t + k + 1]
                            dcol = dstf[:, (t % 4) * 2 + k + 8 * (0):
                                        (t % 4) * 2 + k + 1] \
                                if False else dstf[:, (t // 4) * 8
                                                   + (t % 4) * 2 + k:
                                                   (t // 4) * 8
                                                   + (t % 4) * 2 + k + 1]
                            e_oh = b.op("dve",
                                        [e_p2, e_dst.get(k)] +
                                        ([R.get("dstf_war")] if t == 0
                                         else []),
                                        lambda e, icol=icol: e.tensor_scalar(
                                scr8[:, :], io8_sb[:, :], icol, None,
                                ALU.is_equal), ev=True)
                            e_mu = b.op("dve", [e_oh],
                                        lambda e: e.tensor_mul(
                                scr8[:, :], scr8[:, :], po_ps[:, 0:8]),
                                ev=True)
                            e_ps = b.op("dve", [e_mu],
                                        lambda e, k=k: e.reduce_sum(
                                psl[:, k:k + 1], scr8[:, :], axis=AX.X),
                                ev=True)
                            e_ps = b.op("dve", [e_ps],
                                        lambda e, k=k:
                                        e.tensor_scalar_min(
                                psl[:, k:k + 1], psl[:, k:k + 1],
                                float(CAP)), ev=True)
                            e_df = b.op("dve", [e_ps],
                                        lambda e, icol=icol, dcol=dcol:
                                        e.scalar_tensor_tensor(
                                dcol, icol, float(CS),
                                psl[:, k:k + 1],
                                ALU.mult, ALU.add), ev=True)
                            e_dst[k] = e_df
                        e_pos = b.last("dve")
                        if t % 4 == 3:
                            q = t // 4
                            e_conv = b.op(
                                "dve",
                                [e_dst.get(0), e_dst.get(1), idsc_war],
                                lambda e, q=q: e.tensor_copy(
                                    dsti32[:, q * 8:(q + 1) * 8],
                                    dstf[:, q * 8:(q + 1) * 8]), ev=True)
                            R["dstf_war"] = e_conv
                            for tq in range(4 * q, 4 * q + 4):
                                for k in range(2):
                                    col = q * 8 + (tq % 4) * 2 + k
                                    b.idma("pool",
                                           [e_conv, ld_c, rst_sm_ev],
                                           lambda e, tq=tq, col=col:
                                           e.indirect_dma_start(
                                        slotmap[:, :],
                                        IndirectOffsetOnAxis(
                                            dsti32[:, col:col + 1], 0),
                                        pay_sb[:, tq:tq + 1],
                                        None, bounds_check=None), "idsc")
                    route_done = ("idsc", b._count["idsc"])
                    spills_done = [
                        ("xsp0", b._count["xsp0"]),
                        ("xsp1", b._count["xsp1"]),
                        ("csp0", b._count["csp0"]),
                        ("csp1", b._count["csp1"]),
                        ("ld_ids", b._count["ld_ids"]),
                        ("zsp", b._count["zsp"])]

                bsc2 = contextlib.ExitStack()
                with bsc2:
                    TB = bsc2.enter_context
                    FH = F // 2
                    FTH = FT // 2
                    NTGS = CAP // 512
                    w1_sb = [TB(nc.sbuf_tensor(U("w1_sb"), [128, NCH * FH],
                                               BF16)) for _ in range(2)]
                    w2_sb = [TB(nc.sbuf_tensor(U("w2_sb"), [128, FTH * D],
                                               BF16)) for _ in range(2)]
                    hT = TB(nc.sbuf_tensor(U("hT"), [128, FTH * 512], BF16))
                    xsr = [TB(nc.sbuf_tensor(U("xsr"), [128, 4 * RW], BF16))
                           for i in range(2)]
                    xsT = TB(nc.sbuf_tensor(U("xsT"), [128, NCH * 512], BF16))
                    cfg_sb = [TB(nc.sbuf_tensor(U("cfg"), [128, 4], F32))
                              for i in range(2)]
                    idxB = [TB(nc.sbuf_tensor(U("idxB"), [128, 4], I32))
                            for i in range(2)]
                    sm_sb = [TB(nc.sbuf_tensor(U("smsb"), [128, 4], I32))
                             for i in range(2)]
                    idsc = [TB(nc.sbuf_tensor(U("idsc"), [128, 4], F32))
                            for i in range(2)]
                    tmp_sb = [TB(nc.sbuf_tensor(U("tmpb"), [128, D], F32))
                              for i in range(2)]
                    h_ps = [TB(nc.psum_tensor(U("hps"), [128, 512], F32))
                            for i in range(2)]
                    o_ps = [TB(nc.psum_tensor(U("ops"), [128, 512], F32))
                            for i in range(4)]
                    pt_ps = [TB(nc.psum_tensor(U("ptps"), [128, 512], BF16))
                             for i in range(2)]

                    bbar = b.bar()

                    def load_slot(sl, deps):
                        e_i, fh = divmod(sl, 2)
                        buf = sl % 2
                        lw1 = b.dma("pool", deps,
                                    w1_sb[buf][:].rearrange(
                                        "p (c f) -> p c f", c=NCH),
                                    w1[e_i, :, ts(fh, FH)].rearrange(
                                        "(c p) f -> p c f", p=128),
                                    f"ld_w1{buf}")
                        lw2 = b.dma("pool", [],
                                    w2_sb[buf][:].rearrange(
                                        "p (c d) -> p c d", c=FTH),
                                    w2[e_i, ts(fh, FH), :].rearrange(
                                        "(c p) d -> p c d", p=128),
                                    f"ld_w2{buf}")
                        return [lw1, lw2]

                    ev_hT = {}
                    ev_hps = {}
                    ev_op_war = {}
                    ev_xsr = {}
                    ev_cfg = {}
                    ev_cfg_use = {}
                    ev_dsg_use = {}
                    ev_tmp = {}
                    ev_pt = {}
                    out2_last = None
                    slot_pe_end = {}
                    wload = {0: load_slot(0, bbar)}
                    for sl in range(2 * E):
                        e_i, fh = divmod(sl, 2)
                        buf = sl % 2
                        if sl + 1 < 2 * E:
                            wload[sl + 1] = load_slot(
                                sl + 1, [slot_pe_end.get(sl - 1)] + bbar)
                        lw = wload[sl]
                        for tgi in range(NTGS):
                            xp = tgi % 2
                            xr = xsr[xp]
                            s0 = e_i * CS + tgi * 512
                            lsm = b.dma(
                                "sp",
                                [route_done]
                                + (R.get(("smR", xp)) or []) + bbar,
                                sm_sb[xp][:, :],
                                slotmap[s0:s0 + 512, :].rearrange(
                                    "(p j) one -> p (j one)", p=128),
                                f"ld_sm{xp}")
                            lxq = {}
                            for j4 in range(4):
                                hh = j4 // 2
                                lxq[j4] = b.idma(
                                    "pool",
                                    [lsm, ev_xsr.get((xp, hh)),
                                     ev_cfg.get((xp, j4))]
                                    + spills_done + bbar,
                                    lambda e, j4=j4, xr=xr, xp=xp:
                                    e.indirect_dma_start(
                                        xr[:, j4 * RW:(j4 + 1) * RW],
                                        None, spill_tok[:, :],
                                        IndirectOffsetOnAxis(
                                            sm_sb[xp][:, j4:j4 + 1], 0),
                                        bounds_check=None),
                                    f"gx{xp}{j4}")
                            R[("smR", xp)] = list(lxq.values())
                            e_tps = []
                            for j in range(4):
                                hh = j // 2
                                lxdep = [lxq[j]]
                                e_cf = b.op("dve",
                                            lxdep
                                            + [ev_cfg_use.get((xp, j))],
                                            lambda e, j=j, e_i=e_i, xp=xp,
                                            xr=xr: e.tensor_copy(
                                    cfg_sb[xp][:, j:j + 1],
                                    xr[:, j * RW + 1024 + e_i:
                                       j * RW + 1025 + e_i]), ev=True)
                                e_id = b.op("dve",
                                            [e_cf,
                                             ev_dsg_use.get((xp, j))],
                                            lambda e, j=j, xp=xp, xr=xr:
                                            e.scalar_tensor_tensor(
                                    idsc[xp][:, j:j + 1],
                                    xr[:, j * RW + 1032:j * RW + 1033],
                                    64.0,
                                    xr[:, j * RW + 1033:j * RW + 1034],
                                    ALU.mult, ALU.add), ev=True)
                                padoff = float(NB * SEQ - g * TOKG)
                                e_id = b.op("dve", [e_id],
                                            lambda e, j=j, xp=xp, xr=xr,
                                            padoff=padoff:
                                            e.scalar_tensor_tensor(
                                    idsc[xp][:, j:j + 1],
                                    xr[:, j * RW + 1034:j * RW + 1035],
                                    -padoff,
                                    idsc[xp][:, j:j + 1],
                                    ALU.mult, ALU.add), ev=True)
                                e_id = b.op("dve", [e_id],
                                            lambda e, j=j, xp=xp,
                                            padoff=padoff:
                                            e.tensor_single_scalar(
                                    idsc[xp][:, j:j + 1],
                                    idsc[xp][:, j:j + 1],
                                    padoff + g * TOKG, ALU.add), ev=True)
                                e_dv = b.op("dve", [e_id],
                                            lambda e, j=j, xp=xp:
                                            e.tensor_copy(
                                    idxB[xp][:, j:j + 1],
                                    idsc[xp][:, j:j + 1]), ev=True)
                                ev_cfg[(xp, j)] = e_dv
                                for c2 in range(2):
                                    dep = lxdep + [ev_pt.get(c2), ld_cc,
                                                   out2_last]
                                    e_tp = None
                                    for cc in range(4):
                                        c = c2 * 4 + cc
                                        e_tp = b.op(
                                            "pe", dep if cc == 0 else [],
                                            lambda e, j=j, c=c, cc=cc, c2=c2,
                                            xr=xr: e.transpose(
                                                pt_ps[c2][:, ts(cc, 128)],
                                                xr[:, j * RW + c * 128:
                                                   j * RW + c * 128 + 128],
                                                id_sb[:, :]),
                                            ev=(cc == 3))
                                    e_ev = b.op("dve", [e_tp],
                                                lambda e, j=j, c2=c2:
                                                e.tensor_copy(
                                        xsT[:].rearrange("p (c t) -> p c t",
                                                         c=NCH)
                                        [:, slice(c2 * 4, c2 * 4 + 4),
                                         ts(j, 128)],
                                        pt_ps[c2][:, :].rearrange(
                                            "p (c t) -> p c t", c=4)),
                                        ev=True)
                                    ev_pt[c2] = e_ev
                                    e_tps.append(e_ev)
                                if j % 2 == 1:
                                    ev_xsr[(xp, hh)] = b.last("pe")
                            last_h = None
                            for ft in range(FTH):
                                hp = h_ps[ft % 2]
                                dep0 = lw + e_tps + [ev_hps.get(ft % 2),
                                                     out2_last]
                                for c in range(NCH):
                                    last_h = b.op(
                                        "pe", dep0 if c == 0 else [],
                                        lambda e, hp=hp, ft=ft, c=c,
                                        buf=buf: e.matmul(
                                            hp[:, :],
                                            w1_sb[buf][:, c * FH + ft * 128:
                                                       c * FH + ft * 128
                                                       + 128],
                                            xsT[:, ts(c, 512)],
                                            start=(c == 0),
                                            stop=(c == NCH - 1)),
                                        ev=(c == NCH - 1))
                                e_g = b.op("act",
                                           [last_h, ev_hT.get(ft), out2_last],
                                           lambda e, hp=hp, ft=ft:
                                           e.activation(
                                    hT[:, ts(ft, 512)], hp[:, :],
                                    AF.Gelu_apprx_tanh), ev=True)
                                ev_hps[ft % 2] = e_g
                                ev_hT[ft] = e_g
                            for half in range(2):
                                last_o = {}
                                for ft in range(FTH):
                                    for j in range(2):
                                        for dg in range(2):
                                            pi = j * 2 + dg
                                            dd = [ev_hT[ft]]
                                            if ft == 0:
                                                dd.append(ev_op_war.get(pi))
                                            last_o[pi] = b.op(
                                                "pe", dd,
                                                lambda e, pi=pi, ft=ft, j=j,
                                                dg=dg, half=half, buf=buf:
                                                e.matmul(
                                                    o_ps[pi][:, :],
                                                    hT[:, ft * 512
                                                       + (half * 2 + j) * 128:
                                                       ft * 512
                                                       + (half * 2 + j) * 128
                                                       + 128],
                                                    w2_sb[buf][:, ft * D
                                                               + dg * 512:
                                                               ft * D
                                                               + dg * 512
                                                               + 512],
                                                    start=(ft == 0),
                                                    stop=(ft == FTH - 1)),
                                                ev=(ft == FTH - 1))
                                out2_last = last_o[3]
                                for j in range(2):
                                    jt = half * 2 + j
                                    tmp = tmp_sb[j]
                                    e_t = None
                                    for dg in range(2):
                                        pi = j * 2 + dg
                                        e_t = b.op(
                                            "dve",
                                            [last_o[pi], ev_tmp.get(j),
                                             ev_cfg.get((xp, jt))],
                                            lambda e, tmp=tmp, dg=dg, pi=pi,
                                            jt=jt, xp=xp:
                                            e.tensor_scalar_mul(
                                                tmp[:, ts(dg, 512)],
                                                o_ps[pi][:, :],
                                                cfg_sb[xp][:, jt:jt + 1]),
                                            ev=True)
                                        ev_op_war[pi] = e_t
                                    ev_cfg_use[(xp, jt)] = e_t
                                    e_a = b.idma(
                                        "pool",
                                        [e_t, ev_cfg[(xp, jt)],
                                         ("st_out0",
                                          b._count["st_out0"]),
                                         ("st_out1",
                                          b._count.get("st_out1", 0)),
                                         ("oacc", b._count.get("oacc", 0))],
                                        lambda e, jt=jt, xp=xp, tmp=tmp:
                                        e.indirect_dma_start(
                                            out[:, :],
                                            IndirectOffsetOnAxis(
                                                idxB[xp][:, jt:jt + 1], 0),
                                            tmp[:, :], None,
                                            bounds_check=None,
                                            compute_op=ALU.add), "oacc")
                                    ev_tmp[j] = e_a
                                    ev_dsg_use[(xp, jt)] = e_a
                        slot_pe_end[sl] = out2_last
                    R["sm_war"] = [
                        ("ld_sm0", b._count["ld_sm0"]),
                        ("ld_sm1", b._count.get("ld_sm1", 0))]
                    R["sm_war"] = [d for d in R["sm_war"] if d[1] > 0]

        final_deps = [("st_out0", b._count["st_out0"]),
                      ("st_out1", b._count.get("st_out1", 0)),
                      ("oacc", b._count.get("oacc", 0))]
        final_deps = [d for d in final_deps if d[1] > 0]
        b.op("sp", final_deps, lambda e: e.nop())
    return nc


def emit_top2(b, g_ps, gwrk, comb_sb, ccol, e_gmm, war,
              selg=None, idxg=None, io8=None, gti=0):
    e1 = b.op("dve", [e_gmm, war], lambda e: e.reduce_max(
        gwrk[:, 8:9], g_ps[:, 0:E], axis=AX.X, negate=True), ev=True)
    e2 = b.op("act", [e1, war], lambda e: e.activation(
        gwrk[:, 0:E], g_ps[:, 0:E], AF.Exp, bias=gwrk[:, 8:9]), ev=True)
    e3 = b.op("dve", [e2], lambda e: e.reduce_sum(
        gwrk[:, 9:10], gwrk[:, 0:E], axis=AX.X), ev=True)
    e4 = b.op("dve", [e3], lambda e: e.reciprocal(
        gwrk[:, 10:11], gwrk[:, 9:10]), ev=True)
    e5 = b.op("dve", [e4], lambda e: e.tensor_scalar_mul(
        gwrk[:, 0:E], gwrk[:, 0:E], gwrk[:, 10:11]), ev=True)
    e6 = b.op("dve", [e5], lambda e: e.reduce_max(
        gwrk[:, 11:12], gwrk[:, 0:E], axis=AX.X), ev=True)
    e7 = b.op("dve", [e6], lambda e: e.tensor_scalar(
        gwrk[:, 16:16 + E], gwrk[:, 0:E], gwrk[:, 11:12], None, ALU.is_lt),
        ev=True)
    e8 = b.op("dve", [e7], lambda e: e.tensor_mul(
        gwrk[:, 16:16 + E], gwrk[:, 0:E], gwrk[:, 16:16 + E]), ev=True)
    e9 = b.op("dve", [e8], lambda e: e.reduce_max(
        gwrk[:, 12:13], gwrk[:, 16:16 + E], axis=AX.X), ev=True)
    e10 = b.op("dve", [e9], lambda e: e.tensor_scalar(
        gwrk[:, 16:16 + E], gwrk[:, 0:E], gwrk[:, 12:13], None, ALU.is_ge),
        ev=True)
    n1 = b.op("dve", [e10], lambda e: e.tensor_copy(
        selg[:, ts(gti, E)], gwrk[:, 16:16 + E]), ev=True)
    n2 = b.op("dve", [e10], lambda e: e.tensor_scalar(
        gwrk[:, 24:32], gwrk[:, 0:E], gwrk[:, 11:12], None, ALU.is_ge),
        ev=True)
    n3 = b.op("dve", [n2], lambda e: e.tensor_sub(
        gwrk[:, 32:40], gwrk[:, 16:16 + E], gwrk[:, 24:32]), ev=True)
    n4 = b.op("dve", [n3], lambda e: e.tensor_mul(
        gwrk[:, 24:32], io8[:, :], gwrk[:, 24:32]), ev=True)
    n5 = b.op("dve", [n4], lambda e: e.reduce_sum(
        idxg[:, 2 * gti:2 * gti + 1], gwrk[:, 24:32], axis=AX.X), ev=True)
    n6 = b.op("dve", [n4], lambda e: e.tensor_mul(
        gwrk[:, 32:40], io8[:, :], gwrk[:, 32:40]), ev=True)
    n7 = b.op("dve", [n6], lambda e: e.reduce_sum(
        idxg[:, 2 * gti + 1:2 * gti + 2], gwrk[:, 32:40], axis=AX.X), ev=True)
    e11 = b.op("dve", [n1, n3], lambda e: e.tensor_mul(
        gwrk[:, 16:16 + E], gwrk[:, 0:E], gwrk[:, 16:16 + E]), ev=True)
    e12 = b.op("dve", [e11], lambda e: e.reduce_sum(
        gwrk[:, 13:14], gwrk[:, 16:16 + E], axis=AX.X), ev=True)
    e13 = b.op("dve", [e12], lambda e: e.reciprocal(
        gwrk[:, 14:15], gwrk[:, 13:14]), ev=True)
    e14 = b.op("dve", [e13], lambda e: e.tensor_scalar_mul(
        comb_sb[:, ccol:ccol + E], gwrk[:, 16:16 + E], gwrk[:, 14:15]),
        ev=True)
    return [e1, e2], e14, n7


def host_constants(SEQ, KT):
    ident = np.eye(128, dtype=np.float32)
    ones = np.ones((1, 128), dtype=np.float32)
    sel = np.zeros((128, 256), dtype=np.float32)
    for r in list(range(0, 32)) + list(range(64, 96)):
        sel[r, r] = 1.0
    for r in list(range(32, 64)) + list(range(96, 128)):
        sel[r, 128 + r] = 1.0
    inv_freq = ROPE_BASE ** (-(np.arange(0, DH, 2, dtype=np.float64) / DH))
    pos = np.arange(SEQ, dtype=np.float64)
    theta = inv_freq[:, None] * pos[None, :]
    c64 = np.concatenate([np.cos(theta)] * 2, 0)
    s64 = np.concatenate([np.sin(theta)] * 2, 0)
    cos = np.concatenate([c64, c64], 0).astype(np.float32)
    sin = np.concatenate([-s64, s64], 0).astype(np.float32)
    swp = np.zeros((128, 128), dtype=np.float32)
    swp[np.arange(64), 64 + np.arange(64)] = 1.0
    swp[64 + np.arange(64), np.arange(64)] = 1.0
    mask = np.zeros((128, 4 * 512), dtype=np.float32)
    p = np.arange(128)[:, None]
    q = np.arange(512)[None, :]
    for m in range(4):
        mask[:, m * 512:(m + 1) * 512] = (128 * m + p <= q).astype(np.float32)
    iota = (np.arange(128)[:, None]
            + 128 * np.arange(KT)[None, :]).astype(np.float32)
    return {"c_id": ident, "c_ones": ones, "c_sel": sel, "c_cos": cos,
            "c_sin": sin, "c_mask": mask, "c_iota": iota, "c_swp": swp,
            "c_ek": E_BITS.reshape(1, H).astype(np.float32),
            "c_triu": np.triu(np.ones((128, 128), np.float32), 1),
            "c_io8": np.broadcast_to(np.arange(8, dtype=np.float32),
                                     (128, 8)).copy()}


def host_ids(tokg):
    t = np.arange(tokg)
    ids = np.zeros((tokg, 120), np.float32)
    ids[:, 0] = t // 64
    ids[:, 1] = t % 64
    ids[:, 2] = 1.0
    return np.ascontiguousarray(ids.astype(ml_dtypes.bfloat16))


def host_pay(tokg):
    t = np.arange(tokg // 128)
    p = np.arange(128)
    return np.ascontiguousarray(
        (t[None, :] * 128 + p[:, None]).astype(np.int32))


def permute_wqkv(w):
    w = np.asarray(w)
    perm = np.arange(3 * D)
    within = np.concatenate([np.arange(0, 64, 2), 64 + np.arange(0, 64, 2),
                             np.arange(1, 64, 2), 64 + np.arange(1, 64, 2)])
    for s in range(2):
        for c in ROPE_CHUNKS:
            base = s * D + c * 128
            perm[base:base + 128] = base + within
    return np.ascontiguousarray(w[:, perm])


_NC_CACHE = {}


def kernel(L0_attn_x, L0_attn_w_qkv, L0_attn_w_o, L0_attn_log_xi,
           L0_ffn_gate_w, L0_ffn_w1, L0_ffn_w2):
    NB_TOT, SEQ, _ = L0_attn_x.shape
    NB = NB_TOT // N_CORES
    GB = 2 if NB % 2 == 0 else 1
    F = L0_ffn_w1.shape[2]
    key = (NB, SEQ, GB, F)
    if key not in _NC_CACHE:
        _NC_CACHE[key] = build_program(NB, SEQ, GB, F)
    nc = _NC_CACHE[key]
    consts = host_constants(SEQ, SEQ // 128)
    consts["c_ids"] = host_ids(GB * SEQ)
    consts["c_pay"] = host_pay(GB * SEQ)
    shared = {
        "wqkv": permute_wqkv(L0_attn_w_qkv),
        "wo": np.ascontiguousarray(np.asarray(L0_attn_w_o)),
        "gw": np.ascontiguousarray(np.asarray(L0_ffn_gate_w)),
        "w1": np.ascontiguousarray(np.asarray(L0_ffn_w1)),
        "w2": np.ascontiguousarray(np.asarray(L0_ffn_w2)),
        "lxi": np.ascontiguousarray(
            np.asarray(L0_attn_log_xi).reshape(1, H)),
        **consts,
    }
    x = np.ascontiguousarray(np.asarray(L0_attn_x))
    in_maps = [{"x": x[i * NB:(i + 1) * NB], **shared}
               for i in range(N_CORES)]
    res = run_bass_kernel_spmd(nc, in_maps, core_ids=list(range(N_CORES)))
    return np.concatenate(
        [res.results[i]["out"][:NB * SEQ].reshape(NB, SEQ, D)
         for i in range(N_CORES)],
        axis=0)
